# version 05
# baseline (speedup 1.0000x reference)
import numpy as np
import concourse.bass as bass
import concourse.mybir as mybir
from concourse.bass_utils import run_bass_kernel_spmd
from concourse.bass_core_rust import IndirectOffsetOnAxis

F32 = mybir.dt.float32
BF16 = mybir.dt.bfloat16
I32 = mybir.dt.int32
AF = mybir.ActivationFunctionType
ALU = mybir.AluOpType
AX = mybir.AxisListType

B, T, D = 2, 8192, 1024
H, DH = 8, 128
E, CAP, FE, FS = 16, 1024, 2048, 2048
NCORE = 8
TLOC = 2048
WARM = 512
NTW = WARM // 128
NTR = TLOC // 128
NT = NTW + NTR
KC = D // 128
MF = FE // 128
MS = FS // 128
LN9 = 2.1972245773362196
BIG = 1.0e9
BISECT_ITERS = 42

ENGINES = ("sync", "tensor", "vector", "scalar", "gpsimd")


class TL:
    def __init__(self):
        self.ops = {e: [] for e in ENGINES}
        self.cnt = {}
        self.autofence = {}
        self.last = {}

    def add(self, eng, fn, waits=(), inc=None, incn=1, nofence=False):
        ws = [w for w in waits if w is not None]
        fence_sem = self.autofence.get(eng) if not nofence else None
        if fence_sem is not None:
            last = self.last.get(eng)
            if last is not None:
                ws.append(last)
            if inc is None:
                inc = fence_sem
        m = None
        if inc is not None:
            c = self.cnt.get(id(inc), 0) + incn
            self.cnt[id(inc)] = c
            m = (inc, c)
        if fence_sem is not None:
            self.last[eng] = m
        self.ops[eng].append((fn, ws, inc, incn))
        return m

    def cur(self, sem):
        return (sem, self.cnt.get(id(sem), 0))

    def emit(self, eng_obj, name):
        for fn, waits, inc, incn in self.ops[name]:
            best, order = {}, []
            for (sem, c) in waits:
                if id(sem) not in best:
                    order.append(sem)
                best[id(sem)] = max(best.get(id(sem), -1), c)
            for sem in order:
                eng_obj.wait_ge(sem, best[id(sem)])
            inst = fn(eng_obj)
            if inc is not None:
                if inst is None:
                    inst = eng_obj.nop()
                inst.then_inc(inc, incn)


def build_program(debug_taps=False, attn_only=False, race_detect=True):
    nc = bass.Bass("TRN2", target_bir_lowering=False,
                   detect_race_conditions=race_detect)

    xin = nc.dram_tensor("xin", [NT * 128, D], F32, kind="ExternalInput").ap()
    wfT = nc.dram_tensor("wfT", [128, KC, D], F32, kind="ExternalInput").ap()
    wiT = nc.dram_tensor("wiT", [128, KC, D], F32, kind="ExternalInput").ap()
    woT = nc.dram_tensor("woT", [128, KC, D], F32, kind="ExternalInput").ap()
    normw = nc.dram_tensor("normw", [1, D], F32, kind="ExternalInput").ap()
    gateT = nc.dram_tensor("gateT", [128, KC, E], F32, kind="ExternalInput").ap()
    wsgT = nc.dram_tensor("wsgT", [128, KC, FS], F32, kind="ExternalInput").ap()
    wsuT = nc.dram_tensor("wsuT", [128, KC, FS], F32, kind="ExternalInput").ap()
    wsdT = nc.dram_tensor("wsdT", [128, MS, D], F32, kind="ExternalInput").ap()
    wgT = nc.dram_tensor("wgT", [4, 128, KC, FE], F32, kind="ExternalInput").ap()
    wuT = nc.dram_tensor("wuT", [4, 128, KC, FE], F32, kind="ExternalInput").ap()
    wdT = nc.dram_tensor("wdT", [4, 128, MF, D], F32, kind="ExternalInput").ap()
    ident = nc.dram_tensor("ident", [128, 128], F32, kind="ExternalInput").ap()
    ltri = nc.dram_tensor("ltri", [128, 128], F32, kind="ExternalInput").ap()
    onesc_i = nc.dram_tensor("onesc", [128, 1], F32, kind="ExternalInput").ap()
    ones1_i = nc.dram_tensor("ones1", [1, 128], F32, kind="ExternalInput").ap()
    tokid = nc.dram_tensor("tokid", [128, 64], F32, kind="ExternalInput").ap()
    oh4 = nc.dram_tensor("oh4", [128, 4, E], F32, kind="ExternalInput").ap()
    out = nc.dram_tensor("out", [TLOC, D], F32, kind="ExternalOutput").ap()
    if debug_taps:
        dbg_xmid = nc.dram_tensor("dbg_xmid", [TLOC, D], F32, kind="ExternalOutput").ap()
        dbg_hid = nc.dram_tensor("dbg_hid", [TLOC, D], F32, kind="ExternalOutput").ap()
        dbg_sco = nc.dram_tensor("dbg_sco", [T, E], F32, kind="ExternalOutput").ap()
        dbg_idx = nc.dram_tensor("dbg_idx", [4, CAP], F32, kind="ExternalOutput").ap()
        dbg_dense = nc.dram_tensor("dbg_dense", [T, D], F32, kind="ExternalOutput").ap()
        dbg2 = nc.dram_tensor("dbg2", [12, 128, D], F32, kind="ExternalOutput").ap()
        dbg3 = nc.dram_tensor("dbg3", [6, 128, D], F32, kind="ExternalOutput").ap()

    wsg_b = nc.dram_tensor("wsg_b", [128, KC, FS], BF16).ap()
    wsu_b = nc.dram_tensor("wsu_b", [128, KC, FS], BF16).ap()
    wsd_b = nc.dram_tensor("wsd_b", [128, MS, D], BF16).ap()
    wg_b = nc.dram_tensor("wg_b", [4, 128, KC, FE], BF16).ap()
    wu_b = nc.dram_tensor("wu_b", [4, 128, KC, FE], BF16).ap()
    wd_b = nc.dram_tensor("wd_b", [4, 128, MF, D], BF16).ap()
    hid_loc = nc.dram_tensor("hid_loc", [TLOC, D], BF16).ap()
    hid_full = nc.dram_tensor("hid_full", [T, D], BF16).ap()
    sco_loc = nc.dram_tensor("sco_loc", [TLOC, E], F32).ap()
    sco_full = nc.dram_tensor("sco_full", [T, E], F32).ap()
    dense = nc.dram_tensor("dense", [T, D], F32).ap()
    rs_out = nc.dram_tensor("rs_out", [TLOC, D], F32).ap()
    xpark = nc.dram_tensor("xpark", [TLOC, D], F32).ap()
    spark = nc.dram_tensor("spark", [TLOC, D], F32).ap()
    idxl_d = [nc.dram_tensor(f"idxl_d{ee}", [CAP, 1], F32).ap()
              for ee in range(4)]

    xv = xin.rearrange("(t p) d -> t p d", p=128)
    tl = TL()
    groups = [[0, 1, 2, 3], [4, 5, 6, 7]]

    import contextlib
    es = contextlib.ExitStack()

    def mksem(name):
        return es.enter_context(nc.semaphore(name))
    scc = mksem("scc")
    sv = mksem("sv")
    ss = mksem("ss")
    st_ = mksem("st")
    s_const = mksem("s_const")
    s_dense = mksem("s_dense")
    s_x = [mksem(f"s_x{k}") for k in range(3)]
    s_wsh = mksem("s_wsh")
    s_misc = mksem("s_misc")
    s_wg = [mksem(f"s_wg{k}") for k in range(2)]
    s_wu = [mksem(f"s_wu{k}") for k in range(2)]
    s_wd = mksem("s_wd")
    s_crs = [mksem(f"s_crs{k}") for k in range(2)]
    s_csp = [mksem(f"s_csp{k}") for k in range(2)]
    s_cxp = [mksem(f"s_cxp{k}") for k in range(2)]
    s_xp = [mksem(f"s_xp{k}") for k in range(2)]
    s_hd = [mksem(f"s_hd{k}") for k in range(2)]
    s_sc = mksem("s_sc")
    s_sp = mksem("s_sp")
    s_identb = mksem("s_identb")
    s_attnw = mksem("s_attnw")
    s_cast = mksem("s_cast")
    s_isc = mksem("s_isc")
    s_tw = mksem("s_tw")
    s_xg = mksem("s_xg")
    s_eo = [mksem(f"s_eo{k}") for k in range(2)]
    s_out = mksem("s_out")
    s_dbg = mksem("s_dbg")
    tl.autofence = {"vector": sv}

    P1 = es.enter_context(nc.psum_tensor("P1", [128, 1024], F32))
    P2 = es.enter_context(nc.psum_tensor("P2", [128, 1024], F32))
    P3 = es.enter_context(nc.psum_tensor("P3", [128, 1024], F32))
    PB = es.enter_context(nc.psum_tensor("PB", [128, 1024], BF16))
    P4 = es.enter_context(nc.psum_tensor("P4", [128, 512], F32))

    identb32 = es.enter_context(nc.sbuf_tensor("identb32", [128, 128], F32))
    identbb = es.enter_context(nc.sbuf_tensor("identbb", [128, 128], BF16))
    ltrib = es.enter_context(nc.sbuf_tensor("ltrib", [128, 128], F32))
    onescb = es.enter_context(nc.sbuf_tensor("onescb", [128, 1], F32))
    ones1b = es.enter_context(nc.sbuf_tensor("ones1b", [1, 128], F32))
    tokidb = es.enter_context(nc.sbuf_tensor("tokidb", [128, 64], F32))
    oh4b = es.enter_context(nc.sbuf_tensor("oh4b", [128, 4, E], F32))
    zzz = es.enter_context(nc.sbuf_tensor("zzz", [128, 1024], F32))
    bigt = es.enter_context(nc.sbuf_tensor("bigt", [128, 64], F32))
    idx8 = es.enter_context(nc.sbuf_tensor("idx8", [8, 128], F32))
    idxi = es.enter_context(nc.sbuf_tensor("idxi", [128, 8], I32))
    tw16 = es.enter_context(nc.sbuf_tensor("tw16", [128, E], F32))
    topw = es.enter_context(nc.sbuf_tensor("topw", [128, 8], F32))
    odsti = es.enter_context(nc.sbuf_tensor("odsti", [128, 64], I32))
    xtok = es.enter_context(nc.sbuf_tensor("xtok", [128, 8, D], BF16))
    eo = es.enter_context(nc.sbuf_tensor("eo", [128, 2, D], F32))
    gatb = es.enter_context(nc.sbuf_tensor("gatb", [128, KC, E], BF16))
    sco_all = es.enter_context(nc.sbuf_tensor("sco_all", [128, NTR, E], F32))
    gj = es.enter_context(nc.sbuf_tensor("gj", [128, 8], F32))
    _hidT_guard = nc.sbuf_tensor("hidT", [128, KC, TLOC], BF16)
    hidT = _hidT_guard.__enter__()

    def dma(dst, src):
        return lambda e: e.dma_start(out=dst, in_=src)

    regs = {}
    def mk_regs(e):
        regs['cap'] = e.to_reg(CAP - 1)
        return None
    tl.add('gpsimd', mk_regs)

    ev_const = None
    for dst, src in ((identb32[:], ident), (ltrib[:], ltri), (onescb[:], onesc_i),
                     (ones1b[:], ones1_i), (tokidb[:], tokid), (oh4b[:], oh4)):
        ev_const = tl.add("sync", dma(dst, src), inc=s_const, incn=16)
    ev_identb = tl.add("gpsimd", dma(identbb[:], ident), inc=s_identb, incn=16)
    ev_zzz = tl.add("vector", lambda e: e.memset(zzz[:], 0.0), inc=sv)
    ev_big = tl.add("vector", lambda e: e.memset(bigt[:], BIG), inc=sv)

    ev_wcast = {}
    for name, dst, src in (("wsg", wsg_b, wsgT), ("wsu", wsu_b, wsuT),
                           ("wsd", wsd_b, wsdT)):
        ev_wcast[name] = tl.add("gpsimd", dma(dst, src), inc=s_cast, incn=16)
    for ee in range(4):
        ev_wcast[f"wg{ee}"] = tl.add("gpsimd", dma(wg_b[ee], wgT[ee]), inc=s_cast, incn=16)
        ev_wcast[f"wu{ee}"] = tl.add("gpsimd", dma(wu_b[ee], wuT[ee]), inc=s_cast, incn=16)
        ev_wcast[f"wd{ee}"] = tl.add("gpsimd", dma(wd_b[ee], wdT[ee]), inc=s_cast, incn=16)
    ev_cast_all = ev_wcast["wd3"]
    for k in ev_wcast:
        ev_wcast[k] = ev_cast_all

    dnv = dense.rearrange("(n p) d -> n p d", p=128)
    ev_dense0 = None
    for n in range(T // 128):
        ev_dense0 = tl.add("sync", dma(dnv[n], zzz[:]), waits=[ev_zzz],
                           inc=s_dense, incn=16)

    esA = contextlib.ExitStack()
    wfb = esA.enter_context(nc.sbuf_tensor("wfb", [128, KC, D], BF16))
    wib = esA.enter_context(nc.sbuf_tensor("wib", [128, KC, D], BF16))
    wob = esA.enter_context(nc.sbuf_tensor("wob", [128, KC, D], BF16))
    normwb = esA.enter_context(nc.sbuf_tensor("normwb", [128, D], F32))
    nw1 = esA.enter_context(nc.sbuf_tensor("nw1", [1, D], F32))
    xb = esA.enter_context(nc.sbuf_tensor("xb", [128, 3, D], F32))
    sta = esA.enter_context(nc.sbuf_tensor("sta", [128, 2, 16], F32))
    st2 = esA.enter_context(nc.sbuf_tensor("st2", [128, 2, 16], F32))
    junk = esA.enter_context(nc.sbuf_tensor("junk", [128, D], F32))
    ln1b = esA.enter_context(nc.sbuf_tensor("ln1b", [128, 2, D], F32))
    tpre = esA.enter_context(nc.sbuf_tensor("tpre", [128, D], F32))
    xnb = esA.enter_context(nc.sbuf_tensor("xnb", [128, D], BF16))
    xnT = esA.enter_context(nc.sbuf_tensor("xnT", [128, KC, 128], BF16))
    fb = esA.enter_context(nc.sbuf_tensor("fb", [128, D], F32))
    ob = esA.enter_context(nc.sbuf_tensor("ob", [128, D], F32))
    vs = esA.enter_context(nc.sbuf_tensor("vs", [128, H], F32))
    ub = esA.enter_context(nc.sbuf_tensor("ub", [128, D], F32))
    fTs = esA.enter_context(nc.sbuf_tensor("fTs", [128, D], F32))
    sTb = esA.enter_context(nc.sbuf_tensor("sTb", [128, 2, D], F32))
    yb = esA.enter_context(nc.sbuf_tensor("yb", [128, D], F32))
    retb = esA.enter_context(nc.sbuf_tensor("retb", [128, 2, D], F32))
    ln2b = esA.enter_context(nc.sbuf_tensor("ln2b", [128, D], F32))
    xmb = esA.enter_context(nc.sbuf_tensor("xmb", [128, 2, D], F32))
    hbb = esA.enter_context(nc.sbuf_tensor("hbb", [128, 2, D], BF16))
    if True:
        tl.add("gpsimd", dma(wfb[:], wfT), inc=s_attnw, incn=16)
        tl.add("gpsimd", dma(wib[:], wiT), inc=s_attnw, incn=16)
        tl.add("gpsimd", dma(wob[:], woT), inc=s_attnw, incn=16)
        ev_attnw = tl.add("gpsimd", dma(gatb[:], gateT), inc=s_attnw, incn=16)
        ev_wf = ev_wi = ev_wo = ev_gat = ev_attnw
        ev_nw = tl.add("sync", dma(nw1[:], normw), inc=s_const, incn=16)
        ev_const = ev_nw

        def mk_nwb(e):
            e.matmul(P1[:, 0:512], ones1b[:], nw1[:, 0:512], start=True, stop=True)
            return e.matmul(P1[:, 512:1024], ones1b[:], nw1[:, 512:1024],
                            start=True, stop=True)
        ev_nwmm = tl.add("tensor", mk_nwb, waits=[ev_nw, ev_const], inc=st_)
        ev_nwb = tl.add("vector", lambda e: e.tensor_copy(normwb[:], P1[:]),
                        waits=[ev_nwmm], inc=sv)

        evx = [None] * NT
        ev_stats = [None] * NT
        ev_sq = [None] * NT
        ev_recip = [None] * NT
        ev_xn = [None] * NT
        ev_xnTpe = [None] * NT
        ev_exnT = [None] * NT
        ev_zf = [None] * NT
        ev_zv = [None] * NT
        ev_zo = [None] * NT
        ev_zfm = [None] * NT
        ev_f = [None] * NT
        ev_o = [None] * NT
        ev_vs = [None] * NT
        ev_u = [None] * NT
        ev_fTpe = [None] * NT
        ev_uTpe = [None] * NT
        ev_efT = [None] * NT
        ev_scan = [None] * NT
        ev_stbpe = [None] * NT
        ev_y = [None] * NT
        ev_ret = [None] * NT
        ev_st2 = [None] * NT
        ev_sq2 = [None] * NT
        ev_r2 = [None] * NT
        ev_ln2 = [None] * NT
        ev_xm = [None] * NT
        ev_st3 = [None] * NT
        ev_sq3 = [None] * NT
        ev_hid = [None] * NT
        ev_hTpe = [None] * NT
        ev_ehT = [None] * NT
        ev_xmdma = [None] * NT
        ev_hdma = [None] * NT

        def stageA(i):
            real = i >= NTW
            waits = [evx[i - 3], ev_xm[i - 3] if i - 3 >= NTW else ev_xn[i - 3]] if i >= 3 else []
            evx[i] = tl.add("sync", dma(xb[:, i % 3, :], xv[i]), waits=waits,
                            inc=s_x[i % 3], incn=16)
            tl.add("vector", lambda e, i=i: e.reduce_sum(
                sta[:, i % 2, 0:1], xb[:, i % 3, :], AX.X), waits=[evx[i]],
                nofence=True)
            tl.add("vector", lambda e, i=i: e.scalar_tensor_tensor(
                junk[:], xb[:, i % 3, :], 1.0, xb[:, i % 3, :], ALU.mult, ALU.mult,
                accum_out=sta[:, i % 2, 1:2]), waits=[evx[i]])
            tl.add("vector", lambda e, i=i: e.tensor_scalar(
                sta[:, i % 2, 2:3], sta[:, i % 2, 0:1], -1.0 / D, None, ALU.mult))
            tl.add("vector", lambda e, i=i: e.tensor_scalar(
                sta[:, i % 2, 3:4], sta[:, i % 2, 1:2], 1.0 / D, None, ALU.mult))
            tl.add("vector", lambda e, i=i: e.tensor_mul(
                sta[:, i % 2, 4:5], sta[:, i % 2, 2:3], sta[:, i % 2, 2:3]))
            tl.add("vector", lambda e, i=i: e.tensor_sub(
                sta[:, i % 2, 5:6], sta[:, i % 2, 3:4], sta[:, i % 2, 4:5]))
            if real:
                tl.add("vector", lambda e, i=i: e.tensor_scalar(
                    sta[:, i % 2, 6:7], sta[:, i % 2, 5:6], 1.0, 1e-5,
                    ALU.mult, ALU.add))
            ev_stats[i] = tl.add("vector", lambda e, i=i: e.tensor_scalar(
                sta[:, i % 2, 7:8], sta[:, i % 2, 5:6], 1.0 + 1e-6, 1e-11,
                ALU.mult, ALU.add), inc=sv)

            w = [ev_stats[i]]
            if real:
                tl.add("scalar", lambda e, i=i: e.activation(
                    sta[:, i % 2, 8:9], sta[:, i % 2, 6:7], AF.Sqrt), waits=w)
                w = []
            ev_sq[i] = tl.add("scalar", lambda e, i=i: e.activation(
                sta[:, i % 2, 9:10], sta[:, i % 2, 7:8], AF.Sqrt), waits=w, inc=ss)
            if i - 1 >= NTW and ev_st2[i - 1] is not None:
                ev_sq2[i - 1] = tl.add("scalar", lambda e, i=i: e.activation(
                    st2[:, (i - 1) % 2, 4:5], st2[:, (i - 1) % 2, 3:4], AF.Sqrt),
                    waits=[ev_st2[i - 1]], inc=ss)
            if i - 2 >= NTW and ev_st3[i - 2] is not None:
                ev_sq3[i - 2] = tl.add("scalar", lambda e, i=i: e.activation(
                    st2[:, (i - 2) % 2, 12:13], st2[:, (i - 2) % 2, 11:12], AF.Sqrt),
                    waits=[ev_st3[i - 2]], inc=ss)

            w = [ev_sq[i]]
            if real:
                tl.add("vector", lambda e, i=i: e.reciprocal(
                    sta[:, i % 2, 10:11], sta[:, i % 2, 8:9]), waits=w)
                w = []
            ev_recip[i] = tl.add("vector", lambda e, i=i: e.reciprocal(
                sta[:, i % 2, 11:12], sta[:, i % 2, 9:10]), waits=w, inc=sv)
            if real:
                w = [ev_recip[i]]
                if i - 2 >= NTW:
                    w.append(ev_ret[i - 2])
                tl.add("vector", lambda e, i=i: e.tensor_scalar(
                    ln1b[:, i % 2, :], xb[:, i % 3, :], sta[:, i % 2, 2:3],
                    sta[:, i % 2, 10:11], ALU.add, ALU.mult), waits=w)
            tl.add("vector", lambda e, i=i: e.tensor_scalar(
                tpre[:], xb[:, i % 3, :], sta[:, i % 2, 2:3], sta[:, i % 2, 11:12],
                ALU.add, ALU.mult), waits=[ev_recip[i]])
            w = [ev_nwb, ev_identb]
            if i >= 1:
                w.append(ev_xnTpe[i - 1])
            ev_xn[i] = tl.add("vector", lambda e: e.tensor_mul(
                xnb[:], tpre[:], normwb[:]), waits=w, inc=sv, nofence=True)

            w = [ev_xn[i]]
            if i >= 1:
                w.append(ev_exnT[i - 1])
            if i - 3 >= NTW and ev_ehT[i - 3] is not None:
                w.append(ev_ehT[i - 3])

            def mk_xnT(e):
                r = None
                for g in range(KC):
                    r = e.transpose(PB[:, g * 128:(g + 1) * 128],
                                    xnb[:, g * 128:(g + 1) * 128], identbb[:])
                return r
            ev_xnTpe[i] = tl.add("tensor", mk_xnT, waits=w, inc=st_)
            ev_exnT[i] = tl.add("vector", lambda e: e.tensor_copy(
                xnT[:].rearrange("p k n -> p (k n)"), PB[:]),
                waits=[ev_xnTpe[i]], inc=sv, nofence=True)

            def mk_proj(dst, wsb):
                def f(e):
                    r = None
                    for k in range(KC):
                        for n in range(2):
                            r = e.matmul(dst[:, n * 512:(n + 1) * 512],
                                         xnT[:, k, :],
                                         wsb[:, k, n * 512:(n + 1) * 512],
                                         start=(k == 0), stop=(k == KC - 1))
                    return r
                return f
            w = [ev_exnT[i], ev_wf]
            if i >= 1:
                w.append(ev_efT[i - 1])
            ev_zf[i] = tl.add("tensor", mk_proj(P1, wfb), waits=w, inc=st_)
            w = [ev_exnT[i], ev_wi]
            if i >= 1:
                w.append(ev_scan[i - 1])
            ev_zv[i] = tl.add("tensor", mk_proj(P2, wib), waits=w, inc=st_)
            if real:
                w = [ev_exnT[i], ev_wo]
                if i - 1 >= NTW:
                    w.append(ev_y[i - 1])
                ev_zo[i] = tl.add("tensor", mk_proj(P3, wob), waits=w, inc=st_)

            ev_zfm[i] = tl.add("vector", lambda e: e.tensor_scalar(
                junk[:], P1[:], LN9, None, ALU.max), waits=[ev_zf[i]], inc=sv,
                nofence=True)
            ev_f[i] = tl.add("scalar", lambda e: e.activation(
                fb[:], junk[:], AF.Sigmoid), waits=[ev_zfm[i]], inc=ss)
            if real:
                ev_o[i] = tl.add("scalar", lambda e: e.activation(
                    ob[:], P3[:], AF.Sigmoid), waits=[ev_zo[i]], inc=ss)
            ev_vs[i] = tl.add("vector", lambda e: e.tensor_reduce(
                vs[:], P2[:].rearrange("p (h d) -> p h d", h=H), AX.X, ALU.add),
                waits=[ev_zv[i]], inc=sv, nofence=True)
            tl.add("vector", lambda e: e.tensor_scalar(
                junk[:], fb[:], -1.0, 1.0, ALU.mult, ALU.add), waits=[ev_f[i]],
                nofence=True)
            ev_u[i] = tl.add("vector", lambda e: e.tensor_tensor(
                ub[:].rearrange("p (h d) -> p h d", h=H),
                junk[:].rearrange("p (h d) -> p h d", h=H),
                vs[:].to_broadcast([128, H, DH]), ALU.mult), inc=sv)

            def mk_tr(dst_ps, src):
                def f(e):
                    r = None
                    for g in range(KC):
                        r = e.transpose(dst_ps[:, g * 128:(g + 1) * 128],
                                        src[:, g * 128:(g + 1) * 128], identb32[:])
                    return r
                return f
            ev_fTpe[i] = tl.add("tensor", mk_tr(P1, fb[:]),
                                waits=[ev_f[i], ev_zfm[i]], inc=st_)
            ev_uTpe[i] = tl.add("tensor", mk_tr(P2, ub[:]),
                                waits=[ev_u[i], ev_vs[i]], inc=st_)
            ev_efT[i] = tl.add("vector", lambda e: e.tensor_copy(fTs[:], P1[:]),
                               waits=[ev_fTpe[i]], inc=sv, nofence=True)
            ev_euT = tl.add("vector", lambda e: e.tensor_copy(ub[:], P2[:]),
                            waits=[ev_uTpe[i]], inc=sv, nofence=True)

            w = [ev_efT[i], ev_euT]
            if i >= 1:
                w.append(ev_scan[i - 1])
            if i - 2 >= NTW:
                w.append(ev_stbpe[i - 2])

            def mk_scans(e, i=i):
                r = None
                for g in range(KC):
                    init = 0.0 if i == 0 else sTb[:, (i - 1) % 2,
                                                  g * 128 + 127:g * 128 + 128]
                    r = e.tensor_tensor_scan(
                        sTb[:, i % 2, g * 128:(g + 1) * 128],
                        fTs[:, g * 128:(g + 1) * 128],
                        ub[:, g * 128:(g + 1) * 128],
                        init, ALU.mult, ALU.add)
                return r
            ev_scan[i] = tl.add("vector", mk_scans, waits=w, inc=sv)
            if debug_taps and i == 0:
                tl.add("gpsimd", dma(dbg3[0], xb[:, 0, :]), waits=[ev_scan[0]],
                       inc=s_dbg, incn=16)
                tl.add("gpsimd", dma(dbg3[1], ub[:]), waits=[ev_scan[0]],
                       inc=s_dbg, incn=16)
                tl.add("gpsimd", dma(dbg3[2], sTb[:, 0, :]), waits=[ev_scan[0]],
                       inc=s_dbg, incn=16)
                tl.add("gpsimd", dma(dbg3[3], tpre[:]), waits=[ev_scan[0]],
                       inc=s_dbg, incn=16)
                def _cast_xn(e):
                    return e.dma_start(out=dbg3[4], in_=xnb[:])
                tl.add("gpsimd", _cast_xn, waits=[ev_scan[0]],
                       inc=s_dbg, incn=16)
                tl.add("gpsimd", dma(dbg3[5][:, 0:16], sta[:, 0, :]),
                       waits=[ev_scan[0]], inc=s_dbg, incn=16)

            if not real:
                return
            ev_stbpe[i] = tl.add("tensor", mk_tr(P3, sTb[:, i % 2, :]),
                                 waits=[ev_scan[i], ev_o[i]], inc=st_)
            ev_y[i] = tl.add("vector", lambda e: e.tensor_mul(yb[:], ob[:], P3[:]),
                             waits=[ev_stbpe[i]], inc=sv, nofence=True)
            w = []
            if i - 2 >= NTW:
                w.append(ev_ln2[i - 2])
            ev_ret[i] = tl.add("vector", lambda e, i=i: e.tensor_add(
                retb[:, i % 2, :], ln1b[:, i % 2, :], yb[:]), waits=w, inc=sv,
                nofence=True)
            w = []
            if i - 2 >= NTW and ev_hid[i - 2] is not None:
                w.append(ev_hid[i - 2])
            tl.add("vector", lambda e, i=i: e.reduce_sum(
                st2[:, i % 2, 0:1], retb[:, i % 2, :], AX.X), waits=w,
                nofence=True)
            tl.add("vector", lambda e, i=i: e.scalar_tensor_tensor(
                junk[:], retb[:, i % 2, :], 1.0, retb[:, i % 2, :], ALU.mult,
                ALU.mult, accum_out=st2[:, i % 2, 1:2]))
            tl.add("vector", lambda e, i=i: e.tensor_scalar(
                st2[:, i % 2, 2:3], st2[:, i % 2, 0:1], -1.0 / D, None, ALU.mult))
            tl.add("vector", lambda e, i=i: e.tensor_mul(
                junk[:, 0:1], st2[:, i % 2, 2:3], st2[:, i % 2, 2:3]))
            tl.add("vector", lambda e, i=i: e.tensor_scalar(
                junk[:, 1:2], st2[:, i % 2, 1:2], 1.0 / D, 1e-5, ALU.mult, ALU.add))
            ev_st2[i] = tl.add("vector", lambda e, i=i: e.tensor_sub(
                st2[:, i % 2, 3:4], junk[:, 1:2], junk[:, 0:1]), inc=sv)

        def stageB(i):
            ev_r2[i] = tl.add("vector", lambda e, i=i: e.reciprocal(
                st2[:, i % 2, 5:6], st2[:, i % 2, 4:5]), waits=[ev_sq2[i]], inc=sv)
            ev_ln2[i] = tl.add("vector", lambda e, i=i: e.tensor_scalar(
                ln2b[:], retb[:, i % 2, :], st2[:, i % 2, 2:3],
                st2[:, i % 2, 5:6], ALU.add, ALU.mult),
                waits=[ev_r2[i]], inc=sv)
            w = [ev_ln2[i]]
            if i - 2 >= NTW:
                w.append(ev_xmdma[i - 2])
            ev_xm[i] = tl.add("vector", lambda e, i=i: e.tensor_add(
                xmb[:, i % 2, :], ln2b[:], xb[:, i % 3, :]), waits=w, inc=sv,
                nofence=True)
            ev_xmdma[i] = tl.add("scalar", lambda e, i=i: e.dma_start(
                out=xpark.rearrange("(t p) d -> t p d", p=128)[i - NTW],
                in_=xmb[:, i % 2, :]), waits=[ev_xm[i]], inc=s_xp[i % 2], incn=16)
            tl.add("vector", lambda e, i=i: e.reduce_sum(
                st2[:, i % 2, 8:9], xmb[:, i % 2, :], AX.X), nofence=True)
            tl.add("vector", lambda e, i=i: e.scalar_tensor_tensor(
                junk[:], xmb[:, i % 2, :], 1.0, xmb[:, i % 2, :], ALU.mult,
                ALU.mult, accum_out=st2[:, i % 2, 9:10]))
            tl.add("vector", lambda e, i=i: e.tensor_scalar(
                st2[:, i % 2, 10:11], st2[:, i % 2, 8:9], -1.0 / D, None, ALU.mult))
            tl.add("vector", lambda e, i=i: e.tensor_mul(
                junk[:, 0:1], st2[:, i % 2, 10:11], st2[:, i % 2, 10:11]))
            tl.add("vector", lambda e, i=i: e.tensor_scalar(
                junk[:, 1:2], st2[:, i % 2, 9:10], 1.0 / D, 1e-5, ALU.mult, ALU.add))
            ev_st3[i] = tl.add("vector", lambda e, i=i: e.tensor_sub(
                st2[:, i % 2, 11:12], junk[:, 1:2], junk[:, 0:1]), inc=sv)

        def stageC(i):
            ev_r3 = tl.add("vector", lambda e, i=i: e.reciprocal(
                st2[:, i % 2, 13:14], st2[:, i % 2, 12:13]), waits=[ev_sq3[i]],
                inc=sv)
            w = [ev_r3]
            if i - 2 >= NTW and ev_hTpe[i - 2] is not None:
                w.append(ev_hTpe[i - 2])
                w.append(ev_hdma[i - 2])
            ev_hid[i] = tl.add("vector", lambda e, i=i: e.tensor_scalar(
                hbb[:, i % 2, :], xmb[:, i % 2, :], st2[:, i % 2, 10:11],
                st2[:, i % 2, 13:14], ALU.add, ALU.mult),
                waits=w, inc=sv)
            ev_hdma[i] = tl.add("scalar", lambda e, i=i: e.dma_start(
                out=hid_loc.rearrange("(t p) d -> t p d", p=128)[i - NTW],
                in_=hbb[:, i % 2, :]), waits=[ev_hid[i]], inc=s_hd[i % 2], incn=16)
            w = [ev_hid[i]]
            tslot = min(i + 2, NT - 1)
            w.append(ev_exnT[tslot])
            if i - 1 >= NTW and ev_ehT[i - 1] is not None:
                w.append(ev_ehT[i - 1])

            def mk_hT(e, i=i):
                r = None
                for g in range(KC):
                    r = e.transpose(PB[:, g * 128:(g + 1) * 128],
                                    hbb[:, i % 2, g * 128:(g + 1) * 128], identbb[:])
                return r
            ev_hTpe[i] = tl.add("tensor", mk_hT, waits=w, inc=st_)
            j = i - NTW
            ev_ehT[i] = tl.add("vector", lambda e, j=j: e.tensor_copy(
                hidT[:, :, j * 128:(j + 1) * 128],
                PB[:].rearrange("p (k n) -> p k n", k=KC)),
                waits=[ev_hTpe[i]], inc=sv, nofence=True)

        for i in range(NT):
            stageA(i)
            if i - 1 >= NTW:
                stageB(i - 1)
            if i - 2 >= NTW:
                stageC(i - 2)
        i = NT - 1
        ev_sq2[i] = tl.add("scalar", lambda e, i=i: e.activation(
            st2[:, i % 2, 4:5], st2[:, i % 2, 3:4], AF.Sqrt),
            waits=[ev_st2[i]], inc=ss)
        stageB(NT - 1)
        for i in (NT - 2, NT - 1):
            ev_sq3[i] = tl.add("scalar", lambda e, i=i: e.activation(
                st2[:, i % 2, 12:13], st2[:, i % 2, 11:12], AF.Sqrt),
                waits=[ev_st3[i]], inc=ss)
            stageC(i)

        if True:
            ev_gmm = [None] * NTR
            ev_sco = [None] * NTR
            for j in range(NTR):
                i = j + NTW

                def mk_gate(e, j=j):
                    r = None
                    for k in range(KC):
                        r = e.matmul(P4[:, 0:E], hidT[:, k, j * 128:(j + 1) * 128],
                                     gatb[:, k, :], start=(k == 0),
                                     stop=(k == KC - 1))
                    return r
                w = [ev_ehT[i], ev_gat]
                if j >= 1:
                    w.append(ev_sco[j - 1])
                ev_gmm[j] = tl.add("tensor", mk_gate, waits=w, inc=st_)
                tl.add("vector", lambda e: e.reduce_max(
                    gj[:, 2:3], P4[:, 0:E], AX.X), waits=[ev_gmm[j]])
                ev_mx = tl.add("vector", lambda e: e.tensor_scalar(
                    gj[:, 3:4], gj[:, 2:3], -1.0, None, ALU.mult), inc=sv)
                ev_ex = tl.add("scalar", lambda e, j=j: e.activation(
                    sco_all[:, j, :], P4[:, 0:E], AF.Exp, bias=gj[:, 3:4]),
                    waits=[ev_mx], inc=ss)
                tl.add("vector", lambda e, j=j: e.reduce_sum(
                    gj[:, 4:5], sco_all[:, j, :], AX.X), waits=[ev_ex])
                ev_rc = tl.add("vector", lambda e: e.reciprocal(
                    gj[:, 5:6], gj[:, 4:5]), inc=sv)
                ev_sco[j] = tl.add("vector", lambda e, j=j: e.tensor_scalar(
                    sco_all[:, j, :], sco_all[:, j, :], gj[:, 5:6], None,
                    ALU.mult), waits=[ev_rc], inc=sv)
            ev_scodma = tl.add("scalar", lambda e: e.dma_start(
                out=sco_loc.rearrange("(j p) e -> p j e", p=128), in_=sco_all[:]),
                waits=[ev_sco[NTR - 1]], inc=s_sc, incn=16)

        ev_aghid = tl.add("gpsimd", lambda e: e.collective_compute(
            "AllGather", ALU.bypass, replica_groups=groups,
            ins=[hid_loc.opt()], outs=[hid_full.opt()]),
            waits=[ev_hdma[NT - 1], ev_hdma[NT - 2]], inc=scc)
        ev_agsco = tl.add("gpsimd", lambda e: e.collective_compute(
            "AllGather", ALU.bypass, replica_groups=groups,
            ins=[sco_loc.opt()], outs=[sco_full.opt()]),
            waits=[ev_scodma], inc=scc)

        if debug_taps:
            tl.add("gpsimd", dma(dbg_xmid, xpark),
                   waits=[ev_xmdma[NT - 1], ev_xmdma[NT - 2]],
                   inc=s_dbg, incn=16)
            tl.add("gpsimd", dma(dbg_hid, hid_loc),
                   waits=[ev_hdma[NT - 1], ev_hdma[NT - 2]],
                   inc=s_dbg, incn=16)
            tl.add("gpsimd", dma(dbg_sco, sco_full), waits=[ev_agsco],
                   inc=s_dbg, incn=16)
            _i = NT - 1
            _dumps = [xb[:, _i % 3, :], ln1b[:, _i % 2, :], tpre[:], fb[:],
                      ob[:], ub[:], fTs[:], sTb[:, _i % 2, :], yb[:],
                      retb[:, _i % 2, :], xmb[:, _i % 2, :], junk[:]]
            _dw = [ev_agsco, ev_hdma[NT - 1], ev_xmdma[NT - 1],
                   ev_scan[NT - 1]]
            for _k, _src in enumerate(_dumps):
                tl.add("gpsimd", dma(dbg2[_k], _src), waits=_dw,
                       inc=s_dbg, incn=16)

        ev_last_efT = ev_efT[NT - 1]
        ev_last_scan = ev_scan[NT - 1]
        ev_last_y = ev_y[NT - 1]
        ev_last_ehT = ev_ehT[NT - 1]

    if attn_only:
        ev_fin = ev_hdma[NT - 1]
        if debug_taps:
            ev_fin = tl.add('gpsimd', lambda e: e.nop(), waits=[tl.cur(s_dbg)], inc=None) or ev_fin
        tl.add('sync', lambda e: None, waits=[ev_agsco, ev_fin])
        esA.close()
        _hidT_guard.__exit__(None, None, None)
    else:
        esA.close()
        ev_spark_last = None
        esS = contextlib.ExitStack()
        wsgs = esS.enter_context(nc.sbuf_tensor("wsgs", [128, KC, 2, 1024], BF16))
        wsus = esS.enter_context(nc.sbuf_tensor("wsus", [128, KC, 2, 1024], BF16))
        wsds = esS.enter_context(nc.sbuf_tensor("wsds", [128, MS, D], BF16))
        hsT = esS.enter_context(nc.sbuf_tensor("hsT", [128, MS, 1024], BF16))
        gtmp = esS.enter_context(nc.sbuf_tensor("gtmp", [128, 1024], BF16))
        shout = esS.enter_context(nc.sbuf_tensor("shout", [128, D], F32))
        if True:
            attn_dead = [ev_ehT[NT - 1], ev_hdma[NT - 1], ev_hdma[NT - 2],
                         ev_xmdma[NT - 1], ev_xmdma[NT - 2]]
            tl.add("sync", dma(wsds[:], wsd_b), waits=[ev_wcast["wsd"]] + attn_dead,
                   inc=s_wsh, incn=16)
            for hh in range(2):
                tl.add("sync", dma(
                    wsgs[:, :, hh, :], wsg_b[:, :, hh * 1024:(hh + 1) * 1024]),
                    waits=[ev_wcast["wsg"]] + attn_dead, inc=s_wsh, incn=16)
                ev_wsh_t = tl.add("sync", dma(
                    wsus[:, :, hh, :], wsu_b[:, :, hh * 1024:(hh + 1) * 1024]),
                    waits=[ev_wcast["wsu"]] + attn_dead, inc=s_wsh, incn=16)
            ev_wsd = ev_wsh_t
            ev_wsg_h = [ev_wsh_t, ev_wsh_t]
            ev_wsu_h = [ev_wsh_t, ev_wsh_t]

            ev_sh_h = {}
            ev_gel_prev = None
            ev_dn_half = [None, None]
            ev_shev = None
            for th in range(2):
                for m in range(MS):
                    hh = m // 8

                    def mk_g(e, th=th, m=m, hh=hh):
                        r = None
                        for tb in range(2):
                            tsl = slice(th * 1024 + tb * 512, th * 1024 + (tb + 1) * 512)
                            for k in range(KC):
                                r = e.matmul(
                                    P1[:, tb * 512:(tb + 1) * 512],
                                    wsgs[:, k, hh, (m % 8) * 128:(m % 8 + 1) * 128],
                                    hidT[:, k, tsl],
                                    start=(k == 0), stop=(k == KC - 1))
                        return r
                    w = [ev_wsg_h[hh], ev_last_ehT]
                    if (th, m - 1) in ev_sh_h:
                        w.append(ev_sh_h[(th, m - 1)])
                    if th == 1 and m == 0:
                        w.append(ev_sh_h[(0, MS - 1)])
                        w.append(ev_dn_half[0])
                    if th == 0 and m == 0:
                        w.append(ev_last_efT)
                        w.append(ev_last_scan)
                    ev_gmmS = tl.add("tensor", mk_g, waits=w, inc=st_)

                    def mk_u(e, th=th, m=m, hh=hh):
                        r = None
                        for tb in range(2):
                            tsl = slice(th * 1024 + tb * 512, th * 1024 + (tb + 1) * 512)
                            for k in range(KC):
                                r = e.matmul(
                                    P2[:, tb * 512:(tb + 1) * 512],
                                    wsus[:, k, hh, (m % 8) * 128:(m % 8 + 1) * 128],
                                    hidT[:, k, tsl],
                                    start=(k == 0), stop=(k == KC - 1))
                        return r
                    ev_ummS = tl.add("tensor", mk_u, waits=[ev_wsu_h[hh]] + w, inc=st_)
                    w2 = [ev_gmmS]
                    if ev_gel_prev is not None:
                        w2.append(ev_sh_h.get((th, m - 1)) or ev_sh_h.get((0, MS - 1)))
                    ev_gel = tl.add("scalar", lambda e: e.activation(
                        gtmp[:], P1[:], AF.Gelu), waits=w2, inc=ss)
                    ev_gel_prev = ev_gel
                    ev_sh_h[(th, m)] = tl.add("vector", lambda e, m=m: e.tensor_mul(
                        hsT[:, m, :], gtmp[:], P2[:]), waits=[ev_gel, ev_ummS], inc=sv,
                        nofence=True)
                ev_dn = None
                for tt in range(8):
                    def mk_dn(e, th=th, tt=tt):
                        r = None
                        for n in range(2):
                            for fc in range(MS):
                                r = e.matmul(P3[:, n * 512:(n + 1) * 512],
                                             hsT[:, fc, tt * 128:(tt + 1) * 128],
                                             wsds[:, fc, n * 512:(n + 1) * 512],
                                             start=(fc == 0), stop=(fc == MS - 1))
                        return r
                    w = [ev_sh_h[(th, MS - 1)], ev_wsd]
                    if ev_shev is not None:
                        w.append(ev_shev)
                    if th == 0 and tt == 0:
                        w.append(ev_last_y)
                    ev_dn = tl.add("tensor", mk_dn, waits=w, inc=st_)
                    w = [ev_dn]
                    if ev_spark_last is not None:
                        w.append(ev_spark_last)
                    ev_shev = tl.add("vector", lambda e: e.tensor_copy(
                        shout[:], P3[:]), waits=w, inc=sv, nofence=True)
                    ev_spark_last = tl.add("scalar", lambda e, th=th, tt=tt: e.dma_start(
                        out=spark.rearrange("(t p) d -> t p d", p=128)[th * 8 + tt],
                        in_=shout[:]), waits=[ev_shev], inc=s_sp, incn=16)
                ev_dn_half[th] = ev_dn
            ev_shev_last = ev_shev
            ev_sh_h_last = ev_sh_h[(1, MS - 1)]

        esS.close()
        _hidT_guard.__exit__(None, None, None)
        esT = contextlib.ExitStack()
        sco4 = esT.enter_context(nc.sbuf_tensor("sco4", [128, 64, E], F32))
        lot = esT.enter_context(nc.sbuf_tensor("lot", [1, E], F32))
        hit = esT.enter_context(nc.sbuf_tensor("hit", [1, E], F32))
        midt = esT.enter_context(nc.sbuf_tensor("midt", [1, E], F32))
        cntt = esT.enter_context(nc.sbuf_tensor("cntt", [1, E], F32))
        needt = esT.enter_context(nc.sbuf_tensor("needt", [1, E], F32))
        geb = esT.enter_context(nc.sbuf_tensor("geb", [1, 2 * E], I32))
        midv = esT.enter_context(nc.sbuf_tensor("midv", [128, E], F32))
        maskb = esT.enter_context(nc.sbuf_tensor("maskb", [128, 64 * E], F32))
        pc = esT.enter_context(nc.sbuf_tensor("pc", [128, E], F32))
        hiv = esT.enter_context(nc.sbuf_tensor("hiv", [128, E], F32))
        lov = esT.enter_context(nc.sbuf_tensor("lov", [128, E], F32))
        needv = esT.enter_context(nc.sbuf_tensor("needv", [128, E], F32))
        m_hi = esT.enter_context(nc.sbuf_tensor("m_hi", [128, 64], F32))
        m_bd = esT.enter_context(nc.sbuf_tensor("m_bd", [128, 64], F32))
        cum = esT.enter_context(nc.sbuf_tensor("cum", [128, 64], F32))
        rnk = esT.enter_context(nc.sbuf_tensor("rnk", [128, 64], F32))
        tk = esT.enter_context(nc.sbuf_tensor("tk", [128, 64], F32))
        selb = esT.enter_context(nc.sbuf_tensor("selb", [128, 64], F32))
        selbi = esT.enter_context(nc.sbuf_tensor("selbi", [128, 64], I32))
        dst_all = esT.enter_context(nc.sbuf_tensor("dst_all", [128, 64, E], F32))
        rowt = esT.enter_context(nc.sbuf_tensor("rowt", [128, 1], F32))
        offs = esT.enter_context(nc.sbuf_tensor("offs", [128, 1], F32))
        odst = esT.enter_context(nc.sbuf_tensor("odst", [128, 64], F32))
        if True:
            shared_dead = [ev_spark_last, ev_shev_last]
            ev_sco4 = tl.add("sync", dma(
                sco4[:], sco_full.rearrange("(p j) e -> p j e", p=128)),
                waits=[ev_agsco] + shared_dead, inc=s_misc, incn=16)
            tl.add("vector", lambda e: e.memset(lot[:], 0.0), waits=shared_dead)
            tl.add("vector", lambda e: e.memset(hit[:], 1.0))
            ev_upd = tl.add("vector", lambda e: e.memset(midt[:], 0.5), inc=sv)

            for it in range(BISECT_ITERS + 1):
                last = it == BISECT_ITERS
                src = hit if last else midt
                ev_midB = tl.add("tensor", lambda e, src=src: e.matmul(
                    P4[:, 16:16 + E], ones1b[:], src[:], start=True, stop=True),
                    waits=[ev_upd, ev_const, ev_sco[NTR - 1]], inc=st_)
                ev_mve = tl.add("vector", lambda e: e.tensor_copy(
                    midv[:], P4[:, 16:16 + E]), waits=[ev_midB], inc=sv)
                ev_mask = tl.add("vector", lambda e: e.tensor_tensor(
                    maskb[:].rearrange("p (j e) -> p j e", e=E), sco4[:],
                    midv[:].rearrange("p (a e) -> p a e", a=1).to_broadcast(
                        [128, 64, E]), ALU.is_ge), waits=[ev_mve, ev_sco4], inc=sv)
                ev_pc = tl.add("vector", lambda e: e.tensor_reduce(
                    pc[:], maskb[:].rearrange("p (j e) -> p e j", e=E), AX.X, ALU.add),
                    inc=sv)
                ev_cmm = tl.add("tensor", lambda e: e.matmul(
                    P4[0:1, 0:E], onescb[:], pc[:], start=True, stop=True),
                    waits=[ev_pc, ev_mve], inc=st_)
                ev_cnt = tl.add("vector", lambda e: e.tensor_copy(
                    cntt[:], P4[0:1, 0:E]), waits=[ev_cmm], inc=sv)
                if last:
                    ev_upd = tl.add("vector", lambda e: e.tensor_scalar(
                        needt[:], cntt[:], -1.0, float(CAP), ALU.mult, ALU.add), inc=sv)
                else:
                    tl.add("vector", lambda e: e.tensor_scalar(
                        geb[:, 0:E], cntt[:], float(CAP), None, ALU.is_ge))
                    tl.add("vector", lambda e: e.tensor_scalar(
                        geb[:, E:2 * E], cntt[:], float(CAP), None, ALU.is_lt))
                    tl.add("vector", lambda e: e.copy_predicated(
                        lot[:], geb[:, 0:E], midt[:]))
                    tl.add("vector", lambda e: e.copy_predicated(
                        hit[:], geb[:, E:2 * E], midt[:]))
                    tl.add("vector", lambda e: e.tensor_add(midt[:], lot[:], hit[:]))
                    ev_upd = tl.add("vector", lambda e: e.tensor_scalar(
                        midt[:], midt[:], 0.5, None, ALU.mult), inc=sv)

            ev_bcp = ev_upd
            for srct, dstv in ((lot, lov), (hit, hiv), (needt, needv)):
                evm = tl.add("tensor", lambda e, srct=srct: e.matmul(
                    P4[:, 16:16 + E], ones1b[:], srct[:], start=True, stop=True),
                    waits=[ev_bcp], inc=st_)
                ev_bcp = tl.add("vector", lambda e, dstv=dstv: e.tensor_copy(
                    dstv[:], P4[:, 16:16 + E]), waits=[evm], inc=sv)
            ev_bcast = ev_bcp

            ev_selp = ev_bcast
            for e_ in range(E):
                tl.add("vector", lambda e, e_=e_: e.tensor_tensor(
                    m_hi[:], sco4[:, :, e_], hiv[:, e_:e_ + 1].to_broadcast([128, 64]),
                    ALU.is_ge), waits=[ev_selp])
                tl.add("vector", lambda e, e_=e_: e.tensor_tensor(
                    m_bd[:], sco4[:, :, e_], lov[:, e_:e_ + 1].to_broadcast([128, 64]),
                    ALU.is_ge))
                tl.add("vector", lambda e: e.tensor_sub(m_bd[:], m_bd[:], m_hi[:]))
                tl.add("vector", lambda e: e.tensor_tensor_scan(
                    cum[:], m_bd[:], zzz[:, 0:64], 0.0, ALU.add, ALU.add),
                    waits=[ev_zzz])
                ev_rt = tl.add("vector", lambda e: e.tensor_copy(
                    rowt[:], cum[:, 63:64]), inc=sv)
                ev_om = tl.add("tensor", lambda e: e.matmul(
                    P4[:, 48:49], ltrib[:], rowt[:], start=True, stop=True),
                    waits=[ev_rt], inc=st_)
                ev_of = tl.add("vector", lambda e: e.tensor_copy(
                    offs[:], P4[:, 48:49]), waits=[ev_om], inc=sv)
                tl.add("vector", lambda e: e.scalar_tensor_tensor(
                    rnk[:], cum[:], offs[:], m_bd[:], ALU.add, ALU.subtract),
                    waits=[ev_of])
                tl.add("vector", lambda e, e_=e_: e.tensor_tensor(
                    tk[:], rnk[:], needv[:, e_:e_ + 1].to_broadcast([128, 64]),
                    ALU.is_lt))
                tl.add("vector", lambda e: e.tensor_mul(tk[:], tk[:], m_bd[:]))
                tl.add("vector", lambda e: e.tensor_add(selb[:], m_hi[:], tk[:]))
                tl.add("vector", lambda e: e.tensor_tensor_scan(
                    cum[:], selb[:], zzz[:, 0:64], 0.0, ALU.add, ALU.add))
                ev_rt2 = tl.add("vector", lambda e: e.tensor_copy(
                    rowt[:], cum[:, 63:64]), inc=sv)
                ev_om2 = tl.add("tensor", lambda e: e.matmul(
                    P4[:, 48:49], ltrib[:], rowt[:], start=True, stop=True),
                    waits=[ev_rt2], inc=st_)
                ev_of2 = tl.add("vector", lambda e: e.tensor_copy(
                    offs[:], P4[:, 48:49]), waits=[ev_om2], inc=sv)
                tl.add("vector", lambda e: e.scalar_tensor_tensor(
                    rnk[:], cum[:], offs[:], selb[:], ALU.add, ALU.subtract),
                    waits=[ev_of2])
                tl.add("vector", lambda e: e.tensor_copy(selbi[:], selb[:]))
                ev_selp = tl.add("vector", lambda e, e_=e_: e.select(
                    dst_all[:, :, e_], selbi[:], rnk[:], bigt[:]),
                    waits=[ev_big], inc=sv)

            ev_idxsc = {}
            ev_oi = {}
            for ee in range(4):
                w = [ev_selp]
                if ee > 0:
                    w.append(ev_idxsc[ee - 1])
                tl.add("vector", lambda e, ee=ee: e.tensor_tensor(
                    maskb[:].rearrange("p (j e) -> p j e", e=E), dst_all[:],
                    oh4b[:, ee, :].rearrange("p (a e) -> p a e", a=1).to_broadcast(
                        [128, 64, E]), ALU.mult), waits=w)
                tl.add("vector", lambda e: e.tensor_reduce(
                    odst[:], maskb[:].rearrange("p (j e) -> p j e", e=E), AX.X,
                    ALU.add))
                ev_oi[ee] = tl.add("vector", lambda e: e.tensor_copy(
                    odsti[:], odst[:]), inc=sv)
                evp = None
                for j in range(64):
                    evp = tl.add("gpsimd", lambda e, ee=ee, j=j: e.indirect_dma_start(
                        out=idxl_d[ee],
                        out_offset=IndirectOffsetOnAxis(odsti[:, j:j + 1], 0),
                        in_=tokidb[:, j:j + 1], in_offset=None,
                        bounds_check=regs['cap'], oob_is_err=False),
                        waits=[ev_oi[ee], ev_const] if j == 0 else [],
                        inc=s_isc, incn=16)
                ev_idxsc[ee] = evp

            if debug_taps:
                for ee in range(4):
                    tl.add("gpsimd", dma(dbg_idx[ee].rearrange("(c a) -> c a", a=1), idxl_d[ee]), waits=[ev_idxsc[3]],
                           inc=s_dbg, incn=16)

            esE = contextlib.ExitStack()
            xtokT = esE.enter_context(nc.sbuf_tensor("xtokT", [128, KC, CAP], BF16))
            hTx = esE.enter_context(nc.sbuf_tensor("hTx", [128, MF, CAP], BF16))
            wgs = esE.enter_context(nc.sbuf_tensor("wgs", [128, KC, 2, 1024], BF16))
            wus = esE.enter_context(nc.sbuf_tensor("wus", [128, KC, 2, 1024], BF16))
            wds = esE.enter_context(nc.sbuf_tensor("wds", [128, MF, D], BF16))
            gtmp2 = esE.enter_context(nc.sbuf_tensor("gtmp2", [128, 1024], BF16))
            if True:
                ev_wg_h = {}
                ev_wu_h = {}
                ev_wd_e = {}
                ev_lastmm_gu = {}
                ev_lastmm_d = {}
                ev_scat = {}
                ev_eo_evac = {}
                prev_scat_end = None
                ev_xg_last = None
                ev_ii = {}

                for ee in range(4):
                    w = [ev_idxsc[3]]
                    if ee > 0:
                        w.append(ev_it_prev)
                    ev_i8 = tl.add("sync", dma(
                        idx8[:], idxl_d[ee].rearrange("(a b) c -> a (b c)", a=8)),
                        waits=w, inc=s_misc, incn=16)
                    w = [ev_i8]
                    if ee > 0:
                        w.append(ev_xg_last)
                    ev_it = tl.add("tensor", lambda e: e.transpose(
                        P4[:, 64:72], idx8[:], identb32[0:8, 0:8]), waits=w, inc=st_)
                    ev_it_prev = ev_it
                    w = [ev_it]
                    if ee > 0:
                        w.append(ev_scat[(ee - 1, 7)])
                        w.append(ev_scat[(ee - 1, 6)])
                    ev_ii[ee] = tl.add("vector", lambda e: e.tensor_copy(
                        idxi[:], P4[:, 64:72]), waits=w, inc=sv)

                    ev_xg = None
                    ev_tw = None
                    for s in range(8):
                        w = [ev_ii[ee]]
                        if ev_tw is not None:
                            w.append(ev_tw)
                        ev_g1 = tl.add("gpsimd", lambda e, s=s: e.indirect_dma_start(
                            out=tw16[:], out_offset=None,
                            in_=sco_full,
                            in_offset=IndirectOffsetOnAxis(idxi[:, s:s + 1], 0),
                            bounds_check=None, oob_is_err=False),
                            waits=w, inc=s_tw, incn=16)
                        tl.add("vector", lambda e, ee=ee: e.tensor_mul(
                            tw16[:], tw16[:], oh4b[:, ee, :]), waits=[ev_g1],
                            nofence=True)
                        ev_tw = tl.add("vector", lambda e, s=s: e.tensor_reduce(
                            topw[:, s:s + 1], tw16[:], AX.X, ALU.add), inc=sv)
                        ev_xg = tl.add("gpsimd", lambda e, s=s: e.indirect_dma_start(
                            out=xtok[:, s, :], out_offset=None,
                            in_=hid_full,
                            in_offset=IndirectOffsetOnAxis(idxi[:, s:s + 1], 0),
                            bounds_check=None, oob_is_err=False),
                            waits=[ev_ii[ee], ev_agsco], inc=s_xg, incn=16)
                    ev_xg_last = ev_xg

                    ev_xT = None
                    ev_xTe = None
                    for s in range(8):
                        def mk_xT(e, s=s):
                            r = None
                            for g in range(KC):
                                r = e.transpose(PB[:, g * 128:(g + 1) * 128],
                                                xtok[:, s, g * 128:(g + 1) * 128],
                                                identbb[:])
                            return r
                        if ev_xTe is None:
                            w = [ev_xg, ev_last_ehT]
                            if ee > 0:
                                w.append(ev_prev_xTe)
                        else:
                            w = [ev_xTe]
                        ev_xT = tl.add("tensor", mk_xT, waits=w, inc=st_)
                        ev_xTe = tl.add("vector", lambda e, s=s: e.tensor_copy(
                            xtokT[:, :, s * 128:(s + 1) * 128],
                            PB[:].rearrange("p (k n) -> p k n", k=KC)),
                            waits=[ev_xT], inc=sv, nofence=True)
                    ev_xtokT = ev_xTe
                    ev_prev_xTe = ev_xTe

                    for hh in range(2):
                        w = [ev_wcast[f"wg{ee}"]]
                        if ee > 0:
                            w.append(ev_lastmm_gu.get((ee - 1, hh)))
                        else:
                            w.extend(shared_dead)
                        ev_wg_h[(ee, hh)] = tl.add("sync", dma(
                            wgs[:, :, hh, :],
                            wg_b[ee][:, :, hh * 1024:(hh + 1) * 1024]),
                            waits=w, inc=s_wg[hh], incn=16)
                        w = [ev_wcast[f"wu{ee}"]]
                        if ee > 0:
                            w.append(ev_lastmm_gu.get((ee - 1, hh)))
                        else:
                            w.extend(shared_dead)
                        ev_wu_h[(ee, hh)] = tl.add("sync", dma(
                            wus[:, :, hh, :],
                            wu_b[ee][:, :, hh * 1024:(hh + 1) * 1024]),
                            waits=w, inc=s_wu[hh], incn=16)
                    w = [ev_wcast[f"wd{ee}"]]
                    if ee > 0:
                        w.append(ev_lastmm_d.get(ee - 1))
                    else:
                        w.extend(shared_dead)
                    ev_wd_e[ee] = tl.add("sync", dma(wds[:], wd_b[ee]),
                                         waits=w, inc=s_wd, incn=16)

                    ev_h = {}
                    ev_gel2 = None
                    for m in range(MF):
                        hh = m // 8

                        def mk_gx(e, m=m, hh=hh):
                            r = None
                            for n in range(2):
                                for k in range(KC):
                                    r = e.matmul(
                                        P1[:, n * 512:(n + 1) * 512],
                                        wgs[:, k, hh, (m % 8) * 128:(m % 8 + 1) * 128],
                                        xtokT[:, k, n * 512:(n + 1) * 512],
                                        start=(k == 0), stop=(k == KC - 1))
                            return r
                        w = [ev_wg_h[(ee, hh)], ev_xtokT]
                        if m > 0:
                            w.append(ev_h[m - 1])
                        elif ee > 0:
                            w.append(ev_eo_evac[(ee - 1, 7)])
                        else:
                            w.append(ev_sh_h_last)
                        ev_gmx = tl.add("tensor", mk_gx, waits=w, inc=st_)

                        def mk_ux(e, m=m, hh=hh):
                            r = None
                            for n in range(2):
                                for k in range(KC):
                                    r = e.matmul(
                                        P2[:, n * 512:(n + 1) * 512],
                                        wus[:, k, hh, (m % 8) * 128:(m % 8 + 1) * 128],
                                        xtokT[:, k, n * 512:(n + 1) * 512],
                                        start=(k == 0), stop=(k == KC - 1))
                            return r
                        ev_umx = tl.add("tensor", mk_ux,
                                        waits=[ev_wu_h[(ee, hh)]] + w, inc=st_)
                        ev_lastmm_gu[(ee, hh)] = ev_umx
                        w2 = [ev_gmx]
                        if ev_gel2 is not None and m > 0:
                            w2.append(ev_h[m - 1])
                        ev_gel2 = tl.add("scalar", lambda e: e.activation(
                            gtmp2[:], P1[:], AF.Gelu), waits=w2, inc=ss)
                        ev_h[m] = tl.add("vector", lambda e, m=m: e.tensor_mul(
                            hTx[:, m, :], gtmp2[:], P2[:]),
                            waits=[ev_gel2, ev_umx], inc=sv, nofence=True)

                    ev_dn = None
                    for s in range(8):
                        def mk_dnx(e, s=s):
                            r = None
                            for n in range(2):
                                for fc in range(MF):
                                    r = e.matmul(P3[:, n * 512:(n + 1) * 512],
                                                 hTx[:, fc, s * 128:(s + 1) * 128],
                                                 wds[:, fc, n * 512:(n + 1) * 512],
                                                 start=(fc == 0), stop=(fc == MF - 1))
                            return r
                        w = [ev_h[MF - 1], ev_wd_e[ee]]
                        if ev_dn is not None:
                            w.append(ev_eo_evac[(ee, s - 1)])
                        elif ee == 0:
                            w.append(ev_shev_last)
                        else:
                            w.append(ev_eo_evac[(ee - 1, 7)])
                        ev_dn = tl.add("tensor", mk_dnx, waits=w, inc=st_)
                        ev_lastmm_d[ee] = ev_dn
                        w = [ev_dn, ev_tw]
                        if s >= 2:
                            w.append(ev_scat[(ee, s - 2)])
                        elif ee > 0:
                            w.append(ev_scat[(ee - 1, s + 6)])
                        ev_eo_evac[(ee, s)] = tl.add(
                            "vector", lambda e, s=s: e.tensor_scalar(
                                eo[:, s % 2, :], P3[:], topw[:, s:s + 1], None,
                                ALU.mult), waits=w, inc=sv, nofence=True)
                        w = [ev_eo_evac[(ee, s)], ev_dense0]
                        if s == 0 and prev_scat_end is not None:
                            w.append(prev_scat_end)
                        ev_scat[(ee, s)] = tl.add(
                            "gpsimd", lambda e, ee=ee, s=s: e.indirect_dma_start(
                                out=dense,
                                out_offset=IndirectOffsetOnAxis(idxi[:, s:s + 1], 0),
                                in_=eo[:, s % 2, :], in_offset=None,
                                bounds_check=None, oob_is_err=False,
                                compute_op=ALU.add),
                            waits=w, inc=s_eo[s % 2], incn=16)
                    prev_scat_end = ev_scat[(ee, 7)]

                ev_rs = tl.add("gpsimd", lambda e: e.collective_compute(
                    "ReduceScatter", ALU.add, replica_groups=groups,
                    ins=[dense.opt()], outs=[rs_out.opt()]),
                    waits=[prev_scat_end, ev_scat[(3, 6)]], inc=scc)

                if debug_taps:
                    tl.add("gpsimd", dma(dbg_dense, dense),
                           waits=[prev_scat_end, ev_scat[(3, 6)]],
                           inc=s_dbg, incn=16)

        esE.close()
        esT.close()
        esC = contextlib.ExitStack()
        rsb = esC.enter_context(nc.sbuf_tensor("rsb", [128, 2, D], F32))
        spb = esC.enter_context(nc.sbuf_tensor("spb", [128, 2, D], F32))
        xpb = esC.enter_context(nc.sbuf_tensor("xpb", [128, 2, D], F32))
        fo = esC.enter_context(nc.sbuf_tensor("fo", [128, D], F32))
        ln4 = esC.enter_context(nc.sbuf_tensor("ln4", [128, D], F32))
        ob2 = esC.enter_context(nc.sbuf_tensor("ob2", [128, D], F32))
        cst = esC.enter_context(nc.sbuf_tensor("cst", [128, 8], F32))
        cjk = esC.enter_context(nc.sbuf_tensor("cjk", [128, D], F32))
        if True:
            rsv = rs_out.rearrange("(t p) d -> t p d", p=128)
            spv = spark.rearrange("(t p) d -> t p d", p=128)
            xpv = xpark.rearrange("(t p) d -> t p d", p=128)
            ouv = out.rearrange("(t p) d -> t p d", p=128)
            ev_done = {}
            ev_odma = None
            ev_n4_prev = None
            for j in range(NTR):
                w = [ev_rs] if j < 2 else [ev_done[j - 2]]
                ev_l1 = tl.add("sync", dma(rsb[:, j % 2, :], rsv[j]),
                               waits=w, inc=s_crs[j % 2], incn=16)
                ev_l2 = tl.add("sync", dma(spb[:, j % 2, :], spv[j]),
                               waits=[ev_spark_last] + w, inc=s_csp[j % 2], incn=16)
                ev_l3 = tl.add("sync", dma(xpb[:, j % 2, :], xpv[j]),
                               waits=[ev_xmdma[NT - 1], ev_xmdma[NT - 2]] + w,
                               inc=s_cxp[j % 2], incn=16)
                w = [ev_l1, ev_l2]
                if ev_n4_prev is not None:
                    w.append(ev_n4_prev)
                tl.add("vector", lambda e, j=j: e.tensor_add(
                    fo[:], rsb[:, j % 2, :], spb[:, j % 2, :]), waits=w)
                tl.add("vector", lambda e: e.reduce_sum(cst[:, 0:1], fo[:], AX.X),
                       nofence=True)
                tl.add("vector", lambda e: e.scalar_tensor_tensor(
                    cjk[:], fo[:], 1.0, fo[:], ALU.mult, ALU.mult,
                    accum_out=cst[:, 1:2]))
                tl.add("vector", lambda e: e.tensor_scalar(
                    cst[:, 2:3], cst[:, 0:1], -1.0 / D, None, ALU.mult))
                tl.add("vector", lambda e: e.tensor_mul(
                    cjk[:, 0:1], cst[:, 2:3], cst[:, 2:3]))
                tl.add("vector", lambda e: e.tensor_scalar(
                    cjk[:, 1:2], cst[:, 1:2], 1.0 / D, 1e-5, ALU.mult, ALU.add))
                ev_v4 = tl.add("vector", lambda e: e.tensor_sub(
                    cst[:, 3:4], cjk[:, 1:2], cjk[:, 0:1]), inc=sv)
                ev_s4 = tl.add("scalar", lambda e: e.activation(
                    cst[:, 4:5], cst[:, 3:4], AF.Sqrt), waits=[ev_v4], inc=ss)
                ev_r4 = tl.add("vector", lambda e: e.reciprocal(
                    cst[:, 5:6], cst[:, 4:5]), waits=[ev_s4], inc=sv)
                ev_n4 = tl.add("vector", lambda e: e.tensor_scalar(
                    ln4[:], fo[:], cst[:, 2:3], cst[:, 5:6], ALU.add, ALU.mult),
                    waits=[ev_r4], inc=sv)
                ev_n4_prev = ev_n4
                w = [ev_n4, ev_l3]
                if ev_odma is not None:
                    w.append(ev_odma)
                ev_ob = tl.add("vector", lambda e, j=j: e.tensor_add(
                    ob2[:], ln4[:], xpb[:, j % 2, :]), waits=w, inc=sv,
                    nofence=True)
                ev_odma = tl.add("gpsimd", lambda e, j=j: e.dma_start(
                    out=ouv[j], in_=ob2[:]), waits=[ev_ob], inc=s_out, incn=16)
                ev_done[j] = ev_odma

            tl.add("sync", lambda e: None, waits=[ev_odma])


    if not attn_only:
        esC.close()
    with nc.Block() as block:
        @block.sync
        def _(e):
            tl.emit(e, "sync")

        @block.tensor
        def _(e):
            tl.emit(e, "tensor")

        @block.vector
        def _(e):
            tl.emit(e, "vector")

        @block.scalar
        def _(e):
            tl.emit(e, "scalar")

        @block.gpsimd
        def _(e):
            tl.emit(e, "gpsimd")

    es.close()
    nc.finalize()
    return nc


def _chunked(wT):
    Dd, Ff = wT.shape
    return np.ascontiguousarray(wT.reshape(Dd // 128, 128, Ff).transpose(1, 0, 2))


def make_inputs(inputs):
    x = np.asarray(inputs["L0_attn_x"], np.float32)
    wf = np.asarray(inputs["L0_attn_w_forget"], np.float32)
    wi = np.asarray(inputs["L0_attn_w_input"], np.float32)
    wo = np.asarray(inputs["L0_attn_w_output"], np.float32)
    nw = np.asarray(inputs["L0_attn_norm_w"], np.float32)
    gw = np.asarray(inputs["L0_ffn_gate_w"], np.float32)
    Wg = np.asarray(inputs["L0_ffn_Wg"], np.float32)
    Wu = np.asarray(inputs["L0_ffn_Wu"], np.float32)
    Wd = np.asarray(inputs["L0_ffn_Wd"], np.float32)
    Wsg = np.asarray(inputs["L0_ffn_Ws_g"], np.float32)
    Wsu = np.asarray(inputs["L0_ffn_Ws_u"], np.float32)
    Wsd = np.asarray(inputs["L0_ffn_Ws_d"], np.float32)

    wfT = _chunked(np.ascontiguousarray(wf.T))
    wiT = _chunked(np.ascontiguousarray(wi.T))
    woT = _chunked(np.ascontiguousarray(wo.T))
    gateT = _chunked(np.ascontiguousarray(gw.T))
    wsgT = _chunked(np.ascontiguousarray(Wsg.T))
    wsuT = _chunked(np.ascontiguousarray(Wsu.T))
    wsdT = _chunked(np.ascontiguousarray(Wsd.T))

    identm = np.eye(128, dtype=np.float32)
    k = np.arange(128)
    ltrim = (k[:, None] < k[None, :]).astype(np.float32)
    onesc = np.ones((128, 1), np.float32)
    ones1 = np.ones((1, 128), np.float32)
    tokidm = (np.arange(128)[:, None] * 64 + np.arange(64)[None, :]).astype(np.float32)

    in_maps = []
    for c in range(NCORE):
        b, q = c // 4, c % 4
        lo = q * TLOC - WARM
        if lo < 0:
            xw = np.concatenate([np.zeros((WARM, D), np.float32),
                                 x[b, 0:TLOC]], axis=0)
        else:
            xw = x[b, lo:(q + 1) * TLOC]
        ex = [4 * q + t for t in range(4)]
        wgTs = np.stack([_chunked(np.ascontiguousarray(Wg[e].T)) for e in ex])
        wuTs = np.stack([_chunked(np.ascontiguousarray(Wu[e].T)) for e in ex])
        wdTs = np.stack([_chunked(np.ascontiguousarray(Wd[e].T)) for e in ex])
        oh = np.zeros((128, 4, E), np.float32)
        for t in range(4):
            oh[:, t, 4 * q + t] = 1.0
        in_maps.append({
            "xin": np.ascontiguousarray(xw),
            "wfT": wfT, "wiT": wiT, "woT": woT,
            "normw": nw.reshape(1, D),
            "gateT": gateT, "wsgT": wsgT, "wsuT": wsuT, "wsdT": wsdT,
            "wgT": wgTs, "wuT": wuTs, "wdT": wdTs,
            "ident": identm, "ltri": ltrim, "onesc": onesc, "ones1": ones1,
            "tokid": tokidm, "oh4": oh,
        })
    return in_maps


def kernel(**inputs):
    nc = build_program()
    in_maps = make_inputs(inputs)
    res = run_bass_kernel_spmd(nc, in_maps, core_ids=list(range(NCORE)))
    parts = [res.results[c]["out"] for c in range(NCORE)]
    full = np.stack([np.concatenate(parts[0:4], axis=0),
                     np.concatenate(parts[4:8], axis=0)])
    return full.astype(np.float32)


# version 06
# speedup vs baseline: 1.0727x; 1.0727x over previous
import numpy as np
import concourse.bass as bass
import concourse.mybir as mybir
from concourse.bass_utils import run_bass_kernel_spmd
from concourse.bass_core_rust import IndirectOffsetOnAxis

F32 = mybir.dt.float32
BF16 = mybir.dt.bfloat16
I32 = mybir.dt.int32
AF = mybir.ActivationFunctionType
ALU = mybir.AluOpType
AX = mybir.AxisListType

B, T, D = 2, 8192, 1024
H, DH = 8, 128
E, CAP, FE, FS = 16, 1024, 2048, 2048
NCORE = 8
TLOC = 2048
WARM = 512
NTW = WARM // 128
NTR = TLOC // 128
NT = NTW + NTR
KC = D // 128
MF = FE // 128
MS = FS // 128
LN9 = 2.1972245773362196
BIG = 1.0e9
BISECT_ITERS = 42

ENGINES = ("sync", "tensor", "vector", "scalar", "gpsimd")


class TL:
    def __init__(self):
        self.ops = {e: [] for e in ENGINES}
        self.cnt = {}
        self.autofence = {}
        self.last = {}

    def add(self, eng, fn, waits=(), inc=None, incn=1, nofence=False):
        ws = [w for w in waits if w is not None]
        fence_sem = self.autofence.get(eng) if not nofence else None
        if fence_sem is not None:
            last = self.last.get(eng)
            if last is not None:
                ws.append(last)
            if inc is None:
                inc = fence_sem
        m = None
        if inc is not None:
            c = self.cnt.get(id(inc), 0) + incn
            self.cnt[id(inc)] = c
            m = (inc, c)
        if fence_sem is not None:
            self.last[eng] = m
        self.ops[eng].append((fn, ws, inc, incn))
        return m

    def cur(self, sem):
        return (sem, self.cnt.get(id(sem), 0))

    def emit(self, eng_obj, name):
        for fn, waits, inc, incn in self.ops[name]:
            best, order = {}, []
            for (sem, c) in waits:
                if id(sem) not in best:
                    order.append(sem)
                best[id(sem)] = max(best.get(id(sem), -1), c)
            for sem in order:
                eng_obj.wait_ge(sem, best[id(sem)])
            inst = fn(eng_obj)
            if inc is not None:
                if inst is None:
                    inst = eng_obj.nop()
                inst.then_inc(inc, incn)


def build_program(debug_taps=False, attn_only=False, race_detect=True):
    nc = bass.Bass("TRN2", target_bir_lowering=False,
                   detect_race_conditions=race_detect)

    xin = nc.dram_tensor("xin", [NT * 128, D], F32, kind="ExternalInput").ap()
    wfT = nc.dram_tensor("wfT", [128, KC, D], F32, kind="ExternalInput").ap()
    wiT = nc.dram_tensor("wiT", [128, KC, D], F32, kind="ExternalInput").ap()
    woT = nc.dram_tensor("woT", [128, KC, D], F32, kind="ExternalInput").ap()
    normw = nc.dram_tensor("normw", [1, D], F32, kind="ExternalInput").ap()
    gateT = nc.dram_tensor("gateT", [128, KC, E], F32, kind="ExternalInput").ap()
    wsgT = nc.dram_tensor("wsgT", [128, KC, FS], F32, kind="ExternalInput").ap()
    wsuT = nc.dram_tensor("wsuT", [128, KC, FS], F32, kind="ExternalInput").ap()
    wsdT = nc.dram_tensor("wsdT", [128, MS, D], F32, kind="ExternalInput").ap()
    wgT = nc.dram_tensor("wgT", [4, 128, KC, FE], F32, kind="ExternalInput").ap()
    wuT = nc.dram_tensor("wuT", [4, 128, KC, FE], F32, kind="ExternalInput").ap()
    wdT = nc.dram_tensor("wdT", [4, 128, MF, D], F32, kind="ExternalInput").ap()
    ident = nc.dram_tensor("ident", [128, 128], F32, kind="ExternalInput").ap()
    ltri = nc.dram_tensor("ltri", [128, 128], F32, kind="ExternalInput").ap()
    onesc_i = nc.dram_tensor("onesc", [128, 1], F32, kind="ExternalInput").ap()
    ones1_i = nc.dram_tensor("ones1", [1, 128], F32, kind="ExternalInput").ap()
    tokid = nc.dram_tensor("tokid", [128, 64], F32, kind="ExternalInput").ap()
    oh4 = nc.dram_tensor("oh4", [128, 4, E], F32, kind="ExternalInput").ap()
    out = nc.dram_tensor("out", [TLOC, D], F32, kind="ExternalOutput").ap()
    if debug_taps:
        dbg_xmid = nc.dram_tensor("dbg_xmid", [TLOC, D], F32, kind="ExternalOutput").ap()
        dbg_hid = nc.dram_tensor("dbg_hid", [TLOC, D], F32, kind="ExternalOutput").ap()
        dbg_sco = nc.dram_tensor("dbg_sco", [T, E], F32, kind="ExternalOutput").ap()
        dbg_idx = nc.dram_tensor("dbg_idx", [4, CAP], F32, kind="ExternalOutput").ap()
        dbg_dense = nc.dram_tensor("dbg_dense", [T, D], F32, kind="ExternalOutput").ap()
        dbg2 = nc.dram_tensor("dbg2", [12, 128, D], F32, kind="ExternalOutput").ap()
        dbg3 = nc.dram_tensor("dbg3", [6, 128, D], F32, kind="ExternalOutput").ap()

    wsg_b = nc.dram_tensor("wsg_b", [128, KC, FS], BF16).ap()
    wsu_b = nc.dram_tensor("wsu_b", [128, KC, FS], BF16).ap()
    wsd_b = nc.dram_tensor("wsd_b", [128, MS, D], BF16).ap()
    wg_b = nc.dram_tensor("wg_b", [4, 128, KC, FE], BF16).ap()
    wu_b = nc.dram_tensor("wu_b", [4, 128, KC, FE], BF16).ap()
    wd_b = nc.dram_tensor("wd_b", [4, 128, MF, D], BF16).ap()
    hid_loc = nc.dram_tensor("hid_loc", [TLOC, D], BF16).ap()
    hid_full = nc.dram_tensor("hid_full", [T, D], BF16).ap()
    sco_loc = nc.dram_tensor("sco_loc", [TLOC, E], F32).ap()
    sco_full = nc.dram_tensor("sco_full", [T, E], F32).ap()
    dense = nc.dram_tensor("dense", [T, D], F32).ap()
    rs_out = nc.dram_tensor("rs_out", [TLOC, D], F32).ap()
    xpark = nc.dram_tensor("xpark", [TLOC, D], F32).ap()
    spark = nc.dram_tensor("spark", [TLOC, D], F32).ap()
    idxl_d = [nc.dram_tensor(f"idxl_d{ee}", [CAP, 1], F32).ap()
              for ee in range(4)]

    xv = xin.rearrange("(t p) d -> t p d", p=128)
    tl = TL()
    groups = [[0, 1, 2, 3], [4, 5, 6, 7]]

    import contextlib
    es = contextlib.ExitStack()

    def mksem(name):
        return es.enter_context(nc.semaphore(name))
    scc = mksem("scc")
    sv = mksem("sv")
    ss = mksem("ss")
    st_ = mksem("st")
    s_const = mksem("s_const")
    s_dense = mksem("s_dense")
    s_x = [mksem(f"s_x{k}") for k in range(3)]
    s_wsh = mksem("s_wsh")
    s_misc = mksem("s_misc")
    s_wg = [mksem(f"s_wg{k}") for k in range(2)]
    s_wu = [mksem(f"s_wu{k}") for k in range(2)]
    s_wd = mksem("s_wd")
    s_crs = [mksem(f"s_crs{k}") for k in range(2)]
    s_csp = [mksem(f"s_csp{k}") for k in range(2)]
    s_cxp = [mksem(f"s_cxp{k}") for k in range(2)]
    s_xp = [mksem(f"s_xp{k}") for k in range(2)]
    s_hd = [mksem(f"s_hd{k}") for k in range(2)]
    s_sc = mksem("s_sc")
    s_sp = mksem("s_sp")
    s_identb = mksem("s_identb")
    s_attnw = mksem("s_attnw")
    s_cast = mksem("s_cast")
    s_isc = mksem("s_isc")
    s_tw = mksem("s_tw")
    s_xg = mksem("s_xg")
    s_eo = [mksem(f"s_eo{k}") for k in range(2)]
    s_out = mksem("s_out")
    s_dbg = mksem("s_dbg")
    tl.autofence = {"vector": sv}

    P1 = es.enter_context(nc.psum_tensor("P1", [128, 1024], F32))
    P2 = es.enter_context(nc.psum_tensor("P2", [128, 1024], F32))
    P3 = es.enter_context(nc.psum_tensor("P3", [128, 1024], F32))
    PB = es.enter_context(nc.psum_tensor("PB", [128, 1024], BF16))
    P4 = es.enter_context(nc.psum_tensor("P4", [128, 512], F32))

    identb32 = es.enter_context(nc.sbuf_tensor("identb32", [128, 128], F32))
    identbb = es.enter_context(nc.sbuf_tensor("identbb", [128, 128], BF16))
    ltrib = es.enter_context(nc.sbuf_tensor("ltrib", [128, 128], F32))
    onescb = es.enter_context(nc.sbuf_tensor("onescb", [128, 1], F32))
    ones1b = es.enter_context(nc.sbuf_tensor("ones1b", [1, 128], F32))
    tokidb = es.enter_context(nc.sbuf_tensor("tokidb", [128, 64], F32))
    oh4b = es.enter_context(nc.sbuf_tensor("oh4b", [128, 4, E], F32))
    zzz = es.enter_context(nc.sbuf_tensor("zzz", [128, 1024], F32))
    bigt = es.enter_context(nc.sbuf_tensor("bigt", [128, 64], F32))
    idx8 = es.enter_context(nc.sbuf_tensor("idx8", [8, 128], F32))
    idxi = es.enter_context(nc.sbuf_tensor("idxi", [128, 8], I32))
    tw16 = es.enter_context(nc.sbuf_tensor("tw16", [128, E], F32))
    topw = es.enter_context(nc.sbuf_tensor("topw", [128, 8], F32))
    odsti = es.enter_context(nc.sbuf_tensor("odsti", [128, 64], I32))
    xtok = es.enter_context(nc.sbuf_tensor("xtok", [128, 8, D], BF16))
    eo = es.enter_context(nc.sbuf_tensor("eo", [128, 2, D], F32))
    gatb = es.enter_context(nc.sbuf_tensor("gatb", [128, KC, E], BF16))
    sco_all = es.enter_context(nc.sbuf_tensor("sco_all", [128, NTR, E], F32))
    gj = es.enter_context(nc.sbuf_tensor("gj", [128, 8], F32))
    _hidT_guard = nc.sbuf_tensor("hidT", [128, KC, TLOC], BF16)
    hidT = _hidT_guard.__enter__()

    def dma(dst, src):
        return lambda e: e.dma_start(out=dst, in_=src)

    regs = {}
    def mk_regs(e):
        regs['cap'] = e.to_reg(CAP - 1)
        return None
    tl.add('gpsimd', mk_regs)

    ev_const = None
    for dst, src in ((identb32[:], ident), (ltrib[:], ltri), (onescb[:], onesc_i),
                     (ones1b[:], ones1_i), (tokidb[:], tokid), (oh4b[:], oh4)):
        ev_const = tl.add("sync", dma(dst, src), inc=s_const, incn=16)
    ev_identb = tl.add("gpsimd", dma(identbb[:], ident), inc=s_identb, incn=16)
    ev_zzz = tl.add("vector", lambda e: e.memset(zzz[:], 0.0), inc=sv)
    ev_big = tl.add("vector", lambda e: e.memset(bigt[:], BIG), inc=sv)



    esA = contextlib.ExitStack()
    wfb = esA.enter_context(nc.sbuf_tensor("wfb", [128, KC, D], BF16))
    wib = esA.enter_context(nc.sbuf_tensor("wib", [128, KC, D], BF16))
    wob = esA.enter_context(nc.sbuf_tensor("wob", [128, KC, D], BF16))
    normwb = esA.enter_context(nc.sbuf_tensor("normwb", [128, D], F32))
    nw1 = esA.enter_context(nc.sbuf_tensor("nw1", [1, D], F32))
    xb = esA.enter_context(nc.sbuf_tensor("xb", [128, 3, D], F32))
    sta = esA.enter_context(nc.sbuf_tensor("sta", [128, 2, 16], F32))
    st2 = esA.enter_context(nc.sbuf_tensor("st2", [128, 2, 16], F32))
    junk = esA.enter_context(nc.sbuf_tensor("junk", [128, D], F32))
    ln1b = esA.enter_context(nc.sbuf_tensor("ln1b", [128, 2, D], F32))
    tpre = esA.enter_context(nc.sbuf_tensor("tpre", [128, D], F32))
    xnb = esA.enter_context(nc.sbuf_tensor("xnb", [128, D], BF16))
    xnT = esA.enter_context(nc.sbuf_tensor("xnT", [128, KC, 128], BF16))
    fb = esA.enter_context(nc.sbuf_tensor("fb", [128, D], F32))
    ob = esA.enter_context(nc.sbuf_tensor("ob", [128, D], F32))
    vs = esA.enter_context(nc.sbuf_tensor("vs", [128, H], F32))
    ub = esA.enter_context(nc.sbuf_tensor("ub", [128, D], F32))
    fTs = esA.enter_context(nc.sbuf_tensor("fTs", [128, D], F32))
    sTb = esA.enter_context(nc.sbuf_tensor("sTb", [128, 2, D], F32))
    yb = esA.enter_context(nc.sbuf_tensor("yb", [128, D], F32))
    retb = esA.enter_context(nc.sbuf_tensor("retb", [128, 2, D], F32))
    ln2b = esA.enter_context(nc.sbuf_tensor("ln2b", [128, D], F32))
    xmb = esA.enter_context(nc.sbuf_tensor("xmb", [128, 2, D], F32))
    hbb = esA.enter_context(nc.sbuf_tensor("hbb", [128, 2, D], BF16))
    if True:
        tl.add("gpsimd", dma(wfb[:], wfT), inc=s_attnw, incn=16)
        tl.add("gpsimd", dma(wib[:], wiT), inc=s_attnw, incn=16)
        tl.add("gpsimd", dma(wob[:], woT), inc=s_attnw, incn=16)
        ev_attnw = tl.add("gpsimd", dma(gatb[:], gateT), inc=s_attnw, incn=16)
        ev_wf = ev_wi = ev_wo = ev_gat = ev_attnw
        ev_wcast = {}
        for name_, dst_, src_ in (("wsg", wsg_b, wsgT), ("wsu", wsu_b, wsuT),
                                  ("wsd", wsd_b, wsdT)):
            ev_wcast[name_] = tl.add("gpsimd", dma(dst_, src_), inc=s_cast, incn=16)
        for ee_ in range(4):
            ev_wcast[f"wg{ee_}"] = tl.add("gpsimd", dma(wg_b[ee_], wgT[ee_]),
                                          inc=s_cast, incn=16)
            ev_wcast[f"wu{ee_}"] = tl.add("gpsimd", dma(wu_b[ee_], wuT[ee_]),
                                          inc=s_cast, incn=16)
            ev_wcast[f"wd{ee_}"] = tl.add("gpsimd", dma(wd_b[ee_], wdT[ee_]),
                                          inc=s_cast, incn=16)
        ev_cast_all = ev_wcast["wd3"]
        for k_ in ev_wcast:
            ev_wcast[k_] = ev_cast_all
        ev_nw = tl.add("sync", dma(nw1[:], normw), inc=s_const, incn=16)
        ev_const = ev_nw

        def mk_nwb(e):
            e.matmul(P1[:, 0:512], ones1b[:], nw1[:, 0:512], start=True, stop=True)
            return e.matmul(P1[:, 512:1024], ones1b[:], nw1[:, 512:1024],
                            start=True, stop=True)
        ev_nwmm = tl.add("tensor", mk_nwb, waits=[ev_nw, ev_const], inc=st_)
        ev_nwb = tl.add("vector", lambda e: e.tensor_copy(normwb[:], P1[:]),
                        waits=[ev_nwmm], inc=sv)

        evx = [None] * NT
        ev_stats = [None] * NT
        ev_sq = [None] * NT
        ev_recip = [None] * NT
        ev_xn = [None] * NT
        ev_xnTpe = [None] * NT
        ev_exnT = [None] * NT
        ev_zf = [None] * NT
        ev_zv = [None] * NT
        ev_zo = [None] * NT
        ev_zfm = [None] * NT
        ev_f = [None] * NT
        ev_o = [None] * NT
        ev_vs = [None] * NT
        ev_u = [None] * NT
        ev_fTpe = [None] * NT
        ev_uTpe = [None] * NT
        ev_efT = [None] * NT
        ev_scan = [None] * NT
        ev_stbpe = [None] * NT
        ev_y = [None] * NT
        ev_ret = [None] * NT
        ev_st2 = [None] * NT
        ev_sq2 = [None] * NT
        ev_r2 = [None] * NT
        ev_ln2 = [None] * NT
        ev_xm = [None] * NT
        ev_st3 = [None] * NT
        ev_sq3 = [None] * NT
        ev_hid = [None] * NT
        ev_hTpe = [None] * NT
        ev_ehT = [None] * NT
        ev_xmdma = [None] * NT
        ev_hdma = [None] * NT

        def stageA(i):
            real = i >= NTW
            waits = [evx[i - 3], ev_xm[i - 3] if i - 3 >= NTW else ev_xn[i - 3]] if i >= 3 else []
            evx[i] = tl.add("sync", dma(xb[:, i % 3, :], xv[i]), waits=waits,
                            inc=s_x[i % 3], incn=16)
            tl.add("vector", lambda e, i=i: e.reduce_sum(
                sta[:, i % 2, 0:1], xb[:, i % 3, :], AX.X), waits=[evx[i]],
                nofence=True)
            tl.add("vector", lambda e, i=i: e.scalar_tensor_tensor(
                junk[:], xb[:, i % 3, :], 1.0, xb[:, i % 3, :], ALU.mult, ALU.mult,
                accum_out=sta[:, i % 2, 1:2]), waits=[evx[i]])
            tl.add("vector", lambda e, i=i: e.tensor_scalar(
                sta[:, i % 2, 2:3], sta[:, i % 2, 0:1], -1.0 / D, None, ALU.mult))
            tl.add("vector", lambda e, i=i: e.tensor_scalar(
                sta[:, i % 2, 3:4], sta[:, i % 2, 1:2], 1.0 / D, None, ALU.mult))
            tl.add("vector", lambda e, i=i: e.tensor_mul(
                sta[:, i % 2, 4:5], sta[:, i % 2, 2:3], sta[:, i % 2, 2:3]))
            tl.add("vector", lambda e, i=i: e.tensor_sub(
                sta[:, i % 2, 5:6], sta[:, i % 2, 3:4], sta[:, i % 2, 4:5]))
            if real:
                tl.add("vector", lambda e, i=i: e.tensor_scalar(
                    sta[:, i % 2, 6:7], sta[:, i % 2, 5:6], 1.0, 1e-5,
                    ALU.mult, ALU.add))
            ev_stats[i] = tl.add("vector", lambda e, i=i: e.tensor_scalar(
                sta[:, i % 2, 7:8], sta[:, i % 2, 5:6], 1.0 + 1e-6, 1e-11,
                ALU.mult, ALU.add), inc=sv)

            w = [ev_stats[i]]
            if real:
                tl.add("scalar", lambda e, i=i: e.activation(
                    sta[:, i % 2, 8:9], sta[:, i % 2, 6:7], AF.Sqrt), waits=w)
                w = []
            ev_sq[i] = tl.add("scalar", lambda e, i=i: e.activation(
                sta[:, i % 2, 9:10], sta[:, i % 2, 7:8], AF.Sqrt), waits=w, inc=ss)
            if i - 1 >= NTW and ev_st2[i - 1] is not None:
                ev_sq2[i - 1] = tl.add("scalar", lambda e, i=i: e.activation(
                    st2[:, (i - 1) % 2, 4:5], st2[:, (i - 1) % 2, 3:4], AF.Sqrt),
                    waits=[ev_st2[i - 1]], inc=ss)
            if i - 2 >= NTW and ev_st3[i - 2] is not None:
                ev_sq3[i - 2] = tl.add("scalar", lambda e, i=i: e.activation(
                    st2[:, (i - 2) % 2, 12:13], st2[:, (i - 2) % 2, 11:12], AF.Sqrt),
                    waits=[ev_st3[i - 2]], inc=ss)

            w = [ev_sq[i]]
            if real:
                tl.add("vector", lambda e, i=i: e.reciprocal(
                    sta[:, i % 2, 10:11], sta[:, i % 2, 8:9]), waits=w)
                w = []
            ev_recip[i] = tl.add("vector", lambda e, i=i: e.reciprocal(
                sta[:, i % 2, 11:12], sta[:, i % 2, 9:10]), waits=w, inc=sv)
            if real:
                w = [ev_recip[i]]
                if i - 2 >= NTW:
                    w.append(ev_ret[i - 2])
                tl.add("vector", lambda e, i=i: e.tensor_scalar(
                    ln1b[:, i % 2, :], xb[:, i % 3, :], sta[:, i % 2, 2:3],
                    sta[:, i % 2, 10:11], ALU.add, ALU.mult), waits=w)
            tl.add("vector", lambda e, i=i: e.tensor_scalar(
                tpre[:], xb[:, i % 3, :], sta[:, i % 2, 2:3], sta[:, i % 2, 11:12],
                ALU.add, ALU.mult), waits=[ev_recip[i]])
            w = [ev_nwb, ev_identb]
            if i >= 1:
                w.append(ev_xnTpe[i - 1])
            ev_xn[i] = tl.add("vector", lambda e: e.tensor_mul(
                xnb[:], tpre[:], normwb[:]), waits=w, inc=sv, nofence=True)

            w = [ev_xn[i]]
            if i >= 1:
                w.append(ev_exnT[i - 1])
            if i - 3 >= NTW and ev_ehT[i - 3] is not None:
                w.append(ev_ehT[i - 3])

            def mk_xnT(e):
                r = None
                for g in range(KC):
                    r = e.transpose(PB[:, g * 128:(g + 1) * 128],
                                    xnb[:, g * 128:(g + 1) * 128], identbb[:])
                return r
            ev_xnTpe[i] = tl.add("tensor", mk_xnT, waits=w, inc=st_)
            ev_exnT[i] = tl.add("vector", lambda e: e.tensor_copy(
                xnT[:].rearrange("p k n -> p (k n)"), PB[:]),
                waits=[ev_xnTpe[i]], inc=sv, nofence=True)

            def mk_proj(dst, wsb):
                def f(e):
                    r = None
                    for k in range(KC):
                        for n in range(2):
                            r = e.matmul(dst[:, n * 512:(n + 1) * 512],
                                         xnT[:, k, :],
                                         wsb[:, k, n * 512:(n + 1) * 512],
                                         start=(k == 0), stop=(k == KC - 1))
                    return r
                return f
            w = [ev_exnT[i], ev_wf]
            if i >= 1:
                w.append(ev_efT[i - 1])
            ev_zf[i] = tl.add("tensor", mk_proj(P1, wfb), waits=w, inc=st_)
            w = [ev_exnT[i], ev_wi]
            if i >= 1:
                w.append(ev_scan[i - 1])
            ev_zv[i] = tl.add("tensor", mk_proj(P2, wib), waits=w, inc=st_)
            if real:
                w = [ev_exnT[i], ev_wo]
                if i - 1 >= NTW:
                    w.append(ev_y[i - 1])
                ev_zo[i] = tl.add("tensor", mk_proj(P3, wob), waits=w, inc=st_)

            ev_zfm[i] = tl.add("vector", lambda e: e.tensor_scalar(
                junk[:], P1[:], LN9, None, ALU.max), waits=[ev_zf[i]], inc=sv,
                nofence=True)
            ev_f[i] = tl.add("scalar", lambda e: e.activation(
                fb[:], junk[:], AF.Sigmoid), waits=[ev_zfm[i]], inc=ss)
            if real:
                ev_o[i] = tl.add("scalar", lambda e: e.activation(
                    ob[:], P3[:], AF.Sigmoid), waits=[ev_zo[i]], inc=ss)
            ev_vs[i] = tl.add("vector", lambda e: e.tensor_reduce(
                vs[:], P2[:].rearrange("p (h d) -> p h d", h=H), AX.X, ALU.add),
                waits=[ev_zv[i]], inc=sv, nofence=True)
            tl.add("vector", lambda e: e.tensor_scalar(
                junk[:], fb[:], -1.0, 1.0, ALU.mult, ALU.add), waits=[ev_f[i]],
                nofence=True)
            ev_u[i] = tl.add("vector", lambda e: e.tensor_tensor(
                ub[:].rearrange("p (h d) -> p h d", h=H),
                junk[:].rearrange("p (h d) -> p h d", h=H),
                vs[:].to_broadcast([128, H, DH]), ALU.mult), inc=sv)

            def mk_tr(dst_ps, src):
                def f(e):
                    r = None
                    for g in range(KC):
                        r = e.transpose(dst_ps[:, g * 128:(g + 1) * 128],
                                        src[:, g * 128:(g + 1) * 128], identb32[:])
                    return r
                return f
            ev_fTpe[i] = tl.add("tensor", mk_tr(P1, fb[:]),
                                waits=[ev_f[i], ev_zfm[i]], inc=st_)
            ev_uTpe[i] = tl.add("tensor", mk_tr(P2, ub[:]),
                                waits=[ev_u[i], ev_vs[i]], inc=st_)
            ev_efT[i] = tl.add("vector", lambda e: e.tensor_copy(fTs[:], P1[:]),
                               waits=[ev_fTpe[i]], inc=sv, nofence=True)
            ev_euT = tl.add("vector", lambda e: e.tensor_copy(ub[:], P2[:]),
                            waits=[ev_uTpe[i]], inc=sv, nofence=True)

            w = [ev_efT[i], ev_euT]
            if i >= 1:
                w.append(ev_scan[i - 1])
            if i - 2 >= NTW:
                w.append(ev_stbpe[i - 2])

            def mk_scans(e, i=i):
                r = None
                for g in range(KC):
                    init = 0.0 if i == 0 else sTb[:, (i - 1) % 2,
                                                  g * 128 + 127:g * 128 + 128]
                    r = e.tensor_tensor_scan(
                        sTb[:, i % 2, g * 128:(g + 1) * 128],
                        fTs[:, g * 128:(g + 1) * 128],
                        ub[:, g * 128:(g + 1) * 128],
                        init, ALU.mult, ALU.add)
                return r
            ev_scan[i] = tl.add("vector", mk_scans, waits=w, inc=sv)
            if debug_taps and i == 0:
                tl.add("gpsimd", dma(dbg3[0], xb[:, 0, :]), waits=[ev_scan[0]],
                       inc=s_dbg, incn=16)
                tl.add("gpsimd", dma(dbg3[1], ub[:]), waits=[ev_scan[0]],
                       inc=s_dbg, incn=16)
                tl.add("gpsimd", dma(dbg3[2], sTb[:, 0, :]), waits=[ev_scan[0]],
                       inc=s_dbg, incn=16)
                tl.add("gpsimd", dma(dbg3[3], tpre[:]), waits=[ev_scan[0]],
                       inc=s_dbg, incn=16)
                def _cast_xn(e):
                    return e.dma_start(out=dbg3[4], in_=xnb[:])
                tl.add("gpsimd", _cast_xn, waits=[ev_scan[0]],
                       inc=s_dbg, incn=16)
                tl.add("gpsimd", dma(dbg3[5][:, 0:16], sta[:, 0, :]),
                       waits=[ev_scan[0]], inc=s_dbg, incn=16)

            if not real:
                return
            ev_stbpe[i] = tl.add("tensor", mk_tr(P3, sTb[:, i % 2, :]),
                                 waits=[ev_scan[i], ev_o[i]], inc=st_)
            ev_y[i] = tl.add("vector", lambda e: e.tensor_mul(yb[:], ob[:], P3[:]),
                             waits=[ev_stbpe[i]], inc=sv, nofence=True)
            w = []
            if i - 2 >= NTW:
                w.append(ev_ln2[i - 2])
            ev_ret[i] = tl.add("vector", lambda e, i=i: e.tensor_add(
                retb[:, i % 2, :], ln1b[:, i % 2, :], yb[:]), waits=w, inc=sv,
                nofence=True)
            w = []
            if i - 2 >= NTW and ev_hid[i - 2] is not None:
                w.append(ev_hid[i - 2])
            tl.add("vector", lambda e, i=i: e.reduce_sum(
                st2[:, i % 2, 0:1], retb[:, i % 2, :], AX.X), waits=w,
                nofence=True)
            tl.add("vector", lambda e, i=i: e.scalar_tensor_tensor(
                junk[:], retb[:, i % 2, :], 1.0, retb[:, i % 2, :], ALU.mult,
                ALU.mult, accum_out=st2[:, i % 2, 1:2]))
            tl.add("vector", lambda e, i=i: e.tensor_scalar(
                st2[:, i % 2, 2:3], st2[:, i % 2, 0:1], -1.0 / D, None, ALU.mult))
            tl.add("vector", lambda e, i=i: e.tensor_mul(
                junk[:, 0:1], st2[:, i % 2, 2:3], st2[:, i % 2, 2:3]))
            tl.add("vector", lambda e, i=i: e.tensor_scalar(
                junk[:, 1:2], st2[:, i % 2, 1:2], 1.0 / D, 1e-5, ALU.mult, ALU.add))
            ev_st2[i] = tl.add("vector", lambda e, i=i: e.tensor_sub(
                st2[:, i % 2, 3:4], junk[:, 1:2], junk[:, 0:1]), inc=sv)

        def stageB(i):
            ev_r2[i] = tl.add("vector", lambda e, i=i: e.reciprocal(
                st2[:, i % 2, 5:6], st2[:, i % 2, 4:5]), waits=[ev_sq2[i]], inc=sv)
            ev_ln2[i] = tl.add("vector", lambda e, i=i: e.tensor_scalar(
                ln2b[:], retb[:, i % 2, :], st2[:, i % 2, 2:3],
                st2[:, i % 2, 5:6], ALU.add, ALU.mult),
                waits=[ev_r2[i]], inc=sv)
            w = [ev_ln2[i]]
            if i - 2 >= NTW:
                w.append(ev_xmdma[i - 2])
            ev_xm[i] = tl.add("vector", lambda e, i=i: e.tensor_add(
                xmb[:, i % 2, :], ln2b[:], xb[:, i % 3, :]), waits=w, inc=sv,
                nofence=True)
            ev_xmdma[i] = tl.add("scalar", lambda e, i=i: e.dma_start(
                out=xpark.rearrange("(t p) d -> t p d", p=128)[i - NTW],
                in_=xmb[:, i % 2, :]), waits=[ev_xm[i]], inc=s_xp[i % 2], incn=16)
            tl.add("vector", lambda e, i=i: e.reduce_sum(
                st2[:, i % 2, 8:9], xmb[:, i % 2, :], AX.X), nofence=True)
            tl.add("vector", lambda e, i=i: e.scalar_tensor_tensor(
                junk[:], xmb[:, i % 2, :], 1.0, xmb[:, i % 2, :], ALU.mult,
                ALU.mult, accum_out=st2[:, i % 2, 9:10]))
            tl.add("vector", lambda e, i=i: e.tensor_scalar(
                st2[:, i % 2, 10:11], st2[:, i % 2, 8:9], -1.0 / D, None, ALU.mult))
            tl.add("vector", lambda e, i=i: e.tensor_mul(
                junk[:, 0:1], st2[:, i % 2, 10:11], st2[:, i % 2, 10:11]))
            tl.add("vector", lambda e, i=i: e.tensor_scalar(
                junk[:, 1:2], st2[:, i % 2, 9:10], 1.0 / D, 1e-5, ALU.mult, ALU.add))
            ev_st3[i] = tl.add("vector", lambda e, i=i: e.tensor_sub(
                st2[:, i % 2, 11:12], junk[:, 1:2], junk[:, 0:1]), inc=sv)

        def stageC(i):
            ev_r3 = tl.add("vector", lambda e, i=i: e.reciprocal(
                st2[:, i % 2, 13:14], st2[:, i % 2, 12:13]), waits=[ev_sq3[i]],
                inc=sv)
            w = [ev_r3]
            if i - 2 >= NTW and ev_hTpe[i - 2] is not None:
                w.append(ev_hTpe[i - 2])
                w.append(ev_hdma[i - 2])
            ev_hid[i] = tl.add("vector", lambda e, i=i: e.tensor_scalar(
                hbb[:, i % 2, :], xmb[:, i % 2, :], st2[:, i % 2, 10:11],
                st2[:, i % 2, 13:14], ALU.add, ALU.mult),
                waits=w, inc=sv)
            ev_hdma[i] = tl.add("scalar", lambda e, i=i: e.dma_start(
                out=hid_loc.rearrange("(t p) d -> t p d", p=128)[i - NTW],
                in_=hbb[:, i % 2, :]), waits=[ev_hid[i]], inc=s_hd[i % 2], incn=16)
            w = [ev_hid[i]]
            tslot = min(i + 2, NT - 1)
            w.append(ev_exnT[tslot])
            if i - 1 >= NTW and ev_ehT[i - 1] is not None:
                w.append(ev_ehT[i - 1])

            def mk_hT(e, i=i):
                r = None
                for g in range(KC):
                    r = e.transpose(PB[:, g * 128:(g + 1) * 128],
                                    hbb[:, i % 2, g * 128:(g + 1) * 128], identbb[:])
                return r
            ev_hTpe[i] = tl.add("tensor", mk_hT, waits=w, inc=st_)
            j = i - NTW
            ev_ehT[i] = tl.add("vector", lambda e, j=j: e.tensor_copy(
                hidT[:, :, j * 128:(j + 1) * 128],
                PB[:].rearrange("p (k n) -> p k n", k=KC)),
                waits=[ev_hTpe[i]], inc=sv, nofence=True)

        for i in range(NT):
            stageA(i)
            if i - 1 >= NTW:
                stageB(i - 1)
            if i - 2 >= NTW:
                stageC(i - 2)
        i = NT - 1
        ev_sq2[i] = tl.add("scalar", lambda e, i=i: e.activation(
            st2[:, i % 2, 4:5], st2[:, i % 2, 3:4], AF.Sqrt),
            waits=[ev_st2[i]], inc=ss)
        stageB(NT - 1)
        for i in (NT - 2, NT - 1):
            ev_sq3[i] = tl.add("scalar", lambda e, i=i: e.activation(
                st2[:, i % 2, 12:13], st2[:, i % 2, 11:12], AF.Sqrt),
                waits=[ev_st3[i]], inc=ss)
            stageC(i)

        dnv = dense.rearrange("(n p) d -> n p d", p=128)
        ev_dense0 = None
        for n_ in range(T // 128):
            ev_dense0 = tl.add("scalar", dma(dnv[n_], zzz[:]), waits=[ev_zzz],
                               inc=s_dense, incn=16)

        if True:
            ev_gmm = [None] * NTR
            ev_sco = [None] * NTR
            for j in range(NTR):
                i = j + NTW

                def mk_gate(e, j=j):
                    r = None
                    for k in range(KC):
                        r = e.matmul(P4[:, 0:E], hidT[:, k, j * 128:(j + 1) * 128],
                                     gatb[:, k, :], start=(k == 0),
                                     stop=(k == KC - 1))
                    return r
                w = [ev_ehT[i], ev_gat]
                if j >= 1:
                    w.append(ev_sco[j - 1])
                ev_gmm[j] = tl.add("tensor", mk_gate, waits=w, inc=st_)
                tl.add("vector", lambda e: e.reduce_max(
                    gj[:, 2:3], P4[:, 0:E], AX.X), waits=[ev_gmm[j]])
                ev_mx = tl.add("vector", lambda e: e.tensor_scalar(
                    gj[:, 3:4], gj[:, 2:3], -1.0, None, ALU.mult), inc=sv)
                ev_ex = tl.add("scalar", lambda e, j=j: e.activation(
                    sco_all[:, j, :], P4[:, 0:E], AF.Exp, bias=gj[:, 3:4]),
                    waits=[ev_mx], inc=ss)
                tl.add("vector", lambda e, j=j: e.reduce_sum(
                    gj[:, 4:5], sco_all[:, j, :], AX.X), waits=[ev_ex])
                ev_rc = tl.add("vector", lambda e: e.reciprocal(
                    gj[:, 5:6], gj[:, 4:5]), inc=sv)
                ev_sco[j] = tl.add("vector", lambda e, j=j: e.tensor_scalar(
                    sco_all[:, j, :], sco_all[:, j, :], gj[:, 5:6], None,
                    ALU.mult), waits=[ev_rc], inc=sv)
            ev_scodma = tl.add("scalar", lambda e: e.dma_start(
                out=sco_loc.rearrange("(j p) e -> p j e", p=128), in_=sco_all[:]),
                waits=[ev_sco[NTR - 1]], inc=s_sc, incn=16)

        ev_aghid = tl.add("gpsimd", lambda e: e.collective_compute(
            "AllGather", ALU.bypass, replica_groups=groups,
            ins=[hid_loc.opt()], outs=[hid_full.opt()]),
            waits=[ev_hdma[NT - 1], ev_hdma[NT - 2]], inc=scc)
        ev_agsco = tl.add("gpsimd", lambda e: e.collective_compute(
            "AllGather", ALU.bypass, replica_groups=groups,
            ins=[sco_loc.opt()], outs=[sco_full.opt()]),
            waits=[ev_scodma], inc=scc)

        if debug_taps:
            tl.add("gpsimd", dma(dbg_xmid, xpark),
                   waits=[ev_xmdma[NT - 1], ev_xmdma[NT - 2]],
                   inc=s_dbg, incn=16)
            tl.add("gpsimd", dma(dbg_hid, hid_loc),
                   waits=[ev_hdma[NT - 1], ev_hdma[NT - 2]],
                   inc=s_dbg, incn=16)
            tl.add("gpsimd", dma(dbg_sco, sco_full), waits=[ev_agsco],
                   inc=s_dbg, incn=16)
            _i = NT - 1
            _dumps = [xb[:, _i % 3, :], ln1b[:, _i % 2, :], tpre[:], fb[:],
                      ob[:], ub[:], fTs[:], sTb[:, _i % 2, :], yb[:],
                      retb[:, _i % 2, :], xmb[:, _i % 2, :], junk[:]]
            _dw = [ev_agsco, ev_hdma[NT - 1], ev_xmdma[NT - 1],
                   ev_scan[NT - 1]]
            for _k, _src in enumerate(_dumps):
                tl.add("gpsimd", dma(dbg2[_k], _src), waits=_dw,
                       inc=s_dbg, incn=16)

        ev_last_efT = ev_efT[NT - 1]
        ev_last_scan = ev_scan[NT - 1]
        ev_last_y = ev_y[NT - 1]
        ev_last_ehT = ev_ehT[NT - 1]

    if attn_only:
        ev_fin = ev_hdma[NT - 1]
        if debug_taps:
            ev_fin = tl.add('gpsimd', lambda e: e.nop(), waits=[tl.cur(s_dbg)], inc=None) or ev_fin
        tl.add('sync', lambda e: None, waits=[ev_agsco, ev_fin])
        esA.close()
        _hidT_guard.__exit__(None, None, None)
    else:
        esA.close()
        ev_spark_last = None
        esS = contextlib.ExitStack()
        wsgs = esS.enter_context(nc.sbuf_tensor("wsgs", [128, KC, 2, 1024], BF16))
        wsus = esS.enter_context(nc.sbuf_tensor("wsus", [128, KC, 2, 1024], BF16))
        wsds = esS.enter_context(nc.sbuf_tensor("wsds", [128, MS, D], BF16))
        hsT = esS.enter_context(nc.sbuf_tensor("hsT", [128, MS, 1024], BF16))
        gtmp = esS.enter_context(nc.sbuf_tensor("gtmp", [128, 1024], BF16))
        shout = esS.enter_context(nc.sbuf_tensor("shout", [128, D], F32))
        if True:
            attn_dead = [ev_ehT[NT - 1], ev_hdma[NT - 1], ev_hdma[NT - 2],
                         ev_xmdma[NT - 1], ev_xmdma[NT - 2]]
            tl.add("sync", dma(wsds[:], wsd_b), waits=[ev_wcast["wsd"]] + attn_dead,
                   inc=s_wsh, incn=16)
            for hh in range(2):
                tl.add("sync", dma(
                    wsgs[:, :, hh, :], wsg_b[:, :, hh * 1024:(hh + 1) * 1024]),
                    waits=[ev_wcast["wsg"]] + attn_dead, inc=s_wsh, incn=16)
                ev_wsh_t = tl.add("sync", dma(
                    wsus[:, :, hh, :], wsu_b[:, :, hh * 1024:(hh + 1) * 1024]),
                    waits=[ev_wcast["wsu"]] + attn_dead, inc=s_wsh, incn=16)
            ev_wsd = ev_wsh_t
            ev_wsg_h = [ev_wsh_t, ev_wsh_t]
            ev_wsu_h = [ev_wsh_t, ev_wsh_t]

            ev_sh_h = {}
            ev_gel_prev = None
            ev_dn_half = [None, None]
            ev_shev = None
            for th in range(2):
                for m in range(MS):
                    hh = m // 8

                    def mk_g(e, th=th, m=m, hh=hh):
                        r = None
                        for tb in range(2):
                            tsl = slice(th * 1024 + tb * 512, th * 1024 + (tb + 1) * 512)
                            for k in range(KC):
                                r = e.matmul(
                                    P1[:, tb * 512:(tb + 1) * 512],
                                    wsgs[:, k, hh, (m % 8) * 128:(m % 8 + 1) * 128],
                                    hidT[:, k, tsl],
                                    start=(k == 0), stop=(k == KC - 1))
                        return r
                    w = [ev_wsg_h[hh], ev_last_ehT]
                    if (th, m - 1) in ev_sh_h:
                        w.append(ev_sh_h[(th, m - 1)])
                    if th == 1 and m == 0:
                        w.append(ev_sh_h[(0, MS - 1)])
                        w.append(ev_dn_half[0])
                    if th == 0 and m == 0:
                        w.append(ev_last_efT)
                        w.append(ev_last_scan)
                    ev_gmmS = tl.add("tensor", mk_g, waits=w, inc=st_)

                    def mk_u(e, th=th, m=m, hh=hh):
                        r = None
                        for tb in range(2):
                            tsl = slice(th * 1024 + tb * 512, th * 1024 + (tb + 1) * 512)
                            for k in range(KC):
                                r = e.matmul(
                                    P2[:, tb * 512:(tb + 1) * 512],
                                    wsus[:, k, hh, (m % 8) * 128:(m % 8 + 1) * 128],
                                    hidT[:, k, tsl],
                                    start=(k == 0), stop=(k == KC - 1))
                        return r
                    ev_ummS = tl.add("tensor", mk_u, waits=[ev_wsu_h[hh]] + w, inc=st_)
                    w2 = [ev_gmmS]
                    if ev_gel_prev is not None:
                        w2.append(ev_sh_h.get((th, m - 1)) or ev_sh_h.get((0, MS - 1)))
                    ev_gel = tl.add("scalar", lambda e: e.activation(
                        gtmp[:], P1[:], AF.Gelu), waits=w2, inc=ss)
                    ev_gel_prev = ev_gel
                    ev_sh_h[(th, m)] = tl.add("vector", lambda e, m=m: e.tensor_mul(
                        hsT[:, m, :], gtmp[:], P2[:]), waits=[ev_gel, ev_ummS], inc=sv,
                        nofence=True)
                ev_dn = None
                for tt in range(8):
                    def mk_dn(e, th=th, tt=tt):
                        r = None
                        for n in range(2):
                            for fc in range(MS):
                                r = e.matmul(P3[:, n * 512:(n + 1) * 512],
                                             hsT[:, fc, tt * 128:(tt + 1) * 128],
                                             wsds[:, fc, n * 512:(n + 1) * 512],
                                             start=(fc == 0), stop=(fc == MS - 1))
                        return r
                    w = [ev_sh_h[(th, MS - 1)], ev_wsd]
                    if ev_shev is not None:
                        w.append(ev_shev)
                    if th == 0 and tt == 0:
                        w.append(ev_last_y)
                    ev_dn = tl.add("tensor", mk_dn, waits=w, inc=st_)
                    w = [ev_dn]
                    if ev_spark_last is not None:
                        w.append(ev_spark_last)
                    ev_shev = tl.add("vector", lambda e: e.tensor_copy(
                        shout[:], P3[:]), waits=w, inc=sv, nofence=True)
                    ev_spark_last = tl.add("scalar", lambda e, th=th, tt=tt: e.dma_start(
                        out=spark.rearrange("(t p) d -> t p d", p=128)[th * 8 + tt],
                        in_=shout[:]), waits=[ev_shev], inc=s_sp, incn=16)
                ev_dn_half[th] = ev_dn
            ev_shev_last = ev_shev
            ev_sh_h_last = ev_sh_h[(1, MS - 1)]

        esS.close()
        _hidT_guard.__exit__(None, None, None)
        esT = contextlib.ExitStack()
        sco4 = esT.enter_context(nc.sbuf_tensor("sco4", [128, 64, E], F32))
        lot = esT.enter_context(nc.sbuf_tensor("lot", [1, E], F32))
        hit = esT.enter_context(nc.sbuf_tensor("hit", [1, E], F32))
        midt = esT.enter_context(nc.sbuf_tensor("midt", [1, E], F32))
        cntt = esT.enter_context(nc.sbuf_tensor("cntt", [1, E], F32))
        needt = esT.enter_context(nc.sbuf_tensor("needt", [1, E], F32))
        geb = esT.enter_context(nc.sbuf_tensor("geb", [1, 2 * E], I32))
        midv = esT.enter_context(nc.sbuf_tensor("midv", [128, E], F32))
        maskb = esT.enter_context(nc.sbuf_tensor("maskb", [128, 64 * E], F32))
        pc = esT.enter_context(nc.sbuf_tensor("pc", [128, E], F32))
        hiv = esT.enter_context(nc.sbuf_tensor("hiv", [128, E], F32))
        lov = esT.enter_context(nc.sbuf_tensor("lov", [128, E], F32))
        needv = esT.enter_context(nc.sbuf_tensor("needv", [128, E], F32))
        m_hi = esT.enter_context(nc.sbuf_tensor("m_hi", [128, 64], F32))
        m_bd = esT.enter_context(nc.sbuf_tensor("m_bd", [128, 64], F32))
        cum = esT.enter_context(nc.sbuf_tensor("cum", [128, 64], F32))
        rnk = esT.enter_context(nc.sbuf_tensor("rnk", [128, 64], F32))
        tk = esT.enter_context(nc.sbuf_tensor("tk", [128, 64], F32))
        selb = esT.enter_context(nc.sbuf_tensor("selb", [128, 64], F32))
        selbi = esT.enter_context(nc.sbuf_tensor("selbi", [128, 64], I32))
        dst_all = esT.enter_context(nc.sbuf_tensor("dst_all", [128, 64, E], F32))
        rowt = esT.enter_context(nc.sbuf_tensor("rowt", [128, 1], F32))
        offs = esT.enter_context(nc.sbuf_tensor("offs", [128, 1], F32))
        odst = esT.enter_context(nc.sbuf_tensor("odst", [128, 64], F32))
        if True:
            shared_dead = [ev_spark_last, ev_shev_last]
            ev_sco4 = tl.add("sync", dma(
                sco4[:], sco_full.rearrange("(p j) e -> p j e", p=128)),
                waits=[ev_agsco] + shared_dead, inc=s_misc, incn=16)
            tl.add("vector", lambda e: e.memset(lot[:], 0.0), waits=shared_dead)
            tl.add("vector", lambda e: e.memset(hit[:], 1.0))
            ev_upd = tl.add("vector", lambda e: e.memset(midt[:], 0.5), inc=sv)

            for it in range(BISECT_ITERS + 1):
                last = it == BISECT_ITERS
                src = hit if last else midt
                ev_midB = tl.add("tensor", lambda e, src=src: e.matmul(
                    P4[:, 16:16 + E], ones1b[:], src[:], start=True, stop=True),
                    waits=[ev_upd, ev_const, ev_sco[NTR - 1]], inc=st_)
                ev_mve = tl.add("vector", lambda e: e.tensor_copy(
                    midv[:], P4[:, 16:16 + E]), waits=[ev_midB], inc=sv)
                ev_mask = tl.add("vector", lambda e: e.tensor_tensor(
                    maskb[:].rearrange("p (j e) -> p j e", e=E), sco4[:],
                    midv[:].rearrange("p (a e) -> p a e", a=1).to_broadcast(
                        [128, 64, E]), ALU.is_ge), waits=[ev_mve, ev_sco4], inc=sv)
                ev_pc = tl.add("vector", lambda e: e.tensor_reduce(
                    pc[:], maskb[:].rearrange("p (j e) -> p e j", e=E), AX.X, ALU.add),
                    inc=sv)
                ev_cmm = tl.add("tensor", lambda e: e.matmul(
                    P4[0:1, 0:E], onescb[:], pc[:], start=True, stop=True),
                    waits=[ev_pc, ev_mve], inc=st_)
                ev_cnt = tl.add("vector", lambda e: e.tensor_copy(
                    cntt[:], P4[0:1, 0:E]), waits=[ev_cmm], inc=sv)
                if last:
                    ev_upd = tl.add("vector", lambda e: e.tensor_scalar(
                        needt[:], cntt[:], -1.0, float(CAP), ALU.mult, ALU.add), inc=sv)
                else:
                    tl.add("vector", lambda e: e.tensor_scalar(
                        geb[:, 0:E], cntt[:], float(CAP), None, ALU.is_ge))
                    tl.add("vector", lambda e: e.tensor_scalar(
                        geb[:, E:2 * E], cntt[:], float(CAP), None, ALU.is_lt))
                    tl.add("vector", lambda e: e.copy_predicated(
                        lot[:], geb[:, 0:E], midt[:]))
                    tl.add("vector", lambda e: e.copy_predicated(
                        hit[:], geb[:, E:2 * E], midt[:]))
                    tl.add("vector", lambda e: e.tensor_add(midt[:], lot[:], hit[:]))
                    ev_upd = tl.add("vector", lambda e: e.tensor_scalar(
                        midt[:], midt[:], 0.5, None, ALU.mult), inc=sv)

            ev_bcp = ev_upd
            for srct, dstv in ((lot, lov), (hit, hiv), (needt, needv)):
                evm = tl.add("tensor", lambda e, srct=srct: e.matmul(
                    P4[:, 16:16 + E], ones1b[:], srct[:], start=True, stop=True),
                    waits=[ev_bcp], inc=st_)
                ev_bcp = tl.add("vector", lambda e, dstv=dstv: e.tensor_copy(
                    dstv[:], P4[:, 16:16 + E]), waits=[evm], inc=sv)
            ev_bcast = ev_bcp

            ev_selp = ev_bcast
            for e_ in range(E):
                tl.add("vector", lambda e, e_=e_: e.tensor_tensor(
                    m_hi[:], sco4[:, :, e_], hiv[:, e_:e_ + 1].to_broadcast([128, 64]),
                    ALU.is_ge), waits=[ev_selp])
                tl.add("vector", lambda e, e_=e_: e.tensor_tensor(
                    m_bd[:], sco4[:, :, e_], lov[:, e_:e_ + 1].to_broadcast([128, 64]),
                    ALU.is_ge))
                tl.add("vector", lambda e: e.tensor_sub(m_bd[:], m_bd[:], m_hi[:]))
                tl.add("vector", lambda e: e.tensor_tensor_scan(
                    cum[:], m_bd[:], zzz[:, 0:64], 0.0, ALU.add, ALU.add),
                    waits=[ev_zzz])
                ev_rt = tl.add("vector", lambda e: e.tensor_copy(
                    rowt[:], cum[:, 63:64]), inc=sv)
                ev_om = tl.add("tensor", lambda e: e.matmul(
                    P4[:, 48:49], ltrib[:], rowt[:], start=True, stop=True),
                    waits=[ev_rt], inc=st_)
                ev_of = tl.add("vector", lambda e: e.tensor_copy(
                    offs[:], P4[:, 48:49]), waits=[ev_om], inc=sv)
                tl.add("vector", lambda e: e.scalar_tensor_tensor(
                    rnk[:], cum[:], offs[:], m_bd[:], ALU.add, ALU.subtract),
                    waits=[ev_of])
                tl.add("vector", lambda e, e_=e_: e.tensor_tensor(
                    tk[:], rnk[:], needv[:, e_:e_ + 1].to_broadcast([128, 64]),
                    ALU.is_lt))
                tl.add("vector", lambda e: e.tensor_mul(tk[:], tk[:], m_bd[:]))
                tl.add("vector", lambda e: e.tensor_add(selb[:], m_hi[:], tk[:]))
                tl.add("vector", lambda e: e.tensor_tensor_scan(
                    cum[:], selb[:], zzz[:, 0:64], 0.0, ALU.add, ALU.add))
                ev_rt2 = tl.add("vector", lambda e: e.tensor_copy(
                    rowt[:], cum[:, 63:64]), inc=sv)
                ev_om2 = tl.add("tensor", lambda e: e.matmul(
                    P4[:, 48:49], ltrib[:], rowt[:], start=True, stop=True),
                    waits=[ev_rt2], inc=st_)
                ev_of2 = tl.add("vector", lambda e: e.tensor_copy(
                    offs[:], P4[:, 48:49]), waits=[ev_om2], inc=sv)
                tl.add("vector", lambda e: e.scalar_tensor_tensor(
                    rnk[:], cum[:], offs[:], selb[:], ALU.add, ALU.subtract),
                    waits=[ev_of2])
                tl.add("vector", lambda e: e.tensor_copy(selbi[:], selb[:]))
                ev_selp = tl.add("vector", lambda e, e_=e_: e.select(
                    dst_all[:, :, e_], selbi[:], rnk[:], bigt[:]),
                    waits=[ev_big], inc=sv)

            ev_idxsc = {}
            ev_oi = {}
            for ee in range(4):
                w = [ev_selp]
                if ee > 0:
                    w.append(ev_idxsc[ee - 1])
                tl.add("vector", lambda e, ee=ee: e.tensor_tensor(
                    maskb[:].rearrange("p (j e) -> p j e", e=E), dst_all[:],
                    oh4b[:, ee, :].rearrange("p (a e) -> p a e", a=1).to_broadcast(
                        [128, 64, E]), ALU.mult), waits=w)
                tl.add("vector", lambda e: e.tensor_reduce(
                    odst[:], maskb[:].rearrange("p (j e) -> p j e", e=E), AX.X,
                    ALU.add))
                ev_oi[ee] = tl.add("vector", lambda e: e.tensor_copy(
                    odsti[:], odst[:]), inc=sv)
                evp = None
                for j in range(64):
                    evp = tl.add("gpsimd", lambda e, ee=ee, j=j: e.indirect_dma_start(
                        out=idxl_d[ee],
                        out_offset=IndirectOffsetOnAxis(odsti[:, j:j + 1], 0),
                        in_=tokidb[:, j:j + 1], in_offset=None,
                        bounds_check=regs['cap'], oob_is_err=False),
                        waits=[ev_oi[ee], ev_const] if j == 0 else [],
                        inc=s_isc, incn=16)
                ev_idxsc[ee] = evp

            if debug_taps:
                for ee in range(4):
                    tl.add("gpsimd", dma(dbg_idx[ee].rearrange("(c a) -> c a", a=1), idxl_d[ee]), waits=[ev_idxsc[3]],
                           inc=s_dbg, incn=16)

            esE = contextlib.ExitStack()
            xtokT = esE.enter_context(nc.sbuf_tensor("xtokT", [128, KC, CAP], BF16))
            hTx = esE.enter_context(nc.sbuf_tensor("hTx", [128, MF, CAP], BF16))
            wgs = esE.enter_context(nc.sbuf_tensor("wgs", [128, KC, 2, 1024], BF16))
            wus = esE.enter_context(nc.sbuf_tensor("wus", [128, KC, 2, 1024], BF16))
            wds = esE.enter_context(nc.sbuf_tensor("wds", [128, MF, D], BF16))
            gtmp2 = esE.enter_context(nc.sbuf_tensor("gtmp2", [128, 1024], BF16))
            if True:
                ev_wg_h = {}
                ev_wu_h = {}
                ev_wd_e = {}
                ev_lastmm_gu = {}
                ev_lastmm_d = {}
                ev_scat = {}
                ev_eo_evac = {}
                prev_scat_end = None
                ev_xg_last = None
                ev_ii = {}

                for ee in range(4):
                    w = [ev_idxsc[3]]
                    if ee > 0:
                        w.append(ev_it_prev)
                    ev_i8 = tl.add("sync", dma(
                        idx8[:], idxl_d[ee].rearrange("(a b) c -> a (b c)", a=8)),
                        waits=w, inc=s_misc, incn=16)
                    w = [ev_i8]
                    if ee > 0:
                        w.append(ev_xg_last)
                    ev_it = tl.add("tensor", lambda e: e.transpose(
                        P4[:, 64:72], idx8[:], identb32[0:8, 0:8]), waits=w, inc=st_)
                    ev_it_prev = ev_it
                    w = [ev_it]
                    if ee > 0:
                        w.append(ev_scat[(ee - 1, 7)])
                        w.append(ev_scat[(ee - 1, 6)])
                    ev_ii[ee] = tl.add("vector", lambda e: e.tensor_copy(
                        idxi[:], P4[:, 64:72]), waits=w, inc=sv)

                    ev_xg = None
                    ev_tw = None
                    for s in range(8):
                        w = [ev_ii[ee]]
                        if ev_tw is not None:
                            w.append(ev_tw)
                        ev_g1 = tl.add("gpsimd", lambda e, s=s: e.indirect_dma_start(
                            out=tw16[:], out_offset=None,
                            in_=sco_full,
                            in_offset=IndirectOffsetOnAxis(idxi[:, s:s + 1], 0),
                            bounds_check=None, oob_is_err=False),
                            waits=w, inc=s_tw, incn=16)
                        tl.add("vector", lambda e, ee=ee: e.tensor_mul(
                            tw16[:], tw16[:], oh4b[:, ee, :]), waits=[ev_g1],
                            nofence=True)
                        ev_tw = tl.add("vector", lambda e, s=s: e.tensor_reduce(
                            topw[:, s:s + 1], tw16[:], AX.X, ALU.add), inc=sv)
                        ev_xg = tl.add("gpsimd", lambda e, s=s: e.indirect_dma_start(
                            out=xtok[:, s, :], out_offset=None,
                            in_=hid_full,
                            in_offset=IndirectOffsetOnAxis(idxi[:, s:s + 1], 0),
                            bounds_check=None, oob_is_err=False),
                            waits=[ev_ii[ee], ev_agsco], inc=s_xg, incn=16)
                    ev_xg_last = ev_xg

                    ev_xT = None
                    ev_xTe = None
                    for s in range(8):
                        def mk_xT(e, s=s):
                            r = None
                            for g in range(KC):
                                r = e.transpose(PB[:, g * 128:(g + 1) * 128],
                                                xtok[:, s, g * 128:(g + 1) * 128],
                                                identbb[:])
                            return r
                        if ev_xTe is None:
                            w = [ev_xg, ev_last_ehT]
                            if ee > 0:
                                w.append(ev_prev_xTe)
                        else:
                            w = [ev_xTe]
                        ev_xT = tl.add("tensor", mk_xT, waits=w, inc=st_)
                        ev_xTe = tl.add("vector", lambda e, s=s: e.tensor_copy(
                            xtokT[:, :, s * 128:(s + 1) * 128],
                            PB[:].rearrange("p (k n) -> p k n", k=KC)),
                            waits=[ev_xT], inc=sv, nofence=True)
                    ev_xtokT = ev_xTe
                    ev_prev_xTe = ev_xTe

                    for hh in range(2):
                        w = [ev_wcast[f"wg{ee}"]]
                        if ee > 0:
                            w.append(ev_lastmm_gu.get((ee - 1, hh)))
                        else:
                            w.extend(shared_dead)
                        ev_wg_h[(ee, hh)] = tl.add("sync", dma(
                            wgs[:, :, hh, :],
                            wg_b[ee][:, :, hh * 1024:(hh + 1) * 1024]),
                            waits=w, inc=s_wg[hh], incn=16)
                        w = [ev_wcast[f"wu{ee}"]]
                        if ee > 0:
                            w.append(ev_lastmm_gu.get((ee - 1, hh)))
                        else:
                            w.extend(shared_dead)
                        ev_wu_h[(ee, hh)] = tl.add("sync", dma(
                            wus[:, :, hh, :],
                            wu_b[ee][:, :, hh * 1024:(hh + 1) * 1024]),
                            waits=w, inc=s_wu[hh], incn=16)
                    w = [ev_wcast[f"wd{ee}"]]
                    if ee > 0:
                        w.append(ev_lastmm_d.get(ee - 1))
                    else:
                        w.extend(shared_dead)
                    ev_wd_e[ee] = tl.add("sync", dma(wds[:], wd_b[ee]),
                                         waits=w, inc=s_wd, incn=16)

                    ev_h = {}
                    ev_gel2 = None
                    for m in range(MF):
                        hh = m // 8

                        def mk_gx(e, m=m, hh=hh):
                            r = None
                            for n in range(2):
                                for k in range(KC):
                                    r = e.matmul(
                                        P1[:, n * 512:(n + 1) * 512],
                                        wgs[:, k, hh, (m % 8) * 128:(m % 8 + 1) * 128],
                                        xtokT[:, k, n * 512:(n + 1) * 512],
                                        start=(k == 0), stop=(k == KC - 1))
                            return r
                        w = [ev_wg_h[(ee, hh)], ev_xtokT]
                        if m > 0:
                            w.append(ev_h[m - 1])
                        elif ee > 0:
                            w.append(ev_eo_evac[(ee - 1, 7)])
                        else:
                            w.append(ev_sh_h_last)
                        ev_gmx = tl.add("tensor", mk_gx, waits=w, inc=st_)

                        def mk_ux(e, m=m, hh=hh):
                            r = None
                            for n in range(2):
                                for k in range(KC):
                                    r = e.matmul(
                                        P2[:, n * 512:(n + 1) * 512],
                                        wus[:, k, hh, (m % 8) * 128:(m % 8 + 1) * 128],
                                        xtokT[:, k, n * 512:(n + 1) * 512],
                                        start=(k == 0), stop=(k == KC - 1))
                            return r
                        ev_umx = tl.add("tensor", mk_ux,
                                        waits=[ev_wu_h[(ee, hh)]] + w, inc=st_)
                        ev_lastmm_gu[(ee, hh)] = ev_umx
                        w2 = [ev_gmx]
                        if ev_gel2 is not None and m > 0:
                            w2.append(ev_h[m - 1])
                        ev_gel2 = tl.add("scalar", lambda e: e.activation(
                            gtmp2[:], P1[:], AF.Gelu), waits=w2, inc=ss)
                        ev_h[m] = tl.add("vector", lambda e, m=m: e.tensor_mul(
                            hTx[:, m, :], gtmp2[:], P2[:]),
                            waits=[ev_gel2, ev_umx], inc=sv, nofence=True)

                    ev_dn = None
                    for s in range(8):
                        def mk_dnx(e, s=s):
                            r = None
                            for n in range(2):
                                for fc in range(MF):
                                    r = e.matmul(P3[:, n * 512:(n + 1) * 512],
                                                 hTx[:, fc, s * 128:(s + 1) * 128],
                                                 wds[:, fc, n * 512:(n + 1) * 512],
                                                 start=(fc == 0), stop=(fc == MF - 1))
                            return r
                        w = [ev_h[MF - 1], ev_wd_e[ee]]
                        if ev_dn is not None:
                            w.append(ev_eo_evac[(ee, s - 1)])
                        elif ee == 0:
                            w.append(ev_shev_last)
                        else:
                            w.append(ev_eo_evac[(ee - 1, 7)])
                        ev_dn = tl.add("tensor", mk_dnx, waits=w, inc=st_)
                        ev_lastmm_d[ee] = ev_dn
                        w = [ev_dn, ev_tw]
                        if s >= 2:
                            w.append(ev_scat[(ee, s - 2)])
                        elif ee > 0:
                            w.append(ev_scat[(ee - 1, s + 6)])
                        ev_eo_evac[(ee, s)] = tl.add(
                            "vector", lambda e, s=s: e.tensor_scalar(
                                eo[:, s % 2, :], P3[:], topw[:, s:s + 1], None,
                                ALU.mult), waits=w, inc=sv, nofence=True)
                        w = [ev_eo_evac[(ee, s)], ev_dense0]
                        if s == 0 and prev_scat_end is not None:
                            w.append(prev_scat_end)
                        ev_scat[(ee, s)] = tl.add(
                            "gpsimd", lambda e, ee=ee, s=s: e.indirect_dma_start(
                                out=dense,
                                out_offset=IndirectOffsetOnAxis(idxi[:, s:s + 1], 0),
                                in_=eo[:, s % 2, :], in_offset=None,
                                bounds_check=None, oob_is_err=False,
                                compute_op=ALU.add),
                            waits=w, inc=s_eo[s % 2], incn=16)
                    prev_scat_end = ev_scat[(ee, 7)]

                ev_rs = tl.add("gpsimd", lambda e: e.collective_compute(
                    "ReduceScatter", ALU.add, replica_groups=groups,
                    ins=[dense.opt()], outs=[rs_out.opt()]),
                    waits=[prev_scat_end, ev_scat[(3, 6)]], inc=scc)

                if debug_taps:
                    tl.add("gpsimd", dma(dbg_dense, dense),
                           waits=[prev_scat_end, ev_scat[(3, 6)]],
                           inc=s_dbg, incn=16)

        esE.close()
        esT.close()
        esC = contextlib.ExitStack()
        rsb = esC.enter_context(nc.sbuf_tensor("rsb", [128, 2, D], F32))
        spb = esC.enter_context(nc.sbuf_tensor("spb", [128, 2, D], F32))
        xpb = esC.enter_context(nc.sbuf_tensor("xpb", [128, 2, D], F32))
        fo = esC.enter_context(nc.sbuf_tensor("fo", [128, D], F32))
        ln4 = esC.enter_context(nc.sbuf_tensor("ln4", [128, D], F32))
        ob2 = esC.enter_context(nc.sbuf_tensor("ob2", [128, D], F32))
        cst = esC.enter_context(nc.sbuf_tensor("cst", [128, 8], F32))
        cjk = esC.enter_context(nc.sbuf_tensor("cjk", [128, D], F32))
        if True:
            rsv = rs_out.rearrange("(t p) d -> t p d", p=128)
            spv = spark.rearrange("(t p) d -> t p d", p=128)
            xpv = xpark.rearrange("(t p) d -> t p d", p=128)
            ouv = out.rearrange("(t p) d -> t p d", p=128)
            ev_done = {}
            ev_odma = None
            ev_n4_prev = None
            for j in range(NTR):
                w = [ev_rs] if j < 2 else [ev_done[j - 2]]
                ev_l1 = tl.add("sync", dma(rsb[:, j % 2, :], rsv[j]),
                               waits=w, inc=s_crs[j % 2], incn=16)
                ev_l2 = tl.add("sync", dma(spb[:, j % 2, :], spv[j]),
                               waits=[ev_spark_last] + w, inc=s_csp[j % 2], incn=16)
                ev_l3 = tl.add("sync", dma(xpb[:, j % 2, :], xpv[j]),
                               waits=[ev_xmdma[NT - 1], ev_xmdma[NT - 2]] + w,
                               inc=s_cxp[j % 2], incn=16)
                w = [ev_l1, ev_l2]
                if ev_n4_prev is not None:
                    w.append(ev_n4_prev)
                tl.add("vector", lambda e, j=j: e.tensor_add(
                    fo[:], rsb[:, j % 2, :], spb[:, j % 2, :]), waits=w)
                tl.add("vector", lambda e: e.reduce_sum(cst[:, 0:1], fo[:], AX.X),
                       nofence=True)
                tl.add("vector", lambda e: e.scalar_tensor_tensor(
                    cjk[:], fo[:], 1.0, fo[:], ALU.mult, ALU.mult,
                    accum_out=cst[:, 1:2]))
                tl.add("vector", lambda e: e.tensor_scalar(
                    cst[:, 2:3], cst[:, 0:1], -1.0 / D, None, ALU.mult))
                tl.add("vector", lambda e: e.tensor_mul(
                    cjk[:, 0:1], cst[:, 2:3], cst[:, 2:3]))
                tl.add("vector", lambda e: e.tensor_scalar(
                    cjk[:, 1:2], cst[:, 1:2], 1.0 / D, 1e-5, ALU.mult, ALU.add))
                ev_v4 = tl.add("vector", lambda e: e.tensor_sub(
                    cst[:, 3:4], cjk[:, 1:2], cjk[:, 0:1]), inc=sv)
                ev_s4 = tl.add("scalar", lambda e: e.activation(
                    cst[:, 4:5], cst[:, 3:4], AF.Sqrt), waits=[ev_v4], inc=ss)
                ev_r4 = tl.add("vector", lambda e: e.reciprocal(
                    cst[:, 5:6], cst[:, 4:5]), waits=[ev_s4], inc=sv)
                ev_n4 = tl.add("vector", lambda e: e.tensor_scalar(
                    ln4[:], fo[:], cst[:, 2:3], cst[:, 5:6], ALU.add, ALU.mult),
                    waits=[ev_r4], inc=sv)
                ev_n4_prev = ev_n4
                w = [ev_n4, ev_l3]
                if ev_odma is not None:
                    w.append(ev_odma)
                ev_ob = tl.add("vector", lambda e, j=j: e.tensor_add(
                    ob2[:], ln4[:], xpb[:, j % 2, :]), waits=w, inc=sv,
                    nofence=True)
                ev_odma = tl.add("gpsimd", lambda e, j=j: e.dma_start(
                    out=ouv[j], in_=ob2[:]), waits=[ev_ob], inc=s_out, incn=16)
                ev_done[j] = ev_odma

            tl.add("sync", lambda e: None, waits=[ev_odma])


    if not attn_only:
        esC.close()
    with nc.Block() as block:
        @block.sync
        def _(e):
            tl.emit(e, "sync")

        @block.tensor
        def _(e):
            tl.emit(e, "tensor")

        @block.vector
        def _(e):
            tl.emit(e, "vector")

        @block.scalar
        def _(e):
            tl.emit(e, "scalar")

        @block.gpsimd
        def _(e):
            tl.emit(e, "gpsimd")

    es.close()
    nc.finalize()
    return nc


def _chunked(wT):
    Dd, Ff = wT.shape
    return np.ascontiguousarray(wT.reshape(Dd // 128, 128, Ff).transpose(1, 0, 2))


def make_inputs(inputs):
    x = np.asarray(inputs["L0_attn_x"], np.float32)
    wf = np.asarray(inputs["L0_attn_w_forget"], np.float32)
    wi = np.asarray(inputs["L0_attn_w_input"], np.float32)
    wo = np.asarray(inputs["L0_attn_w_output"], np.float32)
    nw = np.asarray(inputs["L0_attn_norm_w"], np.float32)
    gw = np.asarray(inputs["L0_ffn_gate_w"], np.float32)
    Wg = np.asarray(inputs["L0_ffn_Wg"], np.float32)
    Wu = np.asarray(inputs["L0_ffn_Wu"], np.float32)
    Wd = np.asarray(inputs["L0_ffn_Wd"], np.float32)
    Wsg = np.asarray(inputs["L0_ffn_Ws_g"], np.float32)
    Wsu = np.asarray(inputs["L0_ffn_Ws_u"], np.float32)
    Wsd = np.asarray(inputs["L0_ffn_Ws_d"], np.float32)

    wfT = _chunked(np.ascontiguousarray(wf.T))
    wiT = _chunked(np.ascontiguousarray(wi.T))
    woT = _chunked(np.ascontiguousarray(wo.T))
    gateT = _chunked(np.ascontiguousarray(gw.T))
    wsgT = _chunked(np.ascontiguousarray(Wsg.T))
    wsuT = _chunked(np.ascontiguousarray(Wsu.T))
    wsdT = _chunked(np.ascontiguousarray(Wsd.T))

    identm = np.eye(128, dtype=np.float32)
    k = np.arange(128)
    ltrim = (k[:, None] < k[None, :]).astype(np.float32)
    onesc = np.ones((128, 1), np.float32)
    ones1 = np.ones((1, 128), np.float32)
    tokidm = (np.arange(128)[:, None] * 64 + np.arange(64)[None, :]).astype(np.float32)

    in_maps = []
    for c in range(NCORE):
        b, q = c // 4, c % 4
        lo = q * TLOC - WARM
        if lo < 0:
            xw = np.concatenate([np.zeros((WARM, D), np.float32),
                                 x[b, 0:TLOC]], axis=0)
        else:
            xw = x[b, lo:(q + 1) * TLOC]
        ex = [4 * q + t for t in range(4)]
        wgTs = np.stack([_chunked(np.ascontiguousarray(Wg[e].T)) for e in ex])
        wuTs = np.stack([_chunked(np.ascontiguousarray(Wu[e].T)) for e in ex])
        wdTs = np.stack([_chunked(np.ascontiguousarray(Wd[e].T)) for e in ex])
        oh = np.zeros((128, 4, E), np.float32)
        for t in range(4):
            oh[:, t, 4 * q + t] = 1.0
        in_maps.append({
            "xin": np.ascontiguousarray(xw),
            "wfT": wfT, "wiT": wiT, "woT": woT,
            "normw": nw.reshape(1, D),
            "gateT": gateT, "wsgT": wsgT, "wsuT": wsuT, "wsdT": wsdT,
            "wgT": wgTs, "wuT": wuTs, "wdT": wdTs,
            "ident": identm, "ltri": ltrim, "onesc": onesc, "ones1": ones1,
            "tokid": tokidm, "oh4": oh,
        })
    return in_maps


def kernel(**inputs):
    nc = build_program()
    in_maps = make_inputs(inputs)
    res = run_bass_kernel_spmd(nc, in_maps, core_ids=list(range(NCORE)))
    parts = [res.results[c]["out"] for c in range(NCORE)]
    full = np.stack([np.concatenate(parts[0:4], axis=0),
                     np.concatenate(parts[4:8], axis=0)])
    return full.astype(np.float32)


# version 07
# speedup vs baseline: 1.1059x; 1.0310x over previous
import numpy as np
import concourse.bass as bass
import concourse.mybir as mybir
from concourse.bass_utils import run_bass_kernel_spmd
from concourse.bass_core_rust import IndirectOffsetOnAxis

F32 = mybir.dt.float32
BF16 = mybir.dt.bfloat16
I32 = mybir.dt.int32
AF = mybir.ActivationFunctionType
ALU = mybir.AluOpType
AX = mybir.AxisListType

B, T, D = 2, 8192, 1024
H, DH = 8, 128
E, CAP, FE, FS = 16, 1024, 2048, 2048
NCORE = 8
TLOC = 2048
WARM = 512
NTW = WARM // 128
NTR = TLOC // 128
NT = NTW + NTR
KC = D // 128
MF = FE // 128
MS = FS // 128
LN9 = 2.1972245773362196
BIG = 1.0e9
BISECT_ITERS = 42

ENGINES = ("sync", "tensor", "vector", "scalar", "gpsimd")


class TL:
    def __init__(self):
        self.ops = {e: [] for e in ENGINES}
        self.cnt = {}
        self.autofence = {}
        self.last = {}

    def add(self, eng, fn, waits=(), inc=None, incn=1, nofence=False):
        ws = [w for w in waits if w is not None]
        fence_sem = self.autofence.get(eng) if not nofence else None
        if fence_sem is not None:
            last = self.last.get(eng)
            if last is not None:
                ws.append(last)
            if inc is None:
                inc = fence_sem
        m = None
        if inc is not None:
            c = self.cnt.get(id(inc), 0) + incn
            self.cnt[id(inc)] = c
            m = (inc, c)
        if fence_sem is not None:
            self.last[eng] = m
        self.ops[eng].append((fn, ws, inc, incn))
        return m

    def cur(self, sem):
        return (sem, self.cnt.get(id(sem), 0))

    def emit(self, eng_obj, name):
        for fn, waits, inc, incn in self.ops[name]:
            best, order = {}, []
            for (sem, c) in waits:
                if id(sem) not in best:
                    order.append(sem)
                best[id(sem)] = max(best.get(id(sem), -1), c)
            for sem in order:
                eng_obj.wait_ge(sem, best[id(sem)])
            inst = fn(eng_obj)
            if inc is not None:
                if inst is None:
                    inst = eng_obj.nop()
                inst.then_inc(inc, incn)


def build_program(debug_taps=False, attn_only=False, race_detect=True):
    nc = bass.Bass("TRN2", target_bir_lowering=False,
                   detect_race_conditions=race_detect)

    xin = nc.dram_tensor("xin", [NT * 128, D], F32, kind="ExternalInput").ap()
    wfT = nc.dram_tensor("wfT", [128, KC, D], F32, kind="ExternalInput").ap()
    wiT = nc.dram_tensor("wiT", [128, KC, D], F32, kind="ExternalInput").ap()
    woT = nc.dram_tensor("woT", [128, KC, D], F32, kind="ExternalInput").ap()
    normw = nc.dram_tensor("normw", [1, D], F32, kind="ExternalInput").ap()
    gateT = nc.dram_tensor("gateT", [128, KC, E], F32, kind="ExternalInput").ap()
    wsgT = nc.dram_tensor("wsgT", [128, KC, FS], F32, kind="ExternalInput").ap()
    wsuT = nc.dram_tensor("wsuT", [128, KC, FS], F32, kind="ExternalInput").ap()
    wsdT = nc.dram_tensor("wsdT", [128, MS, D], F32, kind="ExternalInput").ap()
    wgT = nc.dram_tensor("wgT", [4, 128, KC, FE], F32, kind="ExternalInput").ap()
    wuT = nc.dram_tensor("wuT", [4, 128, KC, FE], F32, kind="ExternalInput").ap()
    wdT = nc.dram_tensor("wdT", [4, 128, MF, D], F32, kind="ExternalInput").ap()
    ident = nc.dram_tensor("ident", [128, 128], F32, kind="ExternalInput").ap()
    ltri = nc.dram_tensor("ltri", [128, 128], F32, kind="ExternalInput").ap()
    onesc_i = nc.dram_tensor("onesc", [128, 1], F32, kind="ExternalInput").ap()
    ones1_i = nc.dram_tensor("ones1", [1, 128], F32, kind="ExternalInput").ap()
    tokid = nc.dram_tensor("tokid", [128, 64], F32, kind="ExternalInput").ap()
    oh4 = nc.dram_tensor("oh4", [128, 4, E], F32, kind="ExternalInput").ap()
    out = nc.dram_tensor("out", [TLOC, D], F32, kind="ExternalOutput").ap()
    if debug_taps:
        dbg_xmid = nc.dram_tensor("dbg_xmid", [TLOC, D], F32, kind="ExternalOutput").ap()
        dbg_hid = nc.dram_tensor("dbg_hid", [TLOC, D], F32, kind="ExternalOutput").ap()
        dbg_sco = nc.dram_tensor("dbg_sco", [T, E], F32, kind="ExternalOutput").ap()
        dbg_idx = nc.dram_tensor("dbg_idx", [4, CAP], F32, kind="ExternalOutput").ap()
        dbg_dense = nc.dram_tensor("dbg_dense", [T, D], F32, kind="ExternalOutput").ap()
        dbg2 = nc.dram_tensor("dbg2", [12, 128, D], F32, kind="ExternalOutput").ap()
        dbg3 = nc.dram_tensor("dbg3", [6, 128, D], F32, kind="ExternalOutput").ap()

    wsg_b = nc.dram_tensor("wsg_b", [128, KC, FS], BF16).ap()
    wsu_b = nc.dram_tensor("wsu_b", [128, KC, FS], BF16).ap()
    wsd_b = nc.dram_tensor("wsd_b", [128, MS, D], BF16).ap()
    wg_b = nc.dram_tensor("wg_b", [4, 128, KC, FE], BF16).ap()
    wu_b = nc.dram_tensor("wu_b", [4, 128, KC, FE], BF16).ap()
    wd_b = nc.dram_tensor("wd_b", [4, 128, MF, D], BF16).ap()
    hid_loc = nc.dram_tensor("hid_loc", [TLOC, D], BF16).ap()
    hid_full = nc.dram_tensor("hid_full", [T, D], BF16).ap()
    sco_loc = nc.dram_tensor("sco_loc", [TLOC, E], F32).ap()
    sco_full = nc.dram_tensor("sco_full", [T, E], F32).ap()
    dense = nc.dram_tensor("dense", [T, D], BF16).ap()
    rs_out = nc.dram_tensor("rs_out", [TLOC, D], BF16).ap()
    xpark = nc.dram_tensor("xpark", [TLOC, D], F32).ap()
    spark = nc.dram_tensor("spark", [TLOC, D], F32).ap()
    idxl_d = [nc.dram_tensor(f"idxl_d{ee}", [CAP, 1], F32).ap()
              for ee in range(4)]

    xv = xin.rearrange("(t p) d -> t p d", p=128)
    tl = TL()
    groups = [[0, 1, 2, 3], [4, 5, 6, 7]]

    import contextlib
    es = contextlib.ExitStack()

    def mksem(name):
        return es.enter_context(nc.semaphore(name))
    scc = mksem("scc")
    sv = mksem("sv")
    ss = mksem("ss")
    st_ = mksem("st")
    s_const = mksem("s_const")
    s_dense = mksem("s_dense")
    s_x = [mksem(f"s_x{k}") for k in range(3)]
    s_wsh = mksem("s_wsh")
    s_misc = mksem("s_misc")
    s_wg = [mksem(f"s_wg{k}") for k in range(2)]
    s_wu = [mksem(f"s_wu{k}") for k in range(2)]
    s_wd = mksem("s_wd")
    s_crs = [mksem(f"s_crs{k}") for k in range(2)]
    s_csp = [mksem(f"s_csp{k}") for k in range(2)]
    s_cxp = [mksem(f"s_cxp{k}") for k in range(2)]
    s_xp = [mksem(f"s_xp{k}") for k in range(2)]
    s_hd = [mksem(f"s_hd{k}") for k in range(2)]
    s_sc = mksem("s_sc")
    s_sp = mksem("s_sp")
    s_identb = mksem("s_identb")
    s_attnw = mksem("s_attnw")
    s_cast = mksem("s_cast")
    s_isc = mksem("s_isc")
    s_tw = mksem("s_tw")
    s_xg = mksem("s_xg")
    s_eo = [mksem(f"s_eo{k}") for k in range(2)]
    s_out = mksem("s_out")
    s_dbg = mksem("s_dbg")
    tl.autofence = {"vector": sv}

    P1 = es.enter_context(nc.psum_tensor("P1", [128, 1024], F32))
    P2 = es.enter_context(nc.psum_tensor("P2", [128, 1024], F32))
    P3 = es.enter_context(nc.psum_tensor("P3", [128, 1024], F32))
    PB = es.enter_context(nc.psum_tensor("PB", [128, 1024], BF16))
    P4 = es.enter_context(nc.psum_tensor("P4", [128, 512], F32))

    identb32 = es.enter_context(nc.sbuf_tensor("identb32", [128, 128], F32))
    identbb = es.enter_context(nc.sbuf_tensor("identbb", [128, 128], BF16))
    ltrib = es.enter_context(nc.sbuf_tensor("ltrib", [128, 128], F32))
    onescb = es.enter_context(nc.sbuf_tensor("onescb", [128, 1], F32))
    ones1b = es.enter_context(nc.sbuf_tensor("ones1b", [1, 128], F32))
    tokidb = es.enter_context(nc.sbuf_tensor("tokidb", [128, 64], F32))
    oh4b = es.enter_context(nc.sbuf_tensor("oh4b", [128, 4, E], F32))
    zzz = es.enter_context(nc.sbuf_tensor("zzz", [128, 1024], F32))
    bigt = es.enter_context(nc.sbuf_tensor("bigt", [128, 64], F32))
    idx8 = es.enter_context(nc.sbuf_tensor("idx8", [8, 128], F32))
    idxi = es.enter_context(nc.sbuf_tensor("idxi", [128, 2, 8], I32))
    tw16 = es.enter_context(nc.sbuf_tensor("tw16", [128, E], F32))
    topw = es.enter_context(nc.sbuf_tensor("topw", [128, 2, 8], F32))
    odsti = es.enter_context(nc.sbuf_tensor("odsti", [128, 64], I32))
    xtok = es.enter_context(nc.sbuf_tensor("xtok", [128, 8, D], BF16))
    eo = es.enter_context(nc.sbuf_tensor("eo", [128, 2, D], BF16))
    zzzb = es.enter_context(nc.sbuf_tensor("zzzb", [128, 1024], BF16))
    gatb = es.enter_context(nc.sbuf_tensor("gatb", [128, KC, E], BF16))
    sco_all = es.enter_context(nc.sbuf_tensor("sco_all", [128, NTR, E], F32))
    gj = es.enter_context(nc.sbuf_tensor("gj", [128, 8], F32))
    _hidT_guard = nc.sbuf_tensor("hidT", [128, KC, TLOC], BF16)
    hidT = _hidT_guard.__enter__()

    def dma(dst, src):
        return lambda e: e.dma_start(out=dst, in_=src)

    regs = {}
    def mk_regs(e):
        regs['cap'] = e.to_reg(CAP - 1)
        return None
    tl.add('gpsimd', mk_regs)

    ev_const = None
    for dst, src in ((identb32[:], ident), (ltrib[:], ltri), (onescb[:], onesc_i),
                     (ones1b[:], ones1_i), (tokidb[:], tokid), (oh4b[:], oh4)):
        ev_const = tl.add("sync", dma(dst, src), inc=s_const, incn=16)
    ev_identb = tl.add("gpsimd", dma(identbb[:], ident), inc=s_identb, incn=16)
    ev_zzz = tl.add("vector", lambda e: e.memset(zzz[:], 0.0), inc=sv)
    ev_zzzb = tl.add("vector", lambda e: e.memset(zzzb[:], 0.0), inc=sv)
    ev_big = tl.add("vector", lambda e: e.memset(bigt[:], BIG), inc=sv)



    esA = contextlib.ExitStack()
    wfb = esA.enter_context(nc.sbuf_tensor("wfb", [128, KC, D], BF16))
    wib = esA.enter_context(nc.sbuf_tensor("wib", [128, KC, D], BF16))
    wob = esA.enter_context(nc.sbuf_tensor("wob", [128, KC, D], BF16))
    normwb = esA.enter_context(nc.sbuf_tensor("normwb", [128, D], F32))
    nw1 = esA.enter_context(nc.sbuf_tensor("nw1", [1, D], F32))
    xb = esA.enter_context(nc.sbuf_tensor("xb", [128, 3, D], F32))
    sta = esA.enter_context(nc.sbuf_tensor("sta", [128, 2, 16], F32))
    st2 = esA.enter_context(nc.sbuf_tensor("st2", [128, 2, 16], F32))
    junk = esA.enter_context(nc.sbuf_tensor("junk", [128, D], F32))
    ln1b = esA.enter_context(nc.sbuf_tensor("ln1b", [128, 2, D], F32))
    tpre = esA.enter_context(nc.sbuf_tensor("tpre", [128, D], F32))
    xnb = esA.enter_context(nc.sbuf_tensor("xnb", [128, D], BF16))
    xnT = esA.enter_context(nc.sbuf_tensor("xnT", [128, KC, 128], BF16))
    fb = esA.enter_context(nc.sbuf_tensor("fb", [128, D], F32))
    ob = esA.enter_context(nc.sbuf_tensor("ob", [128, D], F32))
    vs = esA.enter_context(nc.sbuf_tensor("vs", [128, H], F32))
    ub = esA.enter_context(nc.sbuf_tensor("ub", [128, D], F32))
    fTs = esA.enter_context(nc.sbuf_tensor("fTs", [128, D], F32))
    sTb = esA.enter_context(nc.sbuf_tensor("sTb", [128, 2, D], F32))
    yb = esA.enter_context(nc.sbuf_tensor("yb", [128, D], F32))
    retb = esA.enter_context(nc.sbuf_tensor("retb", [128, 2, D], F32))
    ln2b = esA.enter_context(nc.sbuf_tensor("ln2b", [128, D], F32))
    xmb = esA.enter_context(nc.sbuf_tensor("xmb", [128, 2, D], F32))
    hbb = esA.enter_context(nc.sbuf_tensor("hbb", [128, 2, D], BF16))
    if True:
        tl.add("gpsimd", dma(wfb[:], wfT), inc=s_attnw, incn=16)
        tl.add("gpsimd", dma(wib[:], wiT), inc=s_attnw, incn=16)
        tl.add("gpsimd", dma(wob[:], woT), inc=s_attnw, incn=16)
        ev_attnw = tl.add("gpsimd", dma(gatb[:], gateT), inc=s_attnw, incn=16)
        ev_wf = ev_wi = ev_wo = ev_gat = ev_attnw
        ev_wcast = {}
        for name_, dst_, src_ in (("wsg", wsg_b, wsgT), ("wsu", wsu_b, wsuT),
                                  ("wsd", wsd_b, wsdT)):
            ev_wcast[name_] = tl.add("gpsimd", dma(dst_, src_), inc=s_cast, incn=16)
        for ee_ in range(4):
            ev_wcast[f"wg{ee_}"] = tl.add("gpsimd", dma(wg_b[ee_], wgT[ee_]),
                                          inc=s_cast, incn=16)
            ev_wcast[f"wu{ee_}"] = tl.add("gpsimd", dma(wu_b[ee_], wuT[ee_]),
                                          inc=s_cast, incn=16)
            ev_wcast[f"wd{ee_}"] = tl.add("gpsimd", dma(wd_b[ee_], wdT[ee_]),
                                          inc=s_cast, incn=16)
        ev_cast_all = ev_wcast["wd3"]
        for k_ in ev_wcast:
            ev_wcast[k_] = ev_cast_all
        ev_nw = tl.add("sync", dma(nw1[:], normw), inc=s_const, incn=16)
        ev_const = ev_nw

        def mk_nwb(e):
            e.matmul(P1[:, 0:512], ones1b[:], nw1[:, 0:512], start=True, stop=True)
            return e.matmul(P1[:, 512:1024], ones1b[:], nw1[:, 512:1024],
                            start=True, stop=True)
        ev_nwmm = tl.add("tensor", mk_nwb, waits=[ev_nw, ev_const], inc=st_)
        ev_nwb = tl.add("vector", lambda e: e.tensor_copy(normwb[:], P1[:]),
                        waits=[ev_nwmm], inc=sv)

        evx = [None] * NT
        ev_stats = [None] * NT
        ev_sq = [None] * NT
        ev_recip = [None] * NT
        ev_xn = [None] * NT
        ev_xnTpe = [None] * NT
        ev_exnT = [None] * NT
        ev_zf = [None] * NT
        ev_zv = [None] * NT
        ev_zo = [None] * NT
        ev_zfm = [None] * NT
        ev_f = [None] * NT
        ev_o = [None] * NT
        ev_vs = [None] * NT
        ev_u = [None] * NT
        ev_fTpe = [None] * NT
        ev_uTpe = [None] * NT
        ev_efT = [None] * NT
        ev_scan = [None] * NT
        ev_stbpe = [None] * NT
        ev_y = [None] * NT
        ev_ret = [None] * NT
        ev_st2 = [None] * NT
        ev_sq2 = [None] * NT
        ev_r2 = [None] * NT
        ev_ln2 = [None] * NT
        ev_xm = [None] * NT
        ev_st3 = [None] * NT
        ev_sq3 = [None] * NT
        ev_hid = [None] * NT
        ev_hTpe = [None] * NT
        ev_ehT = [None] * NT
        ev_xmdma = [None] * NT
        ev_hdma = [None] * NT

        def stageA(i):
            real = i >= NTW
            waits = [evx[i - 3], ev_xm[i - 3] if i - 3 >= NTW else ev_xn[i - 3]] if i >= 3 else []
            evx[i] = tl.add("sync", dma(xb[:, i % 3, :], xv[i]), waits=waits,
                            inc=s_x[i % 3], incn=16)
            tl.add("vector", lambda e, i=i: e.reduce_sum(
                sta[:, i % 2, 0:1], xb[:, i % 3, :], AX.X), waits=[evx[i]],
                nofence=True)
            tl.add("vector", lambda e, i=i: e.scalar_tensor_tensor(
                junk[:], xb[:, i % 3, :], 1.0, xb[:, i % 3, :], ALU.mult, ALU.mult,
                accum_out=sta[:, i % 2, 1:2]), waits=[evx[i]])
            tl.add("vector", lambda e, i=i: e.tensor_scalar(
                sta[:, i % 2, 2:3], sta[:, i % 2, 0:1], -1.0 / D, None, ALU.mult))
            tl.add("vector", lambda e, i=i: e.tensor_scalar(
                sta[:, i % 2, 3:4], sta[:, i % 2, 1:2], 1.0 / D, None, ALU.mult))
            tl.add("vector", lambda e, i=i: e.tensor_mul(
                sta[:, i % 2, 4:5], sta[:, i % 2, 2:3], sta[:, i % 2, 2:3]))
            tl.add("vector", lambda e, i=i: e.tensor_sub(
                sta[:, i % 2, 5:6], sta[:, i % 2, 3:4], sta[:, i % 2, 4:5]))
            if real:
                tl.add("vector", lambda e, i=i: e.tensor_scalar(
                    sta[:, i % 2, 6:7], sta[:, i % 2, 5:6], 1.0, 1e-5,
                    ALU.mult, ALU.add))
            ev_stats[i] = tl.add("vector", lambda e, i=i: e.tensor_scalar(
                sta[:, i % 2, 7:8], sta[:, i % 2, 5:6], 1.0 + 1e-6, 1e-11,
                ALU.mult, ALU.add), inc=sv)

            w = [ev_stats[i]]
            if real:
                tl.add("scalar", lambda e, i=i: e.activation(
                    sta[:, i % 2, 8:9], sta[:, i % 2, 6:7], AF.Sqrt), waits=w)
                w = []
            ev_sq[i] = tl.add("scalar", lambda e, i=i: e.activation(
                sta[:, i % 2, 9:10], sta[:, i % 2, 7:8], AF.Sqrt), waits=w, inc=ss)
            if i - 1 >= NTW and ev_st2[i - 1] is not None:
                ev_sq2[i - 1] = tl.add("scalar", lambda e, i=i: e.activation(
                    st2[:, (i - 1) % 2, 4:5], st2[:, (i - 1) % 2, 3:4], AF.Sqrt),
                    waits=[ev_st2[i - 1]], inc=ss)
            if i - 2 >= NTW and ev_st3[i - 2] is not None:
                ev_sq3[i - 2] = tl.add("scalar", lambda e, i=i: e.activation(
                    st2[:, (i - 2) % 2, 12:13], st2[:, (i - 2) % 2, 11:12], AF.Sqrt),
                    waits=[ev_st3[i - 2]], inc=ss)

            w = [ev_sq[i]]
            if real:
                tl.add("vector", lambda e, i=i: e.reciprocal(
                    sta[:, i % 2, 10:11], sta[:, i % 2, 8:9]), waits=w)
                w = []
            ev_recip[i] = tl.add("vector", lambda e, i=i: e.reciprocal(
                sta[:, i % 2, 11:12], sta[:, i % 2, 9:10]), waits=w, inc=sv)
            if real:
                w = [ev_recip[i]]
                if i - 2 >= NTW:
                    w.append(ev_ret[i - 2])
                tl.add("vector", lambda e, i=i: e.tensor_scalar(
                    ln1b[:, i % 2, :], xb[:, i % 3, :], sta[:, i % 2, 2:3],
                    sta[:, i % 2, 10:11], ALU.add, ALU.mult), waits=w)
            tl.add("vector", lambda e, i=i: e.tensor_scalar(
                tpre[:], xb[:, i % 3, :], sta[:, i % 2, 2:3], sta[:, i % 2, 11:12],
                ALU.add, ALU.mult), waits=[ev_recip[i]])
            w = [ev_nwb, ev_identb]
            if i >= 1:
                w.append(ev_xnTpe[i - 1])
            ev_xn[i] = tl.add("vector", lambda e: e.tensor_mul(
                xnb[:], tpre[:], normwb[:]), waits=w, inc=sv, nofence=True)

            w = [ev_xn[i]]
            if i >= 1:
                w.append(ev_exnT[i - 1])
            if i - 3 >= NTW and ev_ehT[i - 3] is not None:
                w.append(ev_ehT[i - 3])

            def mk_xnT(e):
                r = None
                for g in range(KC):
                    r = e.transpose(PB[:, g * 128:(g + 1) * 128],
                                    xnb[:, g * 128:(g + 1) * 128], identbb[:])
                return r
            ev_xnTpe[i] = tl.add("tensor", mk_xnT, waits=w, inc=st_)
            ev_exnT[i] = tl.add("vector", lambda e: e.tensor_copy(
                xnT[:].rearrange("p k n -> p (k n)"), PB[:]),
                waits=[ev_xnTpe[i]], inc=sv, nofence=True)

            def mk_proj(dst, wsb):
                def f(e):
                    r = None
                    for k in range(KC):
                        for n in range(2):
                            r = e.matmul(dst[:, n * 512:(n + 1) * 512],
                                         xnT[:, k, :],
                                         wsb[:, k, n * 512:(n + 1) * 512],
                                         start=(k == 0), stop=(k == KC - 1))
                    return r
                return f
            w = [ev_exnT[i], ev_wf]
            if i >= 1:
                w.append(ev_efT[i - 1])
            ev_zf[i] = tl.add("tensor", mk_proj(P1, wfb), waits=w, inc=st_)
            w = [ev_exnT[i], ev_wi]
            if i >= 1:
                w.append(ev_scan[i - 1])
            ev_zv[i] = tl.add("tensor", mk_proj(P2, wib), waits=w, inc=st_)
            if real:
                w = [ev_exnT[i], ev_wo]
                if i - 1 >= NTW:
                    w.append(ev_y[i - 1])
                ev_zo[i] = tl.add("tensor", mk_proj(P3, wob), waits=w, inc=st_)

            ev_zfm[i] = tl.add("vector", lambda e: e.tensor_scalar(
                junk[:], P1[:], LN9, None, ALU.max), waits=[ev_zf[i]], inc=sv,
                nofence=True)
            ev_f[i] = tl.add("scalar", lambda e: e.activation(
                fb[:], junk[:], AF.Sigmoid), waits=[ev_zfm[i]], inc=ss)
            if real:
                ev_o[i] = tl.add("scalar", lambda e: e.activation(
                    ob[:], P3[:], AF.Sigmoid), waits=[ev_zo[i]], inc=ss)
            ev_vs[i] = tl.add("vector", lambda e: e.tensor_reduce(
                vs[:], P2[:].rearrange("p (h d) -> p h d", h=H), AX.X, ALU.add),
                waits=[ev_zv[i]], inc=sv, nofence=True)
            tl.add("vector", lambda e: e.tensor_scalar(
                junk[:], fb[:], -1.0, 1.0, ALU.mult, ALU.add), waits=[ev_f[i]],
                nofence=True)
            ev_u[i] = tl.add("vector", lambda e: e.tensor_tensor(
                ub[:].rearrange("p (h d) -> p h d", h=H),
                junk[:].rearrange("p (h d) -> p h d", h=H),
                vs[:].to_broadcast([128, H, DH]), ALU.mult), inc=sv)

            def mk_tr(dst_ps, src):
                def f(e):
                    r = None
                    for g in range(KC):
                        r = e.transpose(dst_ps[:, g * 128:(g + 1) * 128],
                                        src[:, g * 128:(g + 1) * 128], identb32[:])
                    return r
                return f
            ev_fTpe[i] = tl.add("tensor", mk_tr(P1, fb[:]),
                                waits=[ev_f[i], ev_zfm[i]], inc=st_)
            ev_uTpe[i] = tl.add("tensor", mk_tr(P2, ub[:]),
                                waits=[ev_u[i], ev_vs[i]], inc=st_)
            ev_efT[i] = tl.add("vector", lambda e: e.tensor_copy(fTs[:], P1[:]),
                               waits=[ev_fTpe[i]], inc=sv, nofence=True)
            ev_euT = tl.add("vector", lambda e: e.tensor_copy(ub[:], P2[:]),
                            waits=[ev_uTpe[i]], inc=sv, nofence=True)

            w = [ev_efT[i], ev_euT]
            if i >= 1:
                w.append(ev_scan[i - 1])
            if i - 2 >= NTW:
                w.append(ev_stbpe[i - 2])

            def mk_scans(e, i=i):
                r = None
                for g in range(KC):
                    init = 0.0 if i == 0 else sTb[:, (i - 1) % 2,
                                                  g * 128 + 127:g * 128 + 128]
                    r = e.tensor_tensor_scan(
                        sTb[:, i % 2, g * 128:(g + 1) * 128],
                        fTs[:, g * 128:(g + 1) * 128],
                        ub[:, g * 128:(g + 1) * 128],
                        init, ALU.mult, ALU.add)
                return r
            ev_scan[i] = tl.add("vector", mk_scans, waits=w, inc=sv)
            if debug_taps and i == 0:
                tl.add("gpsimd", dma(dbg3[0], xb[:, 0, :]), waits=[ev_scan[0]],
                       inc=s_dbg, incn=16)
                tl.add("gpsimd", dma(dbg3[1], ub[:]), waits=[ev_scan[0]],
                       inc=s_dbg, incn=16)
                tl.add("gpsimd", dma(dbg3[2], sTb[:, 0, :]), waits=[ev_scan[0]],
                       inc=s_dbg, incn=16)
                tl.add("gpsimd", dma(dbg3[3], tpre[:]), waits=[ev_scan[0]],
                       inc=s_dbg, incn=16)
                def _cast_xn(e):
                    return e.dma_start(out=dbg3[4], in_=xnb[:])
                tl.add("gpsimd", _cast_xn, waits=[ev_scan[0]],
                       inc=s_dbg, incn=16)
                tl.add("gpsimd", dma(dbg3[5][:, 0:16], sta[:, 0, :]),
                       waits=[ev_scan[0]], inc=s_dbg, incn=16)

            if not real:
                return
            ev_stbpe[i] = tl.add("tensor", mk_tr(P3, sTb[:, i % 2, :]),
                                 waits=[ev_scan[i], ev_o[i]], inc=st_)
            ev_y[i] = tl.add("vector", lambda e: e.tensor_mul(yb[:], ob[:], P3[:]),
                             waits=[ev_stbpe[i]], inc=sv, nofence=True)
            w = []
            if i - 2 >= NTW:
                w.append(ev_ln2[i - 2])
            ev_ret[i] = tl.add("vector", lambda e, i=i: e.tensor_add(
                retb[:, i % 2, :], ln1b[:, i % 2, :], yb[:]), waits=w, inc=sv,
                nofence=True)
            w = []
            if i - 2 >= NTW and ev_hid[i - 2] is not None:
                w.append(ev_hid[i - 2])
            tl.add("vector", lambda e, i=i: e.reduce_sum(
                st2[:, i % 2, 0:1], retb[:, i % 2, :], AX.X), waits=w,
                nofence=True)
            tl.add("vector", lambda e, i=i: e.scalar_tensor_tensor(
                junk[:], retb[:, i % 2, :], 1.0, retb[:, i % 2, :], ALU.mult,
                ALU.mult, accum_out=st2[:, i % 2, 1:2]))
            tl.add("vector", lambda e, i=i: e.tensor_scalar(
                st2[:, i % 2, 2:3], st2[:, i % 2, 0:1], -1.0 / D, None, ALU.mult))
            tl.add("vector", lambda e, i=i: e.tensor_mul(
                junk[:, 0:1], st2[:, i % 2, 2:3], st2[:, i % 2, 2:3]))
            tl.add("vector", lambda e, i=i: e.tensor_scalar(
                junk[:, 1:2], st2[:, i % 2, 1:2], 1.0 / D, 1e-5, ALU.mult, ALU.add))
            ev_st2[i] = tl.add("vector", lambda e, i=i: e.tensor_sub(
                st2[:, i % 2, 3:4], junk[:, 1:2], junk[:, 0:1]), inc=sv)

        def stageB(i):
            ev_r2[i] = tl.add("vector", lambda e, i=i: e.reciprocal(
                st2[:, i % 2, 5:6], st2[:, i % 2, 4:5]), waits=[ev_sq2[i]], inc=sv)
            ev_ln2[i] = tl.add("vector", lambda e, i=i: e.tensor_scalar(
                ln2b[:], retb[:, i % 2, :], st2[:, i % 2, 2:3],
                st2[:, i % 2, 5:6], ALU.add, ALU.mult),
                waits=[ev_r2[i]], inc=sv)
            w = [ev_ln2[i]]
            if i - 2 >= NTW:
                w.append(ev_xmdma[i - 2])
            ev_xm[i] = tl.add("vector", lambda e, i=i: e.tensor_add(
                xmb[:, i % 2, :], ln2b[:], xb[:, i % 3, :]), waits=w, inc=sv,
                nofence=True)
            ev_xmdma[i] = tl.add("scalar", lambda e, i=i: e.dma_start(
                out=xpark.rearrange("(t p) d -> t p d", p=128)[i - NTW],
                in_=xmb[:, i % 2, :]), waits=[ev_xm[i]], inc=s_xp[i % 2], incn=16)
            tl.add("vector", lambda e, i=i: e.reduce_sum(
                st2[:, i % 2, 8:9], xmb[:, i % 2, :], AX.X), nofence=True)
            tl.add("vector", lambda e, i=i: e.scalar_tensor_tensor(
                junk[:], xmb[:, i % 2, :], 1.0, xmb[:, i % 2, :], ALU.mult,
                ALU.mult, accum_out=st2[:, i % 2, 9:10]))
            tl.add("vector", lambda e, i=i: e.tensor_scalar(
                st2[:, i % 2, 10:11], st2[:, i % 2, 8:9], -1.0 / D, None, ALU.mult))
            tl.add("vector", lambda e, i=i: e.tensor_mul(
                junk[:, 0:1], st2[:, i % 2, 10:11], st2[:, i % 2, 10:11]))
            tl.add("vector", lambda e, i=i: e.tensor_scalar(
                junk[:, 1:2], st2[:, i % 2, 9:10], 1.0 / D, 1e-5, ALU.mult, ALU.add))
            ev_st3[i] = tl.add("vector", lambda e, i=i: e.tensor_sub(
                st2[:, i % 2, 11:12], junk[:, 1:2], junk[:, 0:1]), inc=sv)

        def stageC(i):
            ev_r3 = tl.add("vector", lambda e, i=i: e.reciprocal(
                st2[:, i % 2, 13:14], st2[:, i % 2, 12:13]), waits=[ev_sq3[i]],
                inc=sv)
            w = [ev_r3]
            if i - 2 >= NTW and ev_hTpe[i - 2] is not None:
                w.append(ev_hTpe[i - 2])
                w.append(ev_hdma[i - 2])
            ev_hid[i] = tl.add("vector", lambda e, i=i: e.tensor_scalar(
                hbb[:, i % 2, :], xmb[:, i % 2, :], st2[:, i % 2, 10:11],
                st2[:, i % 2, 13:14], ALU.add, ALU.mult),
                waits=w, inc=sv)
            ev_hdma[i] = tl.add("scalar", lambda e, i=i: e.dma_start(
                out=hid_loc.rearrange("(t p) d -> t p d", p=128)[i - NTW],
                in_=hbb[:, i % 2, :]), waits=[ev_hid[i]], inc=s_hd[i % 2], incn=16)
            w = [ev_hid[i]]
            tslot = min(i + 2, NT - 1)
            w.append(ev_exnT[tslot])
            if i - 1 >= NTW and ev_ehT[i - 1] is not None:
                w.append(ev_ehT[i - 1])

            def mk_hT(e, i=i):
                r = None
                for g in range(KC):
                    r = e.transpose(PB[:, g * 128:(g + 1) * 128],
                                    hbb[:, i % 2, g * 128:(g + 1) * 128], identbb[:])
                return r
            ev_hTpe[i] = tl.add("tensor", mk_hT, waits=w, inc=st_)
            j = i - NTW
            ev_ehT[i] = tl.add("vector", lambda e, j=j: e.tensor_copy(
                hidT[:, :, j * 128:(j + 1) * 128],
                PB[:].rearrange("p (k n) -> p k n", k=KC)),
                waits=[ev_hTpe[i]], inc=sv, nofence=True)

        for i in range(NT):
            stageA(i)
            if i - 1 >= NTW:
                stageB(i - 1)
            if i - 2 >= NTW:
                stageC(i - 2)
        i = NT - 1
        ev_sq2[i] = tl.add("scalar", lambda e, i=i: e.activation(
            st2[:, i % 2, 4:5], st2[:, i % 2, 3:4], AF.Sqrt),
            waits=[ev_st2[i]], inc=ss)
        stageB(NT - 1)
        for i in (NT - 2, NT - 1):
            ev_sq3[i] = tl.add("scalar", lambda e, i=i: e.activation(
                st2[:, i % 2, 12:13], st2[:, i % 2, 11:12], AF.Sqrt),
                waits=[ev_st3[i]], inc=ss)
            stageC(i)

        dnv = dense.rearrange("(n p) d -> n p d", p=128)
        ev_dense0 = None
        for n_ in range(T // 128):
            ev_dense0 = tl.add("scalar", dma(dnv[n_], zzzb[:]), waits=[ev_zzzb],
                               inc=s_dense, incn=16)

        if True:
            ev_gmm = [None] * NTR
            ev_sco = [None] * NTR
            for j in range(NTR):
                i = j + NTW

                def mk_gate(e, j=j):
                    r = None
                    for k in range(KC):
                        r = e.matmul(P4[:, 0:E], hidT[:, k, j * 128:(j + 1) * 128],
                                     gatb[:, k, :], start=(k == 0),
                                     stop=(k == KC - 1))
                    return r
                w = [ev_ehT[i], ev_gat]
                if j >= 1:
                    w.append(ev_sco[j - 1])
                ev_gmm[j] = tl.add("tensor", mk_gate, waits=w, inc=st_)
                tl.add("vector", lambda e: e.reduce_max(
                    gj[:, 2:3], P4[:, 0:E], AX.X), waits=[ev_gmm[j]])
                ev_mx = tl.add("vector", lambda e: e.tensor_scalar(
                    gj[:, 3:4], gj[:, 2:3], -1.0, None, ALU.mult), inc=sv)
                ev_ex = tl.add("scalar", lambda e, j=j: e.activation(
                    sco_all[:, j, :], P4[:, 0:E], AF.Exp, bias=gj[:, 3:4]),
                    waits=[ev_mx], inc=ss)
                tl.add("vector", lambda e, j=j: e.reduce_sum(
                    gj[:, 4:5], sco_all[:, j, :], AX.X), waits=[ev_ex])
                ev_rc = tl.add("vector", lambda e: e.reciprocal(
                    gj[:, 5:6], gj[:, 4:5]), inc=sv)
                ev_sco[j] = tl.add("vector", lambda e, j=j: e.tensor_scalar(
                    sco_all[:, j, :], sco_all[:, j, :], gj[:, 5:6], None,
                    ALU.mult), waits=[ev_rc], inc=sv)
            ev_scodma = tl.add("scalar", lambda e: e.dma_start(
                out=sco_loc.rearrange("(j p) e -> p j e", p=128), in_=sco_all[:]),
                waits=[ev_sco[NTR - 1]], inc=s_sc, incn=16)

        ev_aghid = tl.add("gpsimd", lambda e: e.collective_compute(
            "AllGather", ALU.bypass, replica_groups=groups,
            ins=[hid_loc.opt()], outs=[hid_full.opt()]),
            waits=[ev_hdma[NT - 1], ev_hdma[NT - 2]], inc=scc)
        ev_agsco = tl.add("gpsimd", lambda e: e.collective_compute(
            "AllGather", ALU.bypass, replica_groups=groups,
            ins=[sco_loc.opt()], outs=[sco_full.opt()]),
            waits=[ev_scodma], inc=scc)

        if debug_taps:
            tl.add("gpsimd", dma(dbg_xmid, xpark),
                   waits=[ev_xmdma[NT - 1], ev_xmdma[NT - 2]],
                   inc=s_dbg, incn=16)
            tl.add("gpsimd", dma(dbg_hid, hid_loc),
                   waits=[ev_hdma[NT - 1], ev_hdma[NT - 2]],
                   inc=s_dbg, incn=16)
            tl.add("gpsimd", dma(dbg_sco, sco_full), waits=[ev_agsco],
                   inc=s_dbg, incn=16)
            _i = NT - 1
            _dumps = [xb[:, _i % 3, :], ln1b[:, _i % 2, :], tpre[:], fb[:],
                      ob[:], ub[:], fTs[:], sTb[:, _i % 2, :], yb[:],
                      retb[:, _i % 2, :], xmb[:, _i % 2, :], junk[:]]
            _dw = [ev_agsco, ev_hdma[NT - 1], ev_xmdma[NT - 1],
                   ev_scan[NT - 1]]
            for _k, _src in enumerate(_dumps):
                tl.add("gpsimd", dma(dbg2[_k], _src), waits=_dw,
                       inc=s_dbg, incn=16)

        ev_last_efT = ev_efT[NT - 1]
        ev_last_scan = ev_scan[NT - 1]
        ev_last_y = ev_y[NT - 1]
        ev_last_ehT = ev_ehT[NT - 1]

    if attn_only:
        ev_fin = ev_hdma[NT - 1]
        if debug_taps:
            ev_fin = tl.add('gpsimd', lambda e: e.nop(), waits=[tl.cur(s_dbg)], inc=None) or ev_fin
        tl.add('sync', lambda e: None, waits=[ev_agsco, ev_fin])
        esA.close()
        _hidT_guard.__exit__(None, None, None)
    else:
        esA.close()
        ev_spark_last = None
        esS = contextlib.ExitStack()
        wsgs = esS.enter_context(nc.sbuf_tensor("wsgs", [128, KC, 2, 1024], BF16))
        wsus = esS.enter_context(nc.sbuf_tensor("wsus", [128, KC, 2, 1024], BF16))
        wsds = esS.enter_context(nc.sbuf_tensor("wsds", [128, MS, D], BF16))
        hsT = esS.enter_context(nc.sbuf_tensor("hsT", [128, MS, 1024], BF16))
        gtmp = esS.enter_context(nc.sbuf_tensor("gtmp", [128, 1024], BF16))
        shout = esS.enter_context(nc.sbuf_tensor("shout", [128, D], F32))
        if True:
            attn_dead = [ev_ehT[NT - 1], ev_hdma[NT - 1], ev_hdma[NT - 2],
                         ev_xmdma[NT - 1], ev_xmdma[NT - 2]]
            tl.add("sync", dma(wsds[:], wsd_b), waits=[ev_wcast["wsd"]] + attn_dead,
                   inc=s_wsh, incn=16)
            for hh in range(2):
                tl.add("sync", dma(
                    wsgs[:, :, hh, :], wsg_b[:, :, hh * 1024:(hh + 1) * 1024]),
                    waits=[ev_wcast["wsg"]] + attn_dead, inc=s_wsh, incn=16)
                ev_wsh_t = tl.add("sync", dma(
                    wsus[:, :, hh, :], wsu_b[:, :, hh * 1024:(hh + 1) * 1024]),
                    waits=[ev_wcast["wsu"]] + attn_dead, inc=s_wsh, incn=16)
            ev_wsd = ev_wsh_t
            ev_wsg_h = [ev_wsh_t, ev_wsh_t]
            ev_wsu_h = [ev_wsh_t, ev_wsh_t]

            ev_sh_h = {}
            ev_gel_prev = None
            ev_dn_half = [None, None]
            ev_shev = None
            for th in range(2):
                for m in range(MS):
                    hh = m // 8

                    def mk_g(e, th=th, m=m, hh=hh):
                        r = None
                        for tb in range(2):
                            tsl = slice(th * 1024 + tb * 512, th * 1024 + (tb + 1) * 512)
                            for k in range(KC):
                                r = e.matmul(
                                    P1[:, tb * 512:(tb + 1) * 512],
                                    wsgs[:, k, hh, (m % 8) * 128:(m % 8 + 1) * 128],
                                    hidT[:, k, tsl],
                                    start=(k == 0), stop=(k == KC - 1))
                        return r
                    w = [ev_wsg_h[hh], ev_last_ehT]
                    if ev_gel_prev is not None:
                        w.append(ev_gel_prev)
                    if th == 1 and m == 0:
                        w.append(ev_dn_half[0])
                    if th == 0 and m == 0:
                        w.append(ev_last_efT)
                        w.append(ev_last_scan)
                    ev_gmmS = tl.add("tensor", mk_g, waits=w, inc=st_)

                    def mk_u(e, th=th, m=m, hh=hh):
                        r = None
                        for tb in range(2):
                            tsl = slice(th * 1024 + tb * 512, th * 1024 + (tb + 1) * 512)
                            for k in range(KC):
                                r = e.matmul(
                                    P2[:, tb * 512:(tb + 1) * 512],
                                    wsus[:, k, hh, (m % 8) * 128:(m % 8 + 1) * 128],
                                    hidT[:, k, tsl],
                                    start=(k == 0), stop=(k == KC - 1))
                        return r
                    wu_ = [ev_wsu_h[hh], ev_last_ehT]
                    if (th, m - 1) in ev_sh_h:
                        wu_.append(ev_sh_h[(th, m - 1)])
                    elif th == 1 and m == 0:
                        wu_.append(ev_sh_h[(0, MS - 1)])
                    elif th == 0 and m == 0:
                        wu_.append(ev_last_scan)
                    ev_ummS = tl.add("tensor", mk_u, waits=wu_, inc=st_)
                    w2 = [ev_gmmS]
                    if ev_gel_prev is not None:
                        w2.append(ev_sh_h.get((th, m - 1)) or ev_sh_h.get((0, MS - 1)))
                    ev_gel = tl.add("scalar", lambda e: e.activation(
                        gtmp[:], P1[:], AF.Gelu), waits=w2, inc=ss)
                    ev_gel_prev = ev_gel
                    ev_sh_h[(th, m)] = tl.add("vector", lambda e, m=m: e.tensor_mul(
                        hsT[:, m, :], gtmp[:], P2[:]), waits=[ev_gel, ev_ummS], inc=sv,
                        nofence=True)
                ev_dn = None
                for tt in range(8):
                    def mk_dn(e, th=th, tt=tt):
                        r = None
                        for n in range(2):
                            for fc in range(MS):
                                r = e.matmul(P3[:, n * 512:(n + 1) * 512],
                                             hsT[:, fc, tt * 128:(tt + 1) * 128],
                                             wsds[:, fc, n * 512:(n + 1) * 512],
                                             start=(fc == 0), stop=(fc == MS - 1))
                        return r
                    w = [ev_sh_h[(th, MS - 1)], ev_wsd]
                    if ev_shev is not None:
                        w.append(ev_shev)
                    if th == 0 and tt == 0:
                        w.append(ev_last_y)
                    ev_dn = tl.add("tensor", mk_dn, waits=w, inc=st_)
                    w = [ev_dn]
                    if ev_spark_last is not None:
                        w.append(ev_spark_last)
                    ev_shev = tl.add("vector", lambda e: e.tensor_copy(
                        shout[:], P3[:]), waits=w, inc=sv, nofence=True)
                    ev_spark_last = tl.add("scalar", lambda e, th=th, tt=tt: e.dma_start(
                        out=spark.rearrange("(t p) d -> t p d", p=128)[th * 8 + tt],
                        in_=shout[:]), waits=[ev_shev], inc=s_sp, incn=16)
                ev_dn_half[th] = ev_dn
            ev_shev_last = ev_shev
            ev_sh_h_last = ev_sh_h[(1, MS - 1)]

        esS.close()
        _hidT_guard.__exit__(None, None, None)
        esT = contextlib.ExitStack()
        sco4 = esT.enter_context(nc.sbuf_tensor("sco4", [128, 64, E], F32))
        lot = esT.enter_context(nc.sbuf_tensor("lot", [1, E], F32))
        hit = esT.enter_context(nc.sbuf_tensor("hit", [1, E], F32))
        midt = esT.enter_context(nc.sbuf_tensor("midt", [1, E], F32))
        cntt = esT.enter_context(nc.sbuf_tensor("cntt", [1, E], F32))
        needt = esT.enter_context(nc.sbuf_tensor("needt", [1, E], F32))
        geb = esT.enter_context(nc.sbuf_tensor("geb", [1, 2 * E], I32))
        midv = esT.enter_context(nc.sbuf_tensor("midv", [128, E], F32))
        maskb = esT.enter_context(nc.sbuf_tensor("maskb", [128, 64 * E], F32))
        pc = esT.enter_context(nc.sbuf_tensor("pc", [128, E], F32))
        hiv = esT.enter_context(nc.sbuf_tensor("hiv", [128, E], F32))
        lov = esT.enter_context(nc.sbuf_tensor("lov", [128, E], F32))
        needv = esT.enter_context(nc.sbuf_tensor("needv", [128, E], F32))
        m_hi = esT.enter_context(nc.sbuf_tensor("m_hi", [128, 64], F32))
        m_bd = esT.enter_context(nc.sbuf_tensor("m_bd", [128, 64], F32))
        cum = esT.enter_context(nc.sbuf_tensor("cum", [128, 64], F32))
        rnk = esT.enter_context(nc.sbuf_tensor("rnk", [128, 64], F32))
        tk = esT.enter_context(nc.sbuf_tensor("tk", [128, 64], F32))
        selb = esT.enter_context(nc.sbuf_tensor("selb", [128, 64], F32))
        selbi = esT.enter_context(nc.sbuf_tensor("selbi", [128, 64], I32))
        dst_all = esT.enter_context(nc.sbuf_tensor("dst_all", [128, 64, E], F32))
        rowt = esT.enter_context(nc.sbuf_tensor("rowt", [128, 1], F32))
        offs = esT.enter_context(nc.sbuf_tensor("offs", [128, 1], F32))
        odst = esT.enter_context(nc.sbuf_tensor("odst", [128, 64], F32))
        if True:
            shared_dead = [ev_spark_last, ev_shev_last]
            ev_sco4 = tl.add("sync", dma(
                sco4[:], sco_full.rearrange("(p j) e -> p j e", p=128)),
                waits=[ev_agsco] + shared_dead, inc=s_misc, incn=16)
            tl.add("vector", lambda e: e.memset(lot[:], 0.0), waits=shared_dead)
            tl.add("vector", lambda e: e.memset(hit[:], 1.0))
            ev_upd = tl.add("vector", lambda e: e.memset(midt[:], 0.5), inc=sv)

            for it in range(BISECT_ITERS + 1):
                last = it == BISECT_ITERS
                src = hit if last else midt
                ev_midB = tl.add("tensor", lambda e, src=src: e.matmul(
                    P4[:, 16:16 + E], ones1b[:], src[:], start=True, stop=True),
                    waits=[ev_upd, ev_const, ev_sco[NTR - 1]], inc=st_)
                ev_mve = tl.add("vector", lambda e: e.tensor_copy(
                    midv[:], P4[:, 16:16 + E]), waits=[ev_midB], inc=sv)
                ev_mask = tl.add("vector", lambda e: e.tensor_tensor(
                    maskb[:].rearrange("p (j e) -> p j e", e=E), sco4[:],
                    midv[:].rearrange("p (a e) -> p a e", a=1).to_broadcast(
                        [128, 64, E]), ALU.is_ge), waits=[ev_mve, ev_sco4], inc=sv)
                ev_pc = tl.add("vector", lambda e: e.tensor_reduce(
                    pc[:], maskb[:].rearrange("p (j e) -> p e j", e=E), AX.X, ALU.add),
                    inc=sv)
                ev_cmm = tl.add("tensor", lambda e: e.matmul(
                    P4[0:1, 0:E], onescb[:], pc[:], start=True, stop=True),
                    waits=[ev_pc, ev_mve], inc=st_)
                ev_cnt = tl.add("vector", lambda e: e.tensor_copy(
                    cntt[:], P4[0:1, 0:E]), waits=[ev_cmm], inc=sv)
                if last:
                    ev_upd = tl.add("vector", lambda e: e.tensor_scalar(
                        needt[:], cntt[:], -1.0, float(CAP), ALU.mult, ALU.add), inc=sv)
                else:
                    tl.add("vector", lambda e: e.tensor_scalar(
                        geb[:, 0:E], cntt[:], float(CAP), None, ALU.is_ge))
                    tl.add("vector", lambda e: e.tensor_scalar(
                        geb[:, E:2 * E], cntt[:], float(CAP), None, ALU.is_lt))
                    tl.add("vector", lambda e: e.copy_predicated(
                        lot[:], geb[:, 0:E], midt[:]))
                    tl.add("vector", lambda e: e.copy_predicated(
                        hit[:], geb[:, E:2 * E], midt[:]))
                    tl.add("vector", lambda e: e.tensor_add(midt[:], lot[:], hit[:]))
                    ev_upd = tl.add("vector", lambda e: e.tensor_scalar(
                        midt[:], midt[:], 0.5, None, ALU.mult), inc=sv)

            ev_bcp = ev_upd
            for srct, dstv in ((lot, lov), (hit, hiv), (needt, needv)):
                evm = tl.add("tensor", lambda e, srct=srct: e.matmul(
                    P4[:, 16:16 + E], ones1b[:], srct[:], start=True, stop=True),
                    waits=[ev_bcp], inc=st_)
                ev_bcp = tl.add("vector", lambda e, dstv=dstv: e.tensor_copy(
                    dstv[:], P4[:, 16:16 + E]), waits=[evm], inc=sv)
            ev_bcast = ev_bcp

            ev_selp = ev_bcast
            for e_ in range(E):
                tl.add("vector", lambda e, e_=e_: e.tensor_tensor(
                    m_hi[:], sco4[:, :, e_], hiv[:, e_:e_ + 1].to_broadcast([128, 64]),
                    ALU.is_ge), waits=[ev_selp])
                tl.add("vector", lambda e, e_=e_: e.tensor_tensor(
                    m_bd[:], sco4[:, :, e_], lov[:, e_:e_ + 1].to_broadcast([128, 64]),
                    ALU.is_ge))
                tl.add("vector", lambda e: e.tensor_sub(m_bd[:], m_bd[:], m_hi[:]))
                tl.add("vector", lambda e: e.tensor_tensor_scan(
                    cum[:], m_bd[:], zzz[:, 0:64], 0.0, ALU.add, ALU.add),
                    waits=[ev_zzz])
                ev_rt = tl.add("vector", lambda e: e.tensor_copy(
                    rowt[:], cum[:, 63:64]), inc=sv)
                ev_om = tl.add("tensor", lambda e: e.matmul(
                    P4[:, 48:49], ltrib[:], rowt[:], start=True, stop=True),
                    waits=[ev_rt], inc=st_)
                ev_of = tl.add("vector", lambda e: e.tensor_copy(
                    offs[:], P4[:, 48:49]), waits=[ev_om], inc=sv)
                tl.add("vector", lambda e: e.scalar_tensor_tensor(
                    rnk[:], cum[:], offs[:], m_bd[:], ALU.add, ALU.subtract),
                    waits=[ev_of])
                tl.add("vector", lambda e, e_=e_: e.tensor_tensor(
                    tk[:], rnk[:], needv[:, e_:e_ + 1].to_broadcast([128, 64]),
                    ALU.is_lt))
                tl.add("vector", lambda e: e.tensor_mul(tk[:], tk[:], m_bd[:]))
                tl.add("vector", lambda e: e.tensor_add(selb[:], m_hi[:], tk[:]))
                tl.add("vector", lambda e: e.tensor_tensor_scan(
                    cum[:], selb[:], zzz[:, 0:64], 0.0, ALU.add, ALU.add))
                ev_rt2 = tl.add("vector", lambda e: e.tensor_copy(
                    rowt[:], cum[:, 63:64]), inc=sv)
                ev_om2 = tl.add("tensor", lambda e: e.matmul(
                    P4[:, 48:49], ltrib[:], rowt[:], start=True, stop=True),
                    waits=[ev_rt2], inc=st_)
                ev_of2 = tl.add("vector", lambda e: e.tensor_copy(
                    offs[:], P4[:, 48:49]), waits=[ev_om2], inc=sv)
                tl.add("vector", lambda e: e.scalar_tensor_tensor(
                    rnk[:], cum[:], offs[:], selb[:], ALU.add, ALU.subtract),
                    waits=[ev_of2])
                tl.add("vector", lambda e: e.tensor_copy(selbi[:], selb[:]))
                ev_selp = tl.add("vector", lambda e, e_=e_: e.select(
                    dst_all[:, :, e_], selbi[:], rnk[:], bigt[:]),
                    waits=[ev_big], inc=sv)

            ev_idxsc = {}
            ev_oi = {}
            for ee in range(4):
                w = [ev_selp]
                if ee > 0:
                    w.append(ev_idxsc[ee - 1])
                tl.add("vector", lambda e, ee=ee: e.tensor_tensor(
                    maskb[:].rearrange("p (j e) -> p j e", e=E), dst_all[:],
                    oh4b[:, ee, :].rearrange("p (a e) -> p a e", a=1).to_broadcast(
                        [128, 64, E]), ALU.mult), waits=w)
                tl.add("vector", lambda e: e.tensor_reduce(
                    odst[:], maskb[:].rearrange("p (j e) -> p j e", e=E), AX.X,
                    ALU.add))
                ev_oi[ee] = tl.add("vector", lambda e: e.tensor_copy(
                    odsti[:], odst[:]), inc=sv)
                evp = None
                for j in range(64):
                    evp = tl.add("gpsimd", lambda e, ee=ee, j=j: e.indirect_dma_start(
                        out=idxl_d[ee],
                        out_offset=IndirectOffsetOnAxis(odsti[:, j:j + 1], 0),
                        in_=tokidb[:, j:j + 1], in_offset=None,
                        bounds_check=regs['cap'], oob_is_err=False),
                        waits=[ev_oi[ee], ev_const] if j == 0 else [],
                        inc=s_isc, incn=16)
                ev_idxsc[ee] = evp

            if debug_taps:
                for ee in range(4):
                    tl.add("gpsimd", dma(dbg_idx[ee].rearrange("(c a) -> c a", a=1), idxl_d[ee]), waits=[ev_idxsc[3]],
                           inc=s_dbg, incn=16)

            esE = contextlib.ExitStack()
            xtokT = esE.enter_context(nc.sbuf_tensor("xtokT", [128, KC, CAP], BF16))
            hTx = esE.enter_context(nc.sbuf_tensor("hTx", [128, MF, CAP], BF16))
            wgs = esE.enter_context(nc.sbuf_tensor("wgs", [128, KC, 2, 1024], BF16))
            wus = esE.enter_context(nc.sbuf_tensor("wus", [128, KC, 2, 1024], BF16))
            wds = esE.enter_context(nc.sbuf_tensor("wds", [128, MF, D], BF16))
            gtmp2 = esE.enter_context(nc.sbuf_tensor("gtmp2", [128, 1024], BF16))
            if True:
                ev_wg_h = {}
                ev_wu_h = {}
                ev_wd_e = {}
                ev_lastmm_gu = {}
                ev_lastmm_d = {}
                ev_scat = {}
                ev_eo_evac = {}
                prev_scat_end = None
                ev_xg_last = None
                ev_ii = {}

                ev_prep = {}

                def prep(ee):
                    par = ee % 2
                    w = [ev_idxsc[ee]]
                    if ee > 0:
                        w.append(ev_prep[ee - 1]["it"])
                    ev_i8 = tl.add("sync", dma(
                        idx8[:], idxl_d[ee].rearrange("(a b) c -> a (b c)", a=8)),
                        waits=w, inc=s_misc, incn=16)
                    w = [ev_i8]
                    if ee > 0:
                        w.append(ev_prep[ee - 1]["ii"])
                    ev_it = tl.add("tensor", lambda e: e.transpose(
                        P4[:, 64:72], idx8[:], identb32[0:8, 0:8]), waits=w, inc=st_)
                    w = [ev_it]
                    if ee > 1:
                        w.append(ev_scat[(ee - 2, 7)])
                        w.append(ev_scat[(ee - 2, 6)])
                    ev_ii = tl.add("vector", lambda e, par=par: e.tensor_copy(
                        idxi[:, par, :], P4[:, 64:72]), waits=w, inc=sv)
                    ev_xg = None
                    ev_tw = None
                    for s in range(8):
                        w = [ev_ii]
                        if ev_tw is not None:
                            w.append(ev_tw)
                        if ee > 1 and s == 0:
                            w.append(ev_eo_evac[(ee - 2, 7)])
                        ev_g1 = tl.add("gpsimd", lambda e, s=s, par=par:
                            e.indirect_dma_start(
                                out=tw16[:], out_offset=None, in_=sco_full,
                                in_offset=IndirectOffsetOnAxis(idxi[:, par, s:s + 1], 0),
                                bounds_check=None, oob_is_err=False),
                            waits=w, inc=s_tw, incn=16)
                        tl.add("vector", lambda e, ee=ee: e.tensor_mul(
                            tw16[:], tw16[:], oh4b[:, ee, :]), waits=[ev_g1],
                            nofence=True)
                        ev_tw = tl.add("vector", lambda e, s=s, par=par:
                            e.tensor_reduce(
                                topw[:, par, s:s + 1], tw16[:], AX.X, ALU.add),
                            inc=sv)
                        w = [ev_ii, ev_agsco]
                        if ee > 0 and s == 0:
                            w.append(ev_prep[ee - 1]["xT"])
                        ev_xg = tl.add("gpsimd", lambda e, s=s, par=par:
                            e.indirect_dma_start(
                                out=xtok[:, s, :], out_offset=None, in_=hid_full,
                                in_offset=IndirectOffsetOnAxis(idxi[:, par, s:s + 1], 0),
                                bounds_check=None, oob_is_err=False),
                            waits=w, inc=s_xg, incn=16)
                    ev_xT = None
                    ev_xTe = None
                    for s in range(8):
                        def mk_xT(e, s=s):
                            r = None
                            for g in range(KC):
                                r = e.transpose(PB[:, g * 128:(g + 1) * 128],
                                                xtok[:, s, g * 128:(g + 1) * 128],
                                                identbb[:])
                            return r
                        if ev_xTe is None:
                            w = [ev_xg, ev_last_ehT]
                            if ee > 0:
                                w.append(ev_lastmm_gu[(ee - 1, 1)])
                        else:
                            w = [ev_xTe]
                        ev_xT = tl.add("tensor", mk_xT, waits=w, inc=st_)
                        ev_xTe = tl.add("vector", lambda e, s=s: e.tensor_copy(
                            xtokT[:, :, s * 128:(s + 1) * 128],
                            PB[:].rearrange("p (k n) -> p k n", k=KC)),
                            waits=[ev_xT], inc=sv, nofence=True)
                    ev_prep[ee] = {"it": ev_it, "ii": ev_ii, "tw": ev_tw,
                                   "xT": ev_xT, "xTe": ev_xTe}

                prep(0)
                ev_gel_last = None
                for ee in range(4):
                    par = ee % 2
                    for hh in range(2):
                        w = [ev_wcast[f"wg{ee}"]]
                        if ee > 0:
                            w.append(ev_lastmm_gu.get((ee - 1, hh)))
                        else:
                            w.extend(shared_dead)
                        ev_wg_h[(ee, hh)] = tl.add("sync", dma(
                            wgs[:, :, hh, :],
                            wg_b[ee][:, :, hh * 1024:(hh + 1) * 1024]),
                            waits=w, inc=s_wg[hh], incn=16)
                        w = [ev_wcast[f"wu{ee}"]]
                        if ee > 0:
                            w.append(ev_lastmm_gu.get((ee - 1, hh)))
                        else:
                            w.extend(shared_dead)
                        ev_wu_h[(ee, hh)] = tl.add("sync", dma(
                            wus[:, :, hh, :],
                            wu_b[ee][:, :, hh * 1024:(hh + 1) * 1024]),
                            waits=w, inc=s_wu[hh], incn=16)
                    w = [ev_wcast[f"wd{ee}"]]
                    if ee > 0:
                        w.append(ev_lastmm_d.get(ee - 1))
                    else:
                        w.extend(shared_dead)
                    ev_wd_e[ee] = tl.add("sync", dma(wds[:], wd_b[ee]),
                                         waits=w, inc=s_wd, incn=16)

                    ev_h = {}
                    ev_gel2 = None
                    for m in range(MF):
                        hh = m // 8

                        def mk_gx(e, m=m, hh=hh):
                            r = None
                            for n in range(2):
                                for k in range(KC):
                                    r = e.matmul(
                                        P1[:, n * 512:(n + 1) * 512],
                                        wgs[:, k, hh, (m % 8) * 128:(m % 8 + 1) * 128],
                                        xtokT[:, k, n * 512:(n + 1) * 512],
                                        start=(k == 0), stop=(k == KC - 1))
                            return r
                        w = [ev_wg_h[(ee, hh)], ev_prep[ee]["xTe"]]
                        if ev_gel2 is not None:
                            w.append(ev_gel2)
                        elif ee > 0:
                            w.append(ev_gel_last)
                        else:
                            w.append(ev_sh_h_last)
                        ev_gmx = tl.add("tensor", mk_gx, waits=w, inc=st_)

                        def mk_ux(e, m=m, hh=hh):
                            r = None
                            for n in range(2):
                                for k in range(KC):
                                    r = e.matmul(
                                        P2[:, n * 512:(n + 1) * 512],
                                        wus[:, k, hh, (m % 8) * 128:(m % 8 + 1) * 128],
                                        xtokT[:, k, n * 512:(n + 1) * 512],
                                        start=(k == 0), stop=(k == KC - 1))
                            return r
                        w = [ev_wu_h[(ee, hh)], ev_prep[ee]["xTe"]]
                        if m > 0:
                            w.append(ev_h[m - 1])
                        elif ee > 0:
                            w.append(ev_eo_evac[(ee - 1, 7)])
                        else:
                            w.append(ev_sh_h_last)
                        ev_umx = tl.add("tensor", mk_ux, waits=w, inc=st_)
                        ev_lastmm_gu[(ee, hh)] = ev_umx
                        w2 = [ev_gmx]
                        if m > 0:
                            w2.append(ev_h[m - 1])
                        ev_gel2 = tl.add("scalar", lambda e: e.activation(
                            gtmp2[:], P1[:], AF.Gelu), waits=w2, inc=ss)
                        ev_h[m] = tl.add("vector", lambda e, m=m: e.tensor_mul(
                            hTx[:, m, :], gtmp2[:], P2[:]),
                            waits=[ev_gel2, ev_umx], inc=sv, nofence=True)
                    ev_gel_last = ev_gel2

                    if ee < 3:
                        prep(ee + 1)

                    ev_dn = None
                    for s in range(8):
                        def mk_dnx(e, s=s):
                            r = None
                            for n in range(2):
                                for fc in range(MF):
                                    r = e.matmul(P3[:, n * 512:(n + 1) * 512],
                                                 hTx[:, fc, s * 128:(s + 1) * 128],
                                                 wds[:, fc, n * 512:(n + 1) * 512],
                                                 start=(fc == 0), stop=(fc == MF - 1))
                            return r
                        w = [ev_h[MF - 1], ev_wd_e[ee]]
                        if ev_dn is not None:
                            w.append(ev_eo_evac[(ee, s - 1)])
                        elif ee == 0:
                            w.append(ev_shev_last)
                        else:
                            w.append(ev_eo_evac[(ee - 1, 7)])
                        ev_dn = tl.add("tensor", mk_dnx, waits=w, inc=st_)
                        ev_lastmm_d[ee] = ev_dn
                        w = [ev_dn, ev_prep[ee]["tw"]]
                        if s >= 2:
                            w.append(ev_scat[(ee, s - 2)])
                        elif ee > 0:
                            w.append(ev_scat[(ee - 1, s + 6)])
                        ev_eo_evac[(ee, s)] = tl.add(
                            "vector", lambda e, s=s, par=par: e.tensor_scalar(
                                eo[:, s % 2, :], P3[:], topw[:, par, s:s + 1], None,
                                ALU.mult), waits=w, inc=sv, nofence=True)
                        w = [ev_eo_evac[(ee, s)], ev_dense0]
                        if s == 0 and prev_scat_end is not None:
                            w.append(prev_scat_end)
                        ev_scat[(ee, s)] = tl.add(
                            "gpsimd", lambda e, ee=ee, s=s, par=par:
                            e.indirect_dma_start(
                                out=dense,
                                out_offset=IndirectOffsetOnAxis(idxi[:, par, s:s + 1], 0),
                                in_=eo[:, s % 2, :], in_offset=None,
                                bounds_check=None, oob_is_err=False,
                                compute_op=ALU.add),
                            waits=w, inc=s_eo[s % 2], incn=16)
                    prev_scat_end = ev_scat[(ee, 7)]

                ev_rs = tl.add("gpsimd", lambda e: e.collective_compute(
                    "ReduceScatter", ALU.add, replica_groups=groups,
                    ins=[dense.opt()], outs=[rs_out.opt()]),
                    waits=[prev_scat_end, ev_scat[(3, 6)]], inc=scc)

                if debug_taps:
                    tl.add("gpsimd", dma(dbg_dense, dense),
                           waits=[prev_scat_end, ev_scat[(3, 6)]],
                           inc=s_dbg, incn=16)

        esE.close()
        esT.close()
        esC = contextlib.ExitStack()
        rsb = esC.enter_context(nc.sbuf_tensor("rsb", [128, 2, D], BF16))
        spb = esC.enter_context(nc.sbuf_tensor("spb", [128, 2, D], F32))
        xpb = esC.enter_context(nc.sbuf_tensor("xpb", [128, 2, D], F32))
        fo = esC.enter_context(nc.sbuf_tensor("fo", [128, D], F32))
        ln4 = esC.enter_context(nc.sbuf_tensor("ln4", [128, D], F32))
        ob2 = esC.enter_context(nc.sbuf_tensor("ob2", [128, D], F32))
        cst = esC.enter_context(nc.sbuf_tensor("cst", [128, 8], F32))
        cjk = esC.enter_context(nc.sbuf_tensor("cjk", [128, D], F32))
        if True:
            rsv = rs_out.rearrange("(t p) d -> t p d", p=128)
            spv = spark.rearrange("(t p) d -> t p d", p=128)
            xpv = xpark.rearrange("(t p) d -> t p d", p=128)
            ouv = out.rearrange("(t p) d -> t p d", p=128)
            ev_done = {}
            ev_odma = None
            ev_n4_prev = None
            for j in range(NTR):
                w = [ev_rs] if j < 2 else [ev_done[j - 2]]
                ev_l1 = tl.add("sync", dma(rsb[:, j % 2, :], rsv[j]),
                               waits=w, inc=s_crs[j % 2], incn=16)
                ev_l2 = tl.add("sync", dma(spb[:, j % 2, :], spv[j]),
                               waits=[ev_spark_last] + w, inc=s_csp[j % 2], incn=16)
                ev_l3 = tl.add("sync", dma(xpb[:, j % 2, :], xpv[j]),
                               waits=[ev_xmdma[NT - 1], ev_xmdma[NT - 2]] + w,
                               inc=s_cxp[j % 2], incn=16)
                w = [ev_l1, ev_l2]
                if ev_n4_prev is not None:
                    w.append(ev_n4_prev)
                tl.add("vector", lambda e, j=j: e.tensor_add(
                    fo[:], rsb[:, j % 2, :], spb[:, j % 2, :]), waits=w)
                tl.add("vector", lambda e: e.reduce_sum(cst[:, 0:1], fo[:], AX.X),
                       nofence=True)
                tl.add("vector", lambda e: e.scalar_tensor_tensor(
                    cjk[:], fo[:], 1.0, fo[:], ALU.mult, ALU.mult,
                    accum_out=cst[:, 1:2]))
                tl.add("vector", lambda e: e.tensor_scalar(
                    cst[:, 2:3], cst[:, 0:1], -1.0 / D, None, ALU.mult))
                tl.add("vector", lambda e: e.tensor_mul(
                    cjk[:, 0:1], cst[:, 2:3], cst[:, 2:3]))
                tl.add("vector", lambda e: e.tensor_scalar(
                    cjk[:, 1:2], cst[:, 1:2], 1.0 / D, 1e-5, ALU.mult, ALU.add))
                ev_v4 = tl.add("vector", lambda e: e.tensor_sub(
                    cst[:, 3:4], cjk[:, 1:2], cjk[:, 0:1]), inc=sv)
                ev_s4 = tl.add("scalar", lambda e: e.activation(
                    cst[:, 4:5], cst[:, 3:4], AF.Sqrt), waits=[ev_v4], inc=ss)
                ev_r4 = tl.add("vector", lambda e: e.reciprocal(
                    cst[:, 5:6], cst[:, 4:5]), waits=[ev_s4], inc=sv)
                ev_n4 = tl.add("vector", lambda e: e.tensor_scalar(
                    ln4[:], fo[:], cst[:, 2:3], cst[:, 5:6], ALU.add, ALU.mult),
                    waits=[ev_r4], inc=sv)
                ev_n4_prev = ev_n4
                w = [ev_n4, ev_l3]
                if ev_odma is not None:
                    w.append(ev_odma)
                ev_ob = tl.add("vector", lambda e, j=j: e.tensor_add(
                    ob2[:], ln4[:], xpb[:, j % 2, :]), waits=w, inc=sv,
                    nofence=True)
                ev_odma = tl.add("gpsimd", lambda e, j=j: e.dma_start(
                    out=ouv[j], in_=ob2[:]), waits=[ev_ob], inc=s_out, incn=16)
                ev_done[j] = ev_odma

            tl.add("sync", lambda e: None, waits=[ev_odma])


    if not attn_only:
        esC.close()
    with nc.Block() as block:
        @block.sync
        def _(e):
            tl.emit(e, "sync")

        @block.tensor
        def _(e):
            tl.emit(e, "tensor")

        @block.vector
        def _(e):
            tl.emit(e, "vector")

        @block.scalar
        def _(e):
            tl.emit(e, "scalar")

        @block.gpsimd
        def _(e):
            tl.emit(e, "gpsimd")

    es.close()
    nc.finalize()
    return nc


def _chunked(wT):
    Dd, Ff = wT.shape
    return np.ascontiguousarray(wT.reshape(Dd // 128, 128, Ff).transpose(1, 0, 2))


def make_inputs(inputs):
    x = np.asarray(inputs["L0_attn_x"], np.float32)
    wf = np.asarray(inputs["L0_attn_w_forget"], np.float32)
    wi = np.asarray(inputs["L0_attn_w_input"], np.float32)
    wo = np.asarray(inputs["L0_attn_w_output"], np.float32)
    nw = np.asarray(inputs["L0_attn_norm_w"], np.float32)
    gw = np.asarray(inputs["L0_ffn_gate_w"], np.float32)
    Wg = np.asarray(inputs["L0_ffn_Wg"], np.float32)
    Wu = np.asarray(inputs["L0_ffn_Wu"], np.float32)
    Wd = np.asarray(inputs["L0_ffn_Wd"], np.float32)
    Wsg = np.asarray(inputs["L0_ffn_Ws_g"], np.float32)
    Wsu = np.asarray(inputs["L0_ffn_Ws_u"], np.float32)
    Wsd = np.asarray(inputs["L0_ffn_Ws_d"], np.float32)

    wfT = _chunked(np.ascontiguousarray(wf.T))
    wiT = _chunked(np.ascontiguousarray(wi.T))
    woT = _chunked(np.ascontiguousarray(wo.T))
    gateT = _chunked(np.ascontiguousarray(gw.T))
    wsgT = _chunked(np.ascontiguousarray(Wsg.T))
    wsuT = _chunked(np.ascontiguousarray(Wsu.T))
    wsdT = _chunked(np.ascontiguousarray(Wsd.T))

    identm = np.eye(128, dtype=np.float32)
    k = np.arange(128)
    ltrim = (k[:, None] < k[None, :]).astype(np.float32)
    onesc = np.ones((128, 1), np.float32)
    ones1 = np.ones((1, 128), np.float32)
    tokidm = (np.arange(128)[:, None] * 64 + np.arange(64)[None, :]).astype(np.float32)

    in_maps = []
    for c in range(NCORE):
        b, q = c // 4, c % 4
        lo = q * TLOC - WARM
        if lo < 0:
            xw = np.concatenate([np.zeros((WARM, D), np.float32),
                                 x[b, 0:TLOC]], axis=0)
        else:
            xw = x[b, lo:(q + 1) * TLOC]
        ex = [4 * q + t for t in range(4)]
        wgTs = np.stack([_chunked(np.ascontiguousarray(Wg[e].T)) for e in ex])
        wuTs = np.stack([_chunked(np.ascontiguousarray(Wu[e].T)) for e in ex])
        wdTs = np.stack([_chunked(np.ascontiguousarray(Wd[e].T)) for e in ex])
        oh = np.zeros((128, 4, E), np.float32)
        for t in range(4):
            oh[:, t, 4 * q + t] = 1.0
        in_maps.append({
            "xin": np.ascontiguousarray(xw),
            "wfT": wfT, "wiT": wiT, "woT": woT,
            "normw": nw.reshape(1, D),
            "gateT": gateT, "wsgT": wsgT, "wsuT": wsuT, "wsdT": wsdT,
            "wgT": wgTs, "wuT": wuTs, "wdT": wdTs,
            "ident": identm, "ltri": ltrim, "onesc": onesc, "ones1": ones1,
            "tokid": tokidm, "oh4": oh,
        })
    return in_maps


def kernel(**inputs):
    nc = build_program()
    in_maps = make_inputs(inputs)
    res = run_bass_kernel_spmd(nc, in_maps, core_ids=list(range(NCORE)))
    parts = [res.results[c]["out"] for c in range(NCORE)]
    full = np.stack([np.concatenate(parts[0:4], axis=0),
                     np.concatenate(parts[4:8], axis=0)])
    return full.astype(np.float32)


# version 09
# speedup vs baseline: 1.1962x; 1.0816x over previous
import numpy as np
import concourse.bass as bass
import concourse.mybir as mybir
from concourse.bass_utils import run_bass_kernel_spmd
from concourse.bass_core_rust import IndirectOffsetOnAxis

F32 = mybir.dt.float32
BF16 = mybir.dt.bfloat16
I32 = mybir.dt.int32
AF = mybir.ActivationFunctionType
ALU = mybir.AluOpType
AX = mybir.AxisListType

B, T, D = 2, 8192, 1024
H, DH = 8, 128
E, CAP, FE, FS = 16, 1024, 2048, 2048
NCORE = 8
TLOC = 2048
WARM = 512
NTW = WARM // 128
NTR = TLOC // 128
NT = NTW + NTR
KC = D // 128
MF = FE // 128
MS = FS // 128
LN9 = 2.1972245773362196
BIG = 1.0e9
BISECT_ITERS = 42

ENGINES = ("sync", "tensor", "vector", "scalar", "gpsimd")


class TL:
    def __init__(self):
        self.ops = {e: [] for e in ENGINES}
        self.cnt = {}
        self.autofence = {}
        self.last = {}

    def add(self, eng, fn, waits=(), inc=None, incn=1, nofence=False):
        ws = [w for w in waits if w is not None]
        fence_sem = self.autofence.get(eng) if not nofence else None
        if fence_sem is not None:
            last = self.last.get(eng)
            if last is not None:
                ws.append(last)
            if inc is None:
                inc = fence_sem
        m = None
        if inc is not None:
            c = self.cnt.get(id(inc), 0) + incn
            self.cnt[id(inc)] = c
            m = (inc, c)
        if fence_sem is not None:
            self.last[eng] = m
        self.ops[eng].append((fn, ws, inc, incn))
        return m

    def cur(self, sem):
        return (sem, self.cnt.get(id(sem), 0))

    def emit(self, eng_obj, name):
        for fn, waits, inc, incn in self.ops[name]:
            best, order = {}, []
            for (sem, c) in waits:
                if id(sem) not in best:
                    order.append(sem)
                best[id(sem)] = max(best.get(id(sem), -1), c)
            for sem in order:
                eng_obj.wait_ge(sem, best[id(sem)])
            inst = fn(eng_obj)
            if inc is not None:
                if inst is None:
                    inst = eng_obj.nop()
                inst.then_inc(inc, incn)


def build_program(debug_taps=False, attn_only=False, race_detect=True):
    nc = bass.Bass("TRN2", target_bir_lowering=False,
                   detect_race_conditions=race_detect)

    xin = nc.dram_tensor("xin", [NT * 128, D], F32, kind="ExternalInput").ap()
    wfT = nc.dram_tensor("wfT", [128, KC, D], F32, kind="ExternalInput").ap()
    wiT = nc.dram_tensor("wiT", [128, KC, D], F32, kind="ExternalInput").ap()
    woT = nc.dram_tensor("woT", [128, KC, D], F32, kind="ExternalInput").ap()
    normw = nc.dram_tensor("normw", [1, D], F32, kind="ExternalInput").ap()
    gateT = nc.dram_tensor("gateT", [128, KC, E], F32, kind="ExternalInput").ap()
    wsgT = nc.dram_tensor("wsgT", [128, KC, FS], F32, kind="ExternalInput").ap()
    wsuT = nc.dram_tensor("wsuT", [128, KC, FS], F32, kind="ExternalInput").ap()
    wsdT = nc.dram_tensor("wsdT", [128, MS, D], F32, kind="ExternalInput").ap()
    wgT = nc.dram_tensor("wgT", [4, 128, KC, FE], F32, kind="ExternalInput").ap()
    wuT = nc.dram_tensor("wuT", [4, 128, KC, FE], F32, kind="ExternalInput").ap()
    wdT = nc.dram_tensor("wdT", [4, 128, MF, D], F32, kind="ExternalInput").ap()
    ident = nc.dram_tensor("ident", [128, 128], F32, kind="ExternalInput").ap()
    ltri = nc.dram_tensor("ltri", [128, 128], F32, kind="ExternalInput").ap()
    onesc_i = nc.dram_tensor("onesc", [128, 1], F32, kind="ExternalInput").ap()
    ones1_i = nc.dram_tensor("ones1", [1, 128], F32, kind="ExternalInput").ap()
    tokid = nc.dram_tensor("tokid", [128, 64], F32, kind="ExternalInput").ap()
    oh4 = nc.dram_tensor("oh4", [128, 4, E], F32, kind="ExternalInput").ap()
    out = nc.dram_tensor("out", [TLOC, D], F32, kind="ExternalOutput").ap()
    if debug_taps:
        dbg_xmid = nc.dram_tensor("dbg_xmid", [TLOC, D], F32, kind="ExternalOutput").ap()
        dbg_hid = nc.dram_tensor("dbg_hid", [TLOC, D], F32, kind="ExternalOutput").ap()
        dbg_sco = nc.dram_tensor("dbg_sco", [T, E], F32, kind="ExternalOutput").ap()
        dbg_idx = nc.dram_tensor("dbg_idx", [4, CAP], F32, kind="ExternalOutput").ap()
        dbg_dense = nc.dram_tensor("dbg_dense", [T, D], F32, kind="ExternalOutput").ap()
        dbg2 = nc.dram_tensor("dbg2", [12, 128, D], F32, kind="ExternalOutput").ap()
        dbg3 = nc.dram_tensor("dbg3", [6, 128, D], F32, kind="ExternalOutput").ap()

    wsg_b = nc.dram_tensor("wsg_b", [128, KC, FS], BF16).ap()
    wsu_b = nc.dram_tensor("wsu_b", [128, KC, FS], BF16).ap()
    wsd_b = nc.dram_tensor("wsd_b", [128, MS, D], BF16).ap()
    wg_b = nc.dram_tensor("wg_b", [4, 128, KC, FE], BF16).ap()
    wu_b = nc.dram_tensor("wu_b", [4, 128, KC, FE], BF16).ap()
    wd_b = nc.dram_tensor("wd_b", [4, 128, MF, D], BF16).ap()
    hid_loc = nc.dram_tensor("hid_loc", [TLOC, D], BF16).ap()
    hid_full = nc.dram_tensor("hid_full", [T, D], BF16).ap()
    sco_loc = nc.dram_tensor("sco_loc", [TLOC, E], F32).ap()
    sco_full = nc.dram_tensor("sco_full", [T, E], F32).ap()
    dense = nc.dram_tensor("dense", [T, D], BF16).ap()
    rs_out = nc.dram_tensor("rs_out", [TLOC, D], BF16).ap()
    xpark = nc.dram_tensor("xpark", [TLOC, D], F32).ap()
    spark = nc.dram_tensor("spark", [TLOC, D], F32).ap()
    idxl_d = [nc.dram_tensor(f"idxl_d{ee}", [CAP, 1], F32).ap()
              for ee in range(4)]

    xv = xin.rearrange("(t p) d -> t p d", p=128)
    tl = TL()
    groups = [[0, 1, 2, 3], [4, 5, 6, 7]]

    import contextlib
    es = contextlib.ExitStack()

    def mksem(name):
        return es.enter_context(nc.semaphore(name))
    scc = mksem("scc")
    sv = mksem("sv")
    ss = mksem("ss")
    st_ = mksem("st")
    s_const = mksem("s_const")
    s_dense = mksem("s_dense")
    s_x = [mksem(f"s_x{k}") for k in range(3)]
    s_wsh = mksem("s_wsh")
    s_misc = mksem("s_misc")
    s_wg = [mksem(f"s_wg{k}") for k in range(2)]
    s_wu = [mksem(f"s_wu{k}") for k in range(2)]
    s_wd = mksem("s_wd")
    s_crs = [mksem(f"s_crs{k}") for k in range(2)]
    s_csp = [mksem(f"s_csp{k}") for k in range(2)]
    s_cxp = [mksem(f"s_cxp{k}") for k in range(2)]
    s_xp = [mksem(f"s_xp{k}") for k in range(2)]
    s_hd = [mksem(f"s_hd{k}") for k in range(2)]
    s_sc = mksem("s_sc")
    s_sp = mksem("s_sp")
    s_identb = mksem("s_identb")
    s_attnw = mksem("s_attnw")
    s_cast = mksem("s_cast")
    s_isc = mksem("s_isc")
    s_tw = mksem("s_tw")
    s_xg = mksem("s_xg")
    s_eo = [mksem(f"s_eo{k}") for k in range(2)]
    s_out = mksem("s_out")
    s_dbg = mksem("s_dbg")
    tl.autofence = {"vector": sv}

    P1 = es.enter_context(nc.psum_tensor("P1", [128, 1024], F32))
    P2 = es.enter_context(nc.psum_tensor("P2", [128, 1024], F32))
    P3 = es.enter_context(nc.psum_tensor("P3", [128, 1024], F32))
    PB = es.enter_context(nc.psum_tensor("PB", [128, 1024], BF16))
    P4 = es.enter_context(nc.psum_tensor("P4", [128, 512], F32))

    identb32 = es.enter_context(nc.sbuf_tensor("identb32", [128, 128], F32))
    identbb = es.enter_context(nc.sbuf_tensor("identbb", [128, 128], BF16))
    ltrib = es.enter_context(nc.sbuf_tensor("ltrib", [128, 128], F32))
    onescb = es.enter_context(nc.sbuf_tensor("onescb", [128, 1], F32))
    ones1b = es.enter_context(nc.sbuf_tensor("ones1b", [1, 128], F32))
    tokidb = es.enter_context(nc.sbuf_tensor("tokidb", [128, 64], F32))
    oh4b = es.enter_context(nc.sbuf_tensor("oh4b", [128, 4, E], F32))
    zzz = es.enter_context(nc.sbuf_tensor("zzz", [128, 64], F32))
    bigt = es.enter_context(nc.sbuf_tensor("bigt", [128, 64], F32))
    idx8 = es.enter_context(nc.sbuf_tensor("idx8", [8, 128], F32))
    idxi = es.enter_context(nc.sbuf_tensor("idxi", [128, 2, 8], I32))
    tw16 = es.enter_context(nc.sbuf_tensor("tw16", [128, E], F32))
    topw = es.enter_context(nc.sbuf_tensor("topw", [128, 2, 8], F32))
    odsti = es.enter_context(nc.sbuf_tensor("odsti", [128, 64], I32))
    xtok = es.enter_context(nc.sbuf_tensor("xtok", [128, 8, D], BF16))
    eo = es.enter_context(nc.sbuf_tensor("eo", [128, 2, D], BF16))
    zzzb = es.enter_context(nc.sbuf_tensor("zzzb", [128, 1024], BF16))
    gatb = es.enter_context(nc.sbuf_tensor("gatb", [128, KC, E], BF16))
    sco_all = es.enter_context(nc.sbuf_tensor("sco_all", [128, NTR, E], F32))
    gj = es.enter_context(nc.sbuf_tensor("gj", [128, 8], F32))
    junk = es.enter_context(nc.sbuf_tensor("junk", [128, D], F32))
    _hidT_guard = nc.sbuf_tensor("hidT", [128, KC, TLOC], BF16)
    hidT = _hidT_guard.__enter__()

    def dma(dst, src):
        return lambda e: e.dma_start(out=dst, in_=src)

    regs = {}
    def mk_regs(e):
        regs['cap'] = e.to_reg(CAP - 1)
        return None
    tl.add('gpsimd', mk_regs)

    ev_const = None
    for dst, src in ((identb32[:], ident), (ltrib[:], ltri), (onescb[:], onesc_i),
                     (ones1b[:], ones1_i), (tokidb[:], tokid), (oh4b[:], oh4)):
        ev_const = tl.add("sync", dma(dst, src), inc=s_const, incn=16)
    ev_identb = tl.add("gpsimd", dma(identbb[:], ident), inc=s_identb, incn=16)
    ev_zzz = tl.add("vector", lambda e: e.memset(zzz[:], 0.0), inc=sv)
    ev_zzzb = tl.add("vector", lambda e: e.memset(zzzb[:], 0.0), inc=sv)
    ev_big = tl.add("vector", lambda e: e.memset(bigt[:], BIG), inc=sv)



    esA = contextlib.ExitStack()
    wfb = esA.enter_context(nc.sbuf_tensor("wfb", [128, KC, D], BF16))
    wib = esA.enter_context(nc.sbuf_tensor("wib", [128, KC, D], BF16))
    wob = esA.enter_context(nc.sbuf_tensor("wob", [128, KC, D], BF16))
    normwb = esA.enter_context(nc.sbuf_tensor("normwb", [128, D], F32))
    nw1 = esA.enter_context(nc.sbuf_tensor("nw1", [1, D], F32))
    xb = esA.enter_context(nc.sbuf_tensor("xb", [128, 3, D], F32))
    sta = esA.enter_context(nc.sbuf_tensor("sta", [128, 2, 16], F32))
    st2 = esA.enter_context(nc.sbuf_tensor("st2", [128, 2, 16], F32))
    ln1b = esA.enter_context(nc.sbuf_tensor("ln1b", [128, 1, D], F32))
    xnb = esA.enter_context(nc.sbuf_tensor("xnb", [128, D], BF16))
    xnT = esA.enter_context(nc.sbuf_tensor("xnT", [128, KC, 128], BF16))
    fb = esA.enter_context(nc.sbuf_tensor("fb", [128, D], F32))
    ob = esA.enter_context(nc.sbuf_tensor("ob", [128, D], F32))
    vs = esA.enter_context(nc.sbuf_tensor("vs", [128, H], F32))
    ub = esA.enter_context(nc.sbuf_tensor("ub", [128, D], F32))
    fTs = esA.enter_context(nc.sbuf_tensor("fTs", [128, D], F32))
    sTb = esA.enter_context(nc.sbuf_tensor("sTb", [128, 2, D], F32))
    yb = esA.enter_context(nc.sbuf_tensor("yb", [128, D], F32))
    retb = esA.enter_context(nc.sbuf_tensor("retb", [128, 2, D], F32))
    ln2b = esA.enter_context(nc.sbuf_tensor("ln2b", [128, D], F32))
    xmb = esA.enter_context(nc.sbuf_tensor("xmb", [128, 2, D], F32))
    hbb = esA.enter_context(nc.sbuf_tensor("hbb", [128, 2, D], BF16))
    if True:
        tl.add("gpsimd", dma(wfb[:], wfT), inc=s_attnw, incn=16)
        tl.add("gpsimd", dma(wib[:], wiT), inc=s_attnw, incn=16)
        tl.add("gpsimd", dma(wob[:], woT), inc=s_attnw, incn=16)
        ev_attnw = tl.add("gpsimd", dma(gatb[:], gateT), inc=s_attnw, incn=16)
        ev_wf = ev_wi = ev_wo = ev_gat = ev_attnw
        ev_wcast = {}
        for name_, dst_, src_ in (("wsg", wsg_b, wsgT), ("wsu", wsu_b, wsuT),
                                  ("wsd", wsd_b, wsdT)):
            ev_wcast[name_] = tl.add("gpsimd", dma(dst_, src_), inc=s_cast, incn=16)
        for ee_ in range(4):
            ev_wcast[f"wg{ee_}"] = tl.add("gpsimd", dma(wg_b[ee_], wgT[ee_]),
                                          inc=s_cast, incn=16)
            ev_wcast[f"wu{ee_}"] = tl.add("gpsimd", dma(wu_b[ee_], wuT[ee_]),
                                          inc=s_cast, incn=16)
            ev_wcast[f"wd{ee_}"] = tl.add("gpsimd", dma(wd_b[ee_], wdT[ee_]),
                                          inc=s_cast, incn=16)
        ev_cast_all = ev_wcast["wd3"]
        for k_ in ev_wcast:
            ev_wcast[k_] = ev_cast_all
        ev_nw = tl.add("sync", dma(nw1[:], normw), inc=s_const, incn=16)
        ev_const = ev_nw

        def mk_nwb(e):
            e.matmul(P1[:, 0:512], ones1b[:], nw1[:, 0:512], start=True, stop=True)
            return e.matmul(P1[:, 512:1024], ones1b[:], nw1[:, 512:1024],
                            start=True, stop=True)
        ev_nwmm = tl.add("tensor", mk_nwb, waits=[ev_nw, ev_const], inc=st_)
        ev_nwb = tl.add("vector", lambda e: e.tensor_copy(normwb[:], P1[:]),
                        waits=[ev_nwmm], inc=sv)

        evx = [None] * NT
        ev_stats = [None] * NT
        ev_sq = [None] * NT
        ev_recip = [None] * NT
        ev_xn = [None] * NT
        ev_xnTpe = [None] * NT
        ev_exnT = [None] * NT
        ev_zf = [None] * NT
        ev_zv = [None] * NT
        ev_zo = [None] * NT
        ev_zfm = [None] * NT
        ev_f = [None] * NT
        ev_o = [None] * NT
        ev_vs = [None] * NT
        ev_u = [None] * NT
        ev_fTpe = [None] * NT
        ev_uTpe = [None] * NT
        ev_efT = [None] * NT
        ev_scan = [None] * NT
        ev_stbpe = [None] * NT
        ev_y = [None] * NT
        ev_ret = [None] * NT
        ev_st2 = [None] * NT
        ev_sq2 = [None] * NT
        ev_r2 = [None] * NT
        ev_ln2 = [None] * NT
        ev_xm = [None] * NT
        ev_st3 = [None] * NT
        ev_sq3 = [None] * NT
        ev_hid = [None] * NT
        ev_hTpe = [None] * NT
        ev_ehT = [None] * NT
        ev_xmdma = [None] * NT
        ev_hdma = [None] * NT

        def stageA(i):
            real = i >= NTW
            waits = [evx[i - 3], ev_xm[i - 3] if i - 3 >= NTW else ev_xn[i - 3]] if i >= 3 else []
            evx[i] = tl.add("sync", dma(xb[:, i % 3, :], xv[i]), waits=waits,
                            inc=s_x[i % 3], incn=16)
            tl.add("vector", lambda e, i=i: e.reduce_sum(
                sta[:, i % 2, 0:1], xb[:, i % 3, :], AX.X), waits=[evx[i]],
                nofence=True)
            tl.add("vector", lambda e, i=i: e.scalar_tensor_tensor(
                junk[:], xb[:, i % 3, :], 1.0, xb[:, i % 3, :], ALU.mult, ALU.mult,
                accum_out=sta[:, i % 2, 1:2]), waits=[evx[i]])
            tl.add("vector", lambda e, i=i: e.tensor_scalar(
                sta[:, i % 2, 2:3], sta[:, i % 2, 0:1], -1.0 / D, None, ALU.mult))
            tl.add("vector", lambda e, i=i: e.tensor_scalar(
                sta[:, i % 2, 3:4], sta[:, i % 2, 1:2], 1.0 / D, None, ALU.mult))
            tl.add("vector", lambda e, i=i: e.tensor_mul(
                sta[:, i % 2, 4:5], sta[:, i % 2, 2:3], sta[:, i % 2, 2:3]))
            tl.add("vector", lambda e, i=i: e.tensor_sub(
                sta[:, i % 2, 5:6], sta[:, i % 2, 3:4], sta[:, i % 2, 4:5]))
            if real:
                tl.add("vector", lambda e, i=i: e.tensor_scalar(
                    sta[:, i % 2, 6:7], sta[:, i % 2, 5:6], 1.0, 1e-5,
                    ALU.mult, ALU.add))
            ev_stats[i] = tl.add("vector", lambda e, i=i: e.tensor_scalar(
                sta[:, i % 2, 7:8], sta[:, i % 2, 5:6], 1.0 + 1e-6, 1e-11,
                ALU.mult, ALU.add), inc=sv)

            w = [ev_stats[i]]
            if real:
                tl.add("scalar", lambda e, i=i: e.activation(
                    sta[:, i % 2, 8:9], sta[:, i % 2, 6:7], AF.Sqrt), waits=w)
                w = []
            ev_sq[i] = tl.add("scalar", lambda e, i=i: e.activation(
                sta[:, i % 2, 9:10], sta[:, i % 2, 7:8], AF.Sqrt), waits=w, inc=ss)
            if i - 1 >= NTW and ev_st2[i - 1] is not None:
                ev_sq2[i - 1] = tl.add("scalar", lambda e, i=i: e.activation(
                    st2[:, (i - 1) % 2, 4:5], st2[:, (i - 1) % 2, 3:4], AF.Sqrt),
                    waits=[ev_st2[i - 1]], inc=ss)
            if i - 2 >= NTW and ev_st3[i - 2] is not None:
                ev_sq3[i - 2] = tl.add("scalar", lambda e, i=i: e.activation(
                    st2[:, (i - 2) % 2, 12:13], st2[:, (i - 2) % 2, 11:12], AF.Sqrt),
                    waits=[ev_st3[i - 2]], inc=ss)

            w = [ev_sq[i]]
            if real:
                tl.add("vector", lambda e, i=i: e.reciprocal(
                    sta[:, i % 2, 10:11], sta[:, i % 2, 8:9]), waits=w)
                w = []
            ev_recip[i] = tl.add("vector", lambda e, i=i: e.reciprocal(
                sta[:, i % 2, 11:12], sta[:, i % 2, 9:10]), waits=w, inc=sv)
            if real:
                w = [ev_recip[i]]
                if i - 1 >= NTW:
                    w.append(ev_ret[i - 1])
                tl.add("vector", lambda e, i=i: e.tensor_scalar(
                    ln1b[:, 0, :], xb[:, i % 3, :], sta[:, i % 2, 2:3],
                    sta[:, i % 2, 10:11], ALU.add, ALU.mult), waits=w)
            tl.add("vector", lambda e, i=i: e.tensor_scalar(
                ln2b[:], xb[:, i % 3, :], sta[:, i % 2, 2:3], sta[:, i % 2, 11:12],
                ALU.add, ALU.mult), waits=[ev_recip[i]])
            w = [ev_nwb, ev_identb]
            if i >= 1:
                w.append(ev_xnTpe[i - 1])
            ev_xn[i] = tl.add("vector", lambda e: e.tensor_mul(
                xnb[:], ln2b[:], normwb[:]), waits=w, inc=sv, nofence=True)

            w = [ev_xn[i]]
            if i >= 1:
                w.append(ev_exnT[i - 1])
            if i - 3 >= NTW and ev_ehT[i - 3] is not None:
                w.append(ev_ehT[i - 3])

            def mk_xnT(e):
                r = None
                for g in range(KC):
                    r = e.transpose(PB[:, g * 128:(g + 1) * 128],
                                    xnb[:, g * 128:(g + 1) * 128], identbb[:])
                return r
            ev_xnTpe[i] = tl.add("tensor", mk_xnT, waits=w, inc=st_)
            ev_exnT[i] = tl.add("vector", lambda e: e.tensor_copy(
                xnT[:].rearrange("p k n -> p (k n)"), PB[:]),
                waits=[ev_xnTpe[i]], inc=sv, nofence=True)

            def mk_proj(dst, wsb):
                def f(e):
                    r = None
                    for k in range(KC):
                        for n in range(2):
                            r = e.matmul(dst[:, n * 512:(n + 1) * 512],
                                         xnT[:, k, :],
                                         wsb[:, k, n * 512:(n + 1) * 512],
                                         start=(k == 0), stop=(k == KC - 1))
                    return r
                return f
            w = [ev_exnT[i], ev_wf]
            if i >= 1:
                w.append(ev_efT[i - 1])
            ev_zf[i] = tl.add("tensor", mk_proj(P1, wfb), waits=w, inc=st_)
            w = [ev_exnT[i], ev_wi]
            if i >= 1:
                w.append(ev_scan[i - 1])
            ev_zv[i] = tl.add("tensor", mk_proj(P2, wib), waits=w, inc=st_)
            if real:
                w = [ev_exnT[i], ev_wo]
                if i - 1 >= NTW:
                    w.append(ev_y[i - 1])
                ev_zo[i] = tl.add("tensor", mk_proj(P3, wob), waits=w, inc=st_)

            ev_zfm[i] = tl.add("vector", lambda e: e.tensor_scalar(
                junk[:], P1[:], LN9, None, ALU.max), waits=[ev_zf[i]], inc=sv,
                nofence=True)
            ev_f[i] = tl.add("scalar", lambda e: e.activation(
                fb[:], junk[:], AF.Sigmoid), waits=[ev_zfm[i]], inc=ss)
            if real:
                ev_o[i] = tl.add("scalar", lambda e: e.activation(
                    ob[:], P3[:], AF.Sigmoid), waits=[ev_zo[i]], inc=ss)
            ev_vs[i] = tl.add("vector", lambda e: e.tensor_reduce(
                vs[:], P2[:].rearrange("p (h d) -> p h d", h=H), AX.X, ALU.add),
                waits=[ev_zv[i]], inc=sv, nofence=True)
            tl.add("vector", lambda e: e.tensor_scalar(
                junk[:], fb[:], -1.0, 1.0, ALU.mult, ALU.add), waits=[ev_f[i]],
                nofence=True)
            ev_u[i] = tl.add("vector", lambda e: e.tensor_tensor(
                ub[:].rearrange("p (h d) -> p h d", h=H),
                junk[:].rearrange("p (h d) -> p h d", h=H),
                vs[:].to_broadcast([128, H, DH]), ALU.mult), inc=sv)

            def mk_tr(dst_ps, src):
                def f(e):
                    r = None
                    for g in range(KC):
                        r = e.transpose(dst_ps[:, g * 128:(g + 1) * 128],
                                        src[:, g * 128:(g + 1) * 128], identb32[:])
                    return r
                return f
            ev_fTpe[i] = tl.add("tensor", mk_tr(P1, fb[:]),
                                waits=[ev_f[i], ev_zfm[i]], inc=st_)
            ev_uTpe[i] = tl.add("tensor", mk_tr(P2, ub[:]),
                                waits=[ev_u[i], ev_vs[i]], inc=st_)
            ev_efT[i] = tl.add("vector", lambda e: e.tensor_copy(fTs[:], P1[:]),
                               waits=[ev_fTpe[i]], inc=sv, nofence=True)
            ev_euT = tl.add("vector", lambda e: e.tensor_copy(ub[:], P2[:]),
                            waits=[ev_uTpe[i]], inc=sv, nofence=True)

            w = [ev_efT[i], ev_euT]
            if i >= 1:
                w.append(ev_scan[i - 1])
            if i - 2 >= NTW:
                w.append(ev_stbpe[i - 2])

            def mk_scans(e, i=i):
                r = None
                for g in range(KC):
                    init = 0.0 if i == 0 else sTb[:, (i - 1) % 2,
                                                  g * 128 + 127:g * 128 + 128]
                    r = e.tensor_tensor_scan(
                        sTb[:, i % 2, g * 128:(g + 1) * 128],
                        fTs[:, g * 128:(g + 1) * 128],
                        ub[:, g * 128:(g + 1) * 128],
                        init, ALU.mult, ALU.add)
                return r
            ev_scan[i] = tl.add("vector", mk_scans, waits=w, inc=sv)
            if debug_taps and i == 0:
                tl.add("gpsimd", dma(dbg3[0], xb[:, 0, :]), waits=[ev_scan[0]],
                       inc=s_dbg, incn=16)
                tl.add("gpsimd", dma(dbg3[1], ub[:]), waits=[ev_scan[0]],
                       inc=s_dbg, incn=16)
                tl.add("gpsimd", dma(dbg3[2], sTb[:, 0, :]), waits=[ev_scan[0]],
                       inc=s_dbg, incn=16)
                tl.add("gpsimd", dma(dbg3[3], ln2b[:]), waits=[ev_scan[0]],
                       inc=s_dbg, incn=16)
                def _cast_xn(e):
                    return e.dma_start(out=dbg3[4], in_=xnb[:])
                tl.add("gpsimd", _cast_xn, waits=[ev_scan[0]],
                       inc=s_dbg, incn=16)
                tl.add("gpsimd", dma(dbg3[5][:, 0:12], sta[:, 0, 0:12]),
                       waits=[ev_scan[0]], inc=s_dbg, incn=16)

            if not real:
                return
            ev_stbpe[i] = tl.add("tensor", mk_tr(P3, sTb[:, i % 2, :]),
                                 waits=[ev_scan[i], ev_o[i]], inc=st_)
            ev_y[i] = tl.add("vector", lambda e: e.tensor_mul(yb[:], ob[:], P3[:]),
                             waits=[ev_stbpe[i]], inc=sv, nofence=True)
            w = []
            if i - 2 >= NTW:
                w.append(ev_ln2[i - 2])
            ev_ret[i] = tl.add("vector", lambda e, i=i: e.tensor_add(
                retb[:, i % 2, :], ln1b[:, 0, :], yb[:]), waits=w, inc=sv,
                nofence=True)
            w = []
            if i - 2 >= NTW and ev_hid[i - 2] is not None:
                w.append(ev_hid[i - 2])
            tl.add("vector", lambda e, i=i: e.reduce_sum(
                st2[:, i % 2, 0:1], retb[:, i % 2, :], AX.X), waits=w,
                nofence=True)
            tl.add("vector", lambda e, i=i: e.scalar_tensor_tensor(
                junk[:], retb[:, i % 2, :], 1.0, retb[:, i % 2, :], ALU.mult,
                ALU.mult, accum_out=st2[:, i % 2, 1:2]))
            tl.add("vector", lambda e, i=i: e.tensor_scalar(
                st2[:, i % 2, 2:3], st2[:, i % 2, 0:1], -1.0 / D, None, ALU.mult))
            tl.add("vector", lambda e, i=i: e.tensor_mul(
                junk[:, 0:1], st2[:, i % 2, 2:3], st2[:, i % 2, 2:3]))
            tl.add("vector", lambda e, i=i: e.tensor_scalar(
                junk[:, 1:2], st2[:, i % 2, 1:2], 1.0 / D, 1e-5, ALU.mult, ALU.add))
            ev_st2[i] = tl.add("vector", lambda e, i=i: e.tensor_sub(
                st2[:, i % 2, 3:4], junk[:, 1:2], junk[:, 0:1]), inc=sv)

        def stageB(i):
            ev_r2[i] = tl.add("vector", lambda e, i=i: e.reciprocal(
                st2[:, i % 2, 5:6], st2[:, i % 2, 4:5]), waits=[ev_sq2[i]], inc=sv)
            ev_ln2[i] = tl.add("vector", lambda e, i=i: e.tensor_scalar(
                ln2b[:], retb[:, i % 2, :], st2[:, i % 2, 2:3],
                st2[:, i % 2, 5:6], ALU.add, ALU.mult),
                waits=[ev_r2[i]], inc=sv)
            w = [ev_ln2[i]]
            if i - 2 >= NTW:
                w.append(ev_xmdma[i - 2])
            ev_xm[i] = tl.add("vector", lambda e, i=i: e.tensor_add(
                xmb[:, i % 2, :], ln2b[:], xb[:, i % 3, :]), waits=w, inc=sv,
                nofence=True)
            ev_xmdma[i] = tl.add("scalar", lambda e, i=i: e.dma_start(
                out=xpark.rearrange("(t p) d -> t p d", p=128)[i - NTW],
                in_=xmb[:, i % 2, :]), waits=[ev_xm[i]], inc=s_xp[i % 2], incn=16)
            tl.add("vector", lambda e, i=i: e.reduce_sum(
                st2[:, i % 2, 8:9], xmb[:, i % 2, :], AX.X), nofence=True)
            tl.add("vector", lambda e, i=i: e.scalar_tensor_tensor(
                junk[:], xmb[:, i % 2, :], 1.0, xmb[:, i % 2, :], ALU.mult,
                ALU.mult, accum_out=st2[:, i % 2, 9:10]))
            tl.add("vector", lambda e, i=i: e.tensor_scalar(
                st2[:, i % 2, 10:11], st2[:, i % 2, 8:9], -1.0 / D, None, ALU.mult))
            tl.add("vector", lambda e, i=i: e.tensor_mul(
                junk[:, 0:1], st2[:, i % 2, 10:11], st2[:, i % 2, 10:11]))
            tl.add("vector", lambda e, i=i: e.tensor_scalar(
                junk[:, 1:2], st2[:, i % 2, 9:10], 1.0 / D, 1e-5, ALU.mult, ALU.add))
            ev_st3[i] = tl.add("vector", lambda e, i=i: e.tensor_sub(
                st2[:, i % 2, 11:12], junk[:, 1:2], junk[:, 0:1]), inc=sv)

        def stageC(i):
            ev_r3 = tl.add("vector", lambda e, i=i: e.reciprocal(
                st2[:, i % 2, 13:14], st2[:, i % 2, 12:13]), waits=[ev_sq3[i]],
                inc=sv)
            w = [ev_r3]
            if i - 2 >= NTW and ev_hTpe[i - 2] is not None:
                w.append(ev_hTpe[i - 2])
                w.append(ev_hdma[i - 2])
            ev_hid[i] = tl.add("vector", lambda e, i=i: e.tensor_scalar(
                hbb[:, i % 2, :], xmb[:, i % 2, :], st2[:, i % 2, 10:11],
                st2[:, i % 2, 13:14], ALU.add, ALU.mult),
                waits=w, inc=sv)
            ev_hdma[i] = tl.add("scalar", lambda e, i=i: e.dma_start(
                out=hid_loc.rearrange("(t p) d -> t p d", p=128)[i - NTW],
                in_=hbb[:, i % 2, :]), waits=[ev_hid[i]], inc=s_hd[i % 2], incn=16)
            w = [ev_hid[i]]
            tslot = min(i + 2, NT - 1)
            w.append(ev_exnT[tslot])
            if i - 1 >= NTW and ev_ehT[i - 1] is not None:
                w.append(ev_ehT[i - 1])

            def mk_hT(e, i=i):
                r = None
                for g in range(KC):
                    r = e.transpose(PB[:, g * 128:(g + 1) * 128],
                                    hbb[:, i % 2, g * 128:(g + 1) * 128], identbb[:])
                return r
            ev_hTpe[i] = tl.add("tensor", mk_hT, waits=w, inc=st_)
            j = i - NTW
            ev_ehT[i] = tl.add("vector", lambda e, j=j: e.tensor_copy(
                hidT[:, :, j * 128:(j + 1) * 128],
                PB[:].rearrange("p (k n) -> p k n", k=KC)),
                waits=[ev_hTpe[i]], inc=sv, nofence=True)

        for i in range(NT):
            stageA(i)
            if i - 1 >= NTW:
                stageB(i - 1)
            if i - 2 >= NTW:
                stageC(i - 2)
        i = NT - 1
        ev_sq2[i] = tl.add("scalar", lambda e, i=i: e.activation(
            st2[:, i % 2, 4:5], st2[:, i % 2, 3:4], AF.Sqrt),
            waits=[ev_st2[i]], inc=ss)
        stageB(NT - 1)
        for i in (NT - 2, NT - 1):
            ev_sq3[i] = tl.add("scalar", lambda e, i=i: e.activation(
                st2[:, i % 2, 12:13], st2[:, i % 2, 11:12], AF.Sqrt),
                waits=[ev_st3[i]], inc=ss)
            stageC(i)

        dnv = dense.rearrange("(n p) d -> n p d", p=128)
        ev_dense0 = None
        for n_ in range(T // 128):
            ev_dense0 = tl.add("scalar", dma(dnv[n_], zzzb[:]), waits=[ev_zzzb],
                               inc=s_dense, incn=16)

        if True:
            ev_gmm = [None] * NTR
            ev_sco = [None] * NTR
            for j in range(NTR):
                i = j + NTW

                def mk_gate(e, j=j):
                    r = None
                    for k in range(KC):
                        r = e.matmul(P4[:, 0:E], hidT[:, k, j * 128:(j + 1) * 128],
                                     gatb[:, k, :], start=(k == 0),
                                     stop=(k == KC - 1))
                    return r
                w = [ev_ehT[i], ev_gat]
                if j >= 1:
                    w.append(ev_sco[j - 1])
                ev_gmm[j] = tl.add("tensor", mk_gate, waits=w, inc=st_)
                tl.add("vector", lambda e: e.reduce_max(
                    gj[:, 2:3], P4[:, 0:E], AX.X), waits=[ev_gmm[j]])
                ev_mx = tl.add("vector", lambda e: e.tensor_scalar(
                    gj[:, 3:4], gj[:, 2:3], -1.0, None, ALU.mult), inc=sv)
                ev_ex = tl.add("scalar", lambda e, j=j: e.activation(
                    sco_all[:, j, :], P4[:, 0:E], AF.Exp, bias=gj[:, 3:4]),
                    waits=[ev_mx], inc=ss)
                tl.add("vector", lambda e, j=j: e.reduce_sum(
                    gj[:, 4:5], sco_all[:, j, :], AX.X), waits=[ev_ex])
                ev_rc = tl.add("vector", lambda e: e.reciprocal(
                    gj[:, 5:6], gj[:, 4:5]), inc=sv)
                ev_sco[j] = tl.add("vector", lambda e, j=j: e.tensor_scalar(
                    sco_all[:, j, :], sco_all[:, j, :], gj[:, 5:6], None,
                    ALU.mult), waits=[ev_rc], inc=sv)
            ev_scodma = tl.add("scalar", lambda e: e.dma_start(
                out=sco_loc.rearrange("(j p) e -> p j e", p=128), in_=sco_all[:]),
                waits=[ev_sco[NTR - 1]], inc=s_sc, incn=16)

        ev_aghid = tl.add("gpsimd", lambda e: e.collective_compute(
            "AllGather", ALU.bypass, replica_groups=groups,
            ins=[hid_loc.opt()], outs=[hid_full.opt()]),
            waits=[ev_hdma[NT - 1], ev_hdma[NT - 2]], inc=scc)
        ev_agsco = tl.add("gpsimd", lambda e: e.collective_compute(
            "AllGather", ALU.bypass, replica_groups=groups,
            ins=[sco_loc.opt()], outs=[sco_full.opt()]),
            waits=[ev_scodma], inc=scc)

        if debug_taps:
            tl.add("gpsimd", dma(dbg_xmid, xpark),
                   waits=[ev_xmdma[NT - 1], ev_xmdma[NT - 2]],
                   inc=s_dbg, incn=16)
            tl.add("gpsimd", dma(dbg_hid, hid_loc),
                   waits=[ev_hdma[NT - 1], ev_hdma[NT - 2]],
                   inc=s_dbg, incn=16)
            tl.add("gpsimd", dma(dbg_sco, sco_full), waits=[ev_agsco],
                   inc=s_dbg, incn=16)
            _i = NT - 1
            _dumps = [xb[:, _i % 3, :], ln1b[:, 0, :], ln2b[:], fb[:],
                      ob[:], ub[:], fTs[:], sTb[:, _i % 2, :], yb[:],
                      retb[:, _i % 2, :], xmb[:, _i % 2, :], junk[:]]
            _dw = [ev_agsco, ev_hdma[NT - 1], ev_xmdma[NT - 1],
                   ev_scan[NT - 1]]
            for _k, _src in enumerate(_dumps):
                tl.add("gpsimd", dma(dbg2[_k], _src), waits=_dw,
                       inc=s_dbg, incn=16)

        ev_last_efT = ev_efT[NT - 1]
        ev_last_scan = ev_scan[NT - 1]
        ev_last_y = ev_y[NT - 1]
        ev_last_ehT = ev_ehT[NT - 1]

    if attn_only:
        ev_fin = ev_hdma[NT - 1]
        if debug_taps:
            ev_fin = tl.add('gpsimd', lambda e: e.nop(), waits=[tl.cur(s_dbg)], inc=None) or ev_fin
        tl.add('sync', lambda e: None, waits=[ev_agsco, ev_fin])
        esA.close()
        _hidT_guard.__exit__(None, None, None)
    else:
        esA.close()
        ev_spark_last = None
        esT = contextlib.ExitStack()
        sco4 = esT.enter_context(nc.sbuf_tensor("sco4", [128, 64, E], F32))
        lot = esT.enter_context(nc.sbuf_tensor("lot", [1, E], F32))
        hit = esT.enter_context(nc.sbuf_tensor("hit", [1, E], F32))
        midt = esT.enter_context(nc.sbuf_tensor("midt", [1, E], F32))
        cntt = esT.enter_context(nc.sbuf_tensor("cntt", [1, E], F32))
        needt = esT.enter_context(nc.sbuf_tensor("needt", [1, E], F32))
        geb = esT.enter_context(nc.sbuf_tensor("geb", [1, 2 * E], I32))
        midv = esT.enter_context(nc.sbuf_tensor("midv", [128, E], F32))
        pc = esT.enter_context(nc.sbuf_tensor("pc", [128, E], F32))
        hiv = esT.enter_context(nc.sbuf_tensor("hiv", [128, E], F32))
        lov = esT.enter_context(nc.sbuf_tensor("lov", [128, E], F32))
        needv = esT.enter_context(nc.sbuf_tensor("needv", [128, E], F32))
        m_hi = esT.enter_context(nc.sbuf_tensor("m_hi", [128, 64], F32))
        m_bd = esT.enter_context(nc.sbuf_tensor("m_bd", [128, 64], F32))
        cum = esT.enter_context(nc.sbuf_tensor("cum", [128, 64], F32))
        rnk = esT.enter_context(nc.sbuf_tensor("rnk", [128, 64], F32))
        tk = esT.enter_context(nc.sbuf_tensor("tk", [128, 64], F32))
        selb = esT.enter_context(nc.sbuf_tensor("selb", [128, 64], F32))
        selbi = esT.enter_context(nc.sbuf_tensor("selbi", [128, 64], I32))
        dst_all = esT.enter_context(nc.sbuf_tensor("dst_all", [128, 64, E], F32))
        rowt = esT.enter_context(nc.sbuf_tensor("rowt", [128, 1], F32))
        offs = esT.enter_context(nc.sbuf_tensor("offs", [128, 1], F32))
        odst = esT.enter_context(nc.sbuf_tensor("odst", [128, 64], F32))
        esS = contextlib.ExitStack()
        wsgs = esS.enter_context(nc.sbuf_tensor("wsgs", [128, KC, 2, 1024], BF16))
        wsus = esS.enter_context(nc.sbuf_tensor("wsus", [128, KC, 2, 1024], BF16))
        wsds = esS.enter_context(nc.sbuf_tensor("wsds", [128, 2, MS, 512], BF16))
        hsT = esS.enter_context(nc.sbuf_tensor("hsT", [128, MS, 1024], BF16))
        gtmp = esS.enter_context(nc.sbuf_tensor("gtmp", [128, 1024], BF16))
        shout = esS.enter_context(nc.sbuf_tensor("shout", [128, 512], F32))
        if True:
            attn_dead = [ev_ehT[NT - 1], ev_hdma[NT - 1], ev_hdma[NT - 2],
                         ev_xmdma[NT - 1], ev_xmdma[NT - 2]]
            if debug_taps:
                attn_dead.append(tl.add(
                    "gpsimd", lambda e: e.nop(),
                    waits=[tl.cur(s_dbg)], inc=scc))
            for nh in range(2):
                tl.add("sync", dma(
                    wsds[:, nh, :, :],
                    wsd_b.rearrange("p m (n c) -> p m n c", n=2)[:, :, nh, :]),
                    waits=[ev_wcast["wsd"]] + attn_dead, inc=s_wsh, incn=16)
            for hh in range(2):
                tl.add("sync", dma(
                    wsgs[:, :, hh, :], wsg_b[:, :, hh * 1024:(hh + 1) * 1024]),
                    waits=[ev_wcast["wsg"]] + attn_dead, inc=s_wsh, incn=16)
                ev_wsh_t = tl.add("sync", dma(
                    wsus[:, :, hh, :], wsu_b[:, :, hh * 1024:(hh + 1) * 1024]),
                    waits=[ev_wcast["wsu"]] + attn_dead, inc=s_wsh, incn=16)
            ev_wsd = ev_wsh_t
            ev_wsg_h = [ev_wsh_t, ev_wsh_t]
            ev_wsu_h = [ev_wsh_t, ev_wsh_t]

            ev_sh_h = {}
            ev_gel_prev = None
            ev_dn_half = [None, None]
            ev_shev = None
            for th in range(2):
                for m in range(MS):
                    hh = m // 8

                    def mk_g(e, th=th, m=m, hh=hh):
                        r = None
                        for tb in range(2):
                            tsl = slice(th * 1024 + tb * 512, th * 1024 + (tb + 1) * 512)
                            for k in range(KC):
                                r = e.matmul(
                                    P1[:, tb * 512:(tb + 1) * 512],
                                    wsgs[:, k, hh, (m % 8) * 128:(m % 8 + 1) * 128],
                                    hidT[:, k, tsl],
                                    start=(k == 0), stop=(k == KC - 1))
                        return r
                    w = [ev_wsg_h[hh], ev_last_ehT]
                    if ev_gel_prev is not None:
                        w.append(ev_gel_prev)
                    if th == 1 and m == 0:
                        w.append(ev_dn_half[0])
                    if th == 0 and m == 0:
                        w.append(ev_last_efT)
                        w.append(ev_last_scan)
                    ev_gmmS = tl.add("tensor", mk_g, waits=w, inc=st_)

                    def mk_u(e, th=th, m=m, hh=hh):
                        r = None
                        for tb in range(2):
                            tsl = slice(th * 1024 + tb * 512, th * 1024 + (tb + 1) * 512)
                            for k in range(KC):
                                r = e.matmul(
                                    P2[:, tb * 512:(tb + 1) * 512],
                                    wsus[:, k, hh, (m % 8) * 128:(m % 8 + 1) * 128],
                                    hidT[:, k, tsl],
                                    start=(k == 0), stop=(k == KC - 1))
                        return r
                    wu_ = [ev_wsu_h[hh], ev_last_ehT]
                    if (th, m - 1) in ev_sh_h:
                        wu_.append(ev_sh_h[(th, m - 1)])
                    elif th == 1 and m == 0:
                        wu_.append(ev_sh_h[(0, MS - 1)])
                    elif th == 0 and m == 0:
                        wu_.append(ev_last_scan)
                    ev_ummS = tl.add("tensor", mk_u, waits=wu_, inc=st_)
                    w2 = [ev_gmmS]
                    if ev_gel_prev is not None:
                        w2.append(ev_sh_h.get((th, m - 1)) or ev_sh_h.get((0, MS - 1)))
                    ev_gel = tl.add("scalar", lambda e: e.activation(
                        gtmp[:], P1[:], AF.Gelu), waits=w2, inc=ss)
                    ev_gel_prev = ev_gel
                    ev_sh_h[(th, m)] = tl.add("vector", lambda e, m=m: e.tensor_mul(
                        hsT[:, m, :], gtmp[:], P2[:]), waits=[ev_gel, ev_ummS], inc=sv,
                        nofence=True)
                ev_dn = None
                for nh in range(2):
                    for tt in range(8):
                        def mk_dn(e, th=th, tt=tt, nh=nh):
                            r = None
                            for fc in range(MS):
                                r = e.matmul(P3[:, nh * 512:(nh + 1) * 512],
                                             hsT[:, fc, tt * 128:(tt + 1) * 128],
                                             wsds[:, nh, fc, :],
                                             start=(fc == 0), stop=(fc == MS - 1))
                            return r
                        w = [ev_sh_h[(th, MS - 1)], ev_wsd]
                        if ev_shev is not None:
                            w.append(ev_shev)
                        if th == 0 and tt == 0:
                            w.append(ev_last_y)
                        ev_dn = tl.add("tensor", mk_dn, waits=w, inc=st_)
                        w = [ev_dn]
                        if ev_spark_last is not None:
                            w.append(ev_spark_last)
                        ev_shev = tl.add("vector", lambda e, nh=nh: e.tensor_copy(
                            shout[:], P3[:, nh * 512:(nh + 1) * 512]),
                            waits=w, inc=sv, nofence=True)
                        ev_spark_last = tl.add(
                            "scalar", lambda e, th=th, tt=tt, nh=nh: e.dma_start(
                                out=spark.rearrange("(t p) d -> t p d", p=128)
                                [th * 8 + tt][:, nh * 512:(nh + 1) * 512],
                                in_=shout[:]),
                            waits=[ev_shev], inc=s_sp, incn=16)
                ev_dn_half[th] = ev_dn
            ev_shev_last = ev_shev
            ev_sh_h_last = ev_sh_h[(1, MS - 1)]

        esS.close()
        if True:
            shared_dead = [ev_spark_last, ev_shev_last]
            ev_sco4 = tl.add("sync", dma(
                sco4[:], sco_full.rearrange("(p j) e -> p j e", p=128)),
                waits=[ev_agsco] + shared_dead, inc=s_misc, incn=16)
            tl.add("vector", lambda e: e.memset(lot[:], 0.0), waits=shared_dead)
            tl.add("vector", lambda e: e.memset(hit[:], 1.0))
            ev_upd = tl.add("vector", lambda e: e.memset(midt[:], 0.5), inc=sv)

            for it in range(BISECT_ITERS + 1):
                last = it == BISECT_ITERS
                src = hit if last else midt
                ev_midB = tl.add("tensor", lambda e, src=src: e.matmul(
                    P4[:, 16:16 + E], ones1b[:], src[:], start=True, stop=True),
                    waits=[ev_upd, ev_const, ev_sco[NTR - 1]], inc=st_)
                ev_mve = tl.add("vector", lambda e: e.tensor_copy(
                    midv[:], P4[:, 16:16 + E]), waits=[ev_midB], inc=sv)
                ev_mask = tl.add("vector", lambda e: e.tensor_tensor(
                    junk[:].rearrange("p (j e) -> p j e", e=E), sco4[:],
                    midv[:].rearrange("p (a e) -> p a e", a=1).to_broadcast(
                        [128, 64, E]), ALU.is_ge), waits=[ev_mve, ev_sco4], inc=sv)
                ev_pc = tl.add("vector", lambda e: e.tensor_reduce(
                    pc[:], junk[:].rearrange("p (j e) -> p e j", e=E), AX.X, ALU.add),
                    inc=sv)
                ev_cmm = tl.add("tensor", lambda e: e.matmul(
                    P4[0:1, 0:E], onescb[:], pc[:], start=True, stop=True),
                    waits=[ev_pc, ev_mve], inc=st_)
                ev_cnt = tl.add("vector", lambda e: e.tensor_copy(
                    cntt[:], P4[0:1, 0:E]), waits=[ev_cmm], inc=sv)
                if last:
                    ev_upd = tl.add("vector", lambda e: e.tensor_scalar(
                        needt[:], cntt[:], -1.0, float(CAP), ALU.mult, ALU.add), inc=sv)
                else:
                    tl.add("vector", lambda e: e.tensor_scalar(
                        geb[:, 0:E], cntt[:], float(CAP), None, ALU.is_ge))
                    tl.add("vector", lambda e: e.tensor_scalar(
                        geb[:, E:2 * E], cntt[:], float(CAP), None, ALU.is_lt))
                    tl.add("vector", lambda e: e.copy_predicated(
                        lot[:], geb[:, 0:E], midt[:]))
                    tl.add("vector", lambda e: e.copy_predicated(
                        hit[:], geb[:, E:2 * E], midt[:]))
                    tl.add("vector", lambda e: e.tensor_add(midt[:], lot[:], hit[:]))
                    ev_upd = tl.add("vector", lambda e: e.tensor_scalar(
                        midt[:], midt[:], 0.5, None, ALU.mult), inc=sv)

            ev_bcp = ev_upd
            for srct, dstv in ((lot, lov), (hit, hiv), (needt, needv)):
                evm = tl.add("tensor", lambda e, srct=srct: e.matmul(
                    P4[:, 16:16 + E], ones1b[:], srct[:], start=True, stop=True),
                    waits=[ev_bcp], inc=st_)
                ev_bcp = tl.add("vector", lambda e, dstv=dstv: e.tensor_copy(
                    dstv[:], P4[:, 16:16 + E]), waits=[evm], inc=sv)
            ev_bcast = ev_bcp

            ev_selp = ev_bcast
            for e_ in range(E):
                tl.add("vector", lambda e, e_=e_: e.tensor_tensor(
                    m_hi[:], sco4[:, :, e_], hiv[:, e_:e_ + 1].to_broadcast([128, 64]),
                    ALU.is_ge), waits=[ev_selp])
                tl.add("vector", lambda e, e_=e_: e.tensor_tensor(
                    m_bd[:], sco4[:, :, e_], lov[:, e_:e_ + 1].to_broadcast([128, 64]),
                    ALU.is_ge))
                tl.add("vector", lambda e: e.tensor_sub(m_bd[:], m_bd[:], m_hi[:]))
                tl.add("vector", lambda e: e.tensor_tensor_scan(
                    cum[:], m_bd[:], zzz[:, 0:64], 0.0, ALU.add, ALU.add),
                    waits=[ev_zzz])
                ev_rt = tl.add("vector", lambda e: e.tensor_copy(
                    rowt[:], cum[:, 63:64]), inc=sv)
                ev_om = tl.add("tensor", lambda e: e.matmul(
                    P4[:, 48:49], ltrib[:], rowt[:], start=True, stop=True),
                    waits=[ev_rt], inc=st_)
                ev_of = tl.add("vector", lambda e: e.tensor_copy(
                    offs[:], P4[:, 48:49]), waits=[ev_om], inc=sv)
                tl.add("vector", lambda e: e.scalar_tensor_tensor(
                    rnk[:], cum[:], offs[:], m_bd[:], ALU.add, ALU.subtract),
                    waits=[ev_of])
                tl.add("vector", lambda e, e_=e_: e.tensor_tensor(
                    tk[:], rnk[:], needv[:, e_:e_ + 1].to_broadcast([128, 64]),
                    ALU.is_lt))
                tl.add("vector", lambda e: e.tensor_mul(tk[:], tk[:], m_bd[:]))
                tl.add("vector", lambda e: e.tensor_add(selb[:], m_hi[:], tk[:]))
                tl.add("vector", lambda e: e.tensor_tensor_scan(
                    cum[:], selb[:], zzz[:, 0:64], 0.0, ALU.add, ALU.add))
                ev_rt2 = tl.add("vector", lambda e: e.tensor_copy(
                    rowt[:], cum[:, 63:64]), inc=sv)
                ev_om2 = tl.add("tensor", lambda e: e.matmul(
                    P4[:, 48:49], ltrib[:], rowt[:], start=True, stop=True),
                    waits=[ev_rt2], inc=st_)
                ev_of2 = tl.add("vector", lambda e: e.tensor_copy(
                    offs[:], P4[:, 48:49]), waits=[ev_om2], inc=sv)
                tl.add("vector", lambda e: e.scalar_tensor_tensor(
                    rnk[:], cum[:], offs[:], selb[:], ALU.add, ALU.subtract),
                    waits=[ev_of2])
                tl.add("vector", lambda e: e.tensor_copy(selbi[:], selb[:]))
                ev_selp = tl.add("vector", lambda e, e_=e_: e.select(
                    dst_all[:, :, e_], selbi[:], rnk[:], bigt[:]),
                    waits=[ev_big], inc=sv)

            ev_idxsc = {}
            ev_oi = {}
            for ee in range(4):
                w = [ev_selp]
                if ee > 0:
                    w.append(ev_idxsc[ee - 1])
                tl.add("vector", lambda e, ee=ee: e.tensor_tensor(
                    junk[:].rearrange("p (j e) -> p j e", e=E), dst_all[:],
                    oh4b[:, ee, :].rearrange("p (a e) -> p a e", a=1).to_broadcast(
                        [128, 64, E]), ALU.mult), waits=w)
                tl.add("vector", lambda e: e.tensor_reduce(
                    odst[:], junk[:].rearrange("p (j e) -> p j e", e=E), AX.X,
                    ALU.add))
                ev_oi[ee] = tl.add("vector", lambda e: e.tensor_copy(
                    odsti[:], odst[:]), inc=sv)
                evp = None
                for j in range(64):
                    evp = tl.add("gpsimd", lambda e, ee=ee, j=j: e.indirect_dma_start(
                        out=idxl_d[ee],
                        out_offset=IndirectOffsetOnAxis(odsti[:, j:j + 1], 0),
                        in_=tokidb[:, j:j + 1], in_offset=None,
                        bounds_check=regs['cap'], oob_is_err=False),
                        waits=[ev_oi[ee], ev_const] if j == 0 else [],
                        inc=s_isc, incn=16)
                ev_idxsc[ee] = evp

            if debug_taps:
                for ee in range(4):
                    tl.add("gpsimd", dma(dbg_idx[ee].rearrange("(c a) -> c a", a=1), idxl_d[ee]), waits=[ev_idxsc[3]],
                           inc=s_dbg, incn=16)

            esE = contextlib.ExitStack()
            xtokT = esE.enter_context(nc.sbuf_tensor("xtokT", [128, KC, CAP], BF16))
            hTx = esE.enter_context(nc.sbuf_tensor("hTx", [128, MF, CAP], BF16))
            wgs = esE.enter_context(nc.sbuf_tensor("wgs", [128, KC, 2, 512], BF16))
            wus = esE.enter_context(nc.sbuf_tensor("wus", [128, KC, 2, 512], BF16))
            wds = esE.enter_context(nc.sbuf_tensor("wds", [128, MF, D], BF16))
            gtmp2 = esE.enter_context(nc.sbuf_tensor("gtmp2", [128, 1024], BF16))
            if True:
                ev_wg_h = {}
                ev_wu_h = {}
                ev_mmq = {}
                ev_wd_e = {}
                ev_lastmm_gu = {}
                ev_lastmm_d = {}
                ev_scat = {}
                ev_eo_evac = {}
                prev_scat_end = None
                ev_xg_last = None
                ev_ii = {}

                ev_prep = {}

                def prep(ee):
                    par = ee % 2
                    w = [ev_idxsc[ee]]
                    if ee > 0:
                        w.append(ev_prep[ee - 1]["it"])
                    ev_i8 = tl.add("sync", dma(
                        idx8[:], idxl_d[ee].rearrange("(a b) c -> a (b c)", a=8)),
                        waits=w, inc=s_misc, incn=16)
                    w = [ev_i8]
                    if ee > 0:
                        w.append(ev_prep[ee - 1]["ii"])
                    ev_it = tl.add("tensor", lambda e: e.transpose(
                        P4[:, 64:72], idx8[:], identb32[0:8, 0:8]), waits=w, inc=st_)
                    w = [ev_it]
                    if ee > 1:
                        w.append(ev_scat[(ee - 2, 7)])
                        w.append(ev_scat[(ee - 2, 6)])
                    ev_ii = tl.add("vector", lambda e, par=par: e.tensor_copy(
                        idxi[:, par, :], P4[:, 64:72]), waits=w, inc=sv)
                    ev_xg = None
                    ev_tw = None
                    for s in range(8):
                        w = [ev_ii]
                        if ev_tw is not None:
                            w.append(ev_tw)
                        if ee > 1 and s == 0:
                            w.append(ev_eo_evac[(ee - 2, 7)])
                        ev_g1 = tl.add("gpsimd", lambda e, s=s, par=par:
                            e.indirect_dma_start(
                                out=tw16[:], out_offset=None, in_=sco_full,
                                in_offset=IndirectOffsetOnAxis(idxi[:, par, s:s + 1], 0),
                                bounds_check=None, oob_is_err=False),
                            waits=w, inc=s_tw, incn=16)
                        tl.add("vector", lambda e, ee=ee: e.tensor_mul(
                            tw16[:], tw16[:], oh4b[:, ee, :]), waits=[ev_g1],
                            nofence=True)
                        ev_tw = tl.add("vector", lambda e, s=s, par=par:
                            e.tensor_reduce(
                                topw[:, par, s:s + 1], tw16[:], AX.X, ALU.add),
                            inc=sv)
                        w = [ev_ii, ev_agsco]
                        if ee > 0 and s == 0:
                            w.append(ev_prep[ee - 1]["xT"])
                        ev_xg = tl.add("gpsimd", lambda e, s=s, par=par:
                            e.indirect_dma_start(
                                out=xtok[:, s, :], out_offset=None, in_=hid_full,
                                in_offset=IndirectOffsetOnAxis(idxi[:, par, s:s + 1], 0),
                                bounds_check=None, oob_is_err=False),
                            waits=w, inc=s_xg, incn=16)
                    ev_xT = None
                    ev_xTe = None
                    for s in range(8):
                        def mk_xT(e, s=s):
                            r = None
                            for g in range(KC):
                                r = e.transpose(PB[:, g * 128:(g + 1) * 128],
                                                xtok[:, s, g * 128:(g + 1) * 128],
                                                identbb[:])
                            return r
                        if ev_xTe is None:
                            w = [ev_xg, ev_last_ehT]
                            if ee > 0:
                                w.append(ev_mmq[(ee - 1, 3)])
                        else:
                            w = [ev_xTe]
                        ev_xT = tl.add("tensor", mk_xT, waits=w, inc=st_)
                        ev_xTe = tl.add("vector", lambda e, s=s: e.tensor_copy(
                            xtokT[:, :, s * 128:(s + 1) * 128],
                            PB[:].rearrange("p (k n) -> p k n", k=KC)),
                            waits=[ev_xT], inc=sv, nofence=True)
                    ev_prep[ee] = {"it": ev_it, "ii": ev_ii, "tw": ev_tw,
                                   "xT": ev_xT, "xTe": ev_xTe}

                prep(0)
                ev_gel_last = None
                for ee in range(4):
                    par = ee % 2
                    def load_q(ee, qd):
                        w = [ev_wcast[f"wg{ee}"]]
                        if qd >= 2:
                            w.append(ev_mmq[(ee, qd - 2)])
                        elif ee > 0:
                            w.append(ev_mmq[(ee - 1, qd + 2)])
                        else:
                            w.extend(shared_dead)
                        ev_wg_h[(ee, qd)] = tl.add("sync", dma(
                            wgs[:, :, qd % 2, :],
                            wg_b[ee][:, :, qd * 512:(qd + 1) * 512]),
                            waits=w, inc=s_wg[qd % 2], incn=16)
                        ev_wu_h[(ee, qd)] = tl.add("sync", dma(
                            wus[:, :, qd % 2, :],
                            wu_b[ee][:, :, qd * 512:(qd + 1) * 512]),
                            waits=[ev_wcast[f"wu{ee}"]] + w[1:],
                            inc=s_wu[qd % 2], incn=16)
                    for qd in range(2):
                        load_q(ee, qd)
                    w = [ev_wcast[f"wd{ee}"]]
                    if ee > 0:
                        w.append(ev_lastmm_d.get(ee - 1))
                    else:
                        w.extend(shared_dead)
                    ev_wd_e[ee] = tl.add("sync", dma(wds[:], wd_b[ee]),
                                         waits=w, inc=s_wd, incn=16)

                    ev_h = {}
                    ev_gel2 = None
                    for m in range(MF):
                        hh = m // 8

                        def mk_gx(e, m=m):
                            r = None
                            for n in range(2):
                                for k in range(KC):
                                    r = e.matmul(
                                        P1[:, n * 512:(n + 1) * 512],
                                        wgs[:, k, (m // 4) % 2,
                                            (m % 4) * 128:(m % 4 + 1) * 128],
                                        xtokT[:, k, n * 512:(n + 1) * 512],
                                        start=(k == 0), stop=(k == KC - 1))
                            return r
                        w = [ev_wg_h[(ee, m // 4)], ev_prep[ee]["xTe"]]
                        if ev_gel2 is not None:
                            w.append(ev_gel2)
                        elif ee > 0:
                            w.append(ev_gel_last)
                        else:
                            w.append(ev_sh_h_last)
                        ev_gmx = tl.add("tensor", mk_gx, waits=w, inc=st_)

                        def mk_ux(e, m=m):
                            r = None
                            for n in range(2):
                                for k in range(KC):
                                    r = e.matmul(
                                        P2[:, n * 512:(n + 1) * 512],
                                        wus[:, k, (m // 4) % 2,
                                            (m % 4) * 128:(m % 4 + 1) * 128],
                                        xtokT[:, k, n * 512:(n + 1) * 512],
                                        start=(k == 0), stop=(k == KC - 1))
                            return r
                        w = [ev_wu_h[(ee, m // 4)], ev_prep[ee]["xTe"]]
                        if m > 0:
                            w.append(ev_h[m - 1])
                        elif ee > 0:
                            w.append(ev_eo_evac[(ee - 1, 7)])
                        else:
                            w.append(ev_sh_h_last)
                        ev_umx = tl.add("tensor", mk_ux, waits=w, inc=st_)
                        ev_mmq[(ee, m // 4)] = ev_umx
                        if m == 3 or m == 7:
                            load_q(ee, m // 4 + 2)
                        w2 = [ev_gmx]
                        if m > 0:
                            w2.append(ev_h[m - 1])
                        ev_gel2 = tl.add("scalar", lambda e: e.activation(
                            gtmp2[:], P1[:], AF.Gelu), waits=w2, inc=ss)
                        ev_h[m] = tl.add("vector", lambda e, m=m: e.tensor_mul(
                            hTx[:, m, :], gtmp2[:], P2[:]),
                            waits=[ev_gel2, ev_umx], inc=sv, nofence=True)
                    ev_gel_last = ev_gel2

                    if ee < 3:
                        prep(ee + 1)

                    ev_dn = None
                    for s in range(8):
                        def mk_dnx(e, s=s):
                            r = None
                            for n in range(2):
                                for fc in range(MF):
                                    r = e.matmul(P3[:, n * 512:(n + 1) * 512],
                                                 hTx[:, fc, s * 128:(s + 1) * 128],
                                                 wds[:, fc, n * 512:(n + 1) * 512],
                                                 start=(fc == 0), stop=(fc == MF - 1))
                            return r
                        w = [ev_h[MF - 1], ev_wd_e[ee]]
                        if ev_dn is not None:
                            w.append(ev_eo_evac[(ee, s - 1)])
                        elif ee == 0:
                            w.append(ev_shev_last)
                        else:
                            w.append(ev_eo_evac[(ee - 1, 7)])
                        ev_dn = tl.add("tensor", mk_dnx, waits=w, inc=st_)
                        ev_lastmm_d[ee] = ev_dn
                        w = [ev_dn, ev_prep[ee]["tw"]]
                        if s >= 2:
                            w.append(ev_scat[(ee, s - 2)])
                        elif ee > 0:
                            w.append(ev_scat[(ee - 1, s + 6)])
                        ev_eo_evac[(ee, s)] = tl.add(
                            "vector", lambda e, s=s, par=par: e.tensor_scalar(
                                eo[:, s % 2, :], P3[:], topw[:, par, s:s + 1], None,
                                ALU.mult), waits=w, inc=sv, nofence=True)
                        w = [ev_eo_evac[(ee, s)], ev_dense0]
                        if s == 0 and prev_scat_end is not None:
                            w.append(prev_scat_end)
                        ev_scat[(ee, s)] = tl.add(
                            "gpsimd", lambda e, ee=ee, s=s, par=par:
                            e.indirect_dma_start(
                                out=dense,
                                out_offset=IndirectOffsetOnAxis(idxi[:, par, s:s + 1], 0),
                                in_=eo[:, s % 2, :], in_offset=None,
                                bounds_check=None, oob_is_err=False,
                                compute_op=ALU.add),
                            waits=w, inc=s_eo[s % 2], incn=16)
                    prev_scat_end = ev_scat[(ee, 7)]

                ev_rs = tl.add("gpsimd", lambda e: e.collective_compute(
                    "ReduceScatter", ALU.add, replica_groups=groups,
                    ins=[dense.opt()], outs=[rs_out.opt()]),
                    waits=[prev_scat_end, ev_scat[(3, 6)]], inc=scc)

                if debug_taps:
                    tl.add("gpsimd", dma(dbg_dense, dense),
                           waits=[prev_scat_end, ev_scat[(3, 6)]],
                           inc=s_dbg, incn=16)

        esE.close()
        esT.close()
        _hidT_guard.__exit__(None, None, None)
        esC = contextlib.ExitStack()
        rsb = esC.enter_context(nc.sbuf_tensor("rsb", [128, 2, D], BF16))
        spb = esC.enter_context(nc.sbuf_tensor("spb", [128, 2, D], F32))
        xpb = esC.enter_context(nc.sbuf_tensor("xpb", [128, 2, D], F32))
        fo = esC.enter_context(nc.sbuf_tensor("fo", [128, D], F32))
        ln4 = esC.enter_context(nc.sbuf_tensor("ln4", [128, D], F32))
        ob2 = esC.enter_context(nc.sbuf_tensor("ob2", [128, D], F32))
        cst = esC.enter_context(nc.sbuf_tensor("cst", [128, 8], F32))
        cjk = esC.enter_context(nc.sbuf_tensor("cjk", [128, D], F32))
        if True:
            rsv = rs_out.rearrange("(t p) d -> t p d", p=128)
            spv = spark.rearrange("(t p) d -> t p d", p=128)
            xpv = xpark.rearrange("(t p) d -> t p d", p=128)
            ouv = out.rearrange("(t p) d -> t p d", p=128)
            ev_done = {}
            ev_odma = None
            ev_n4_prev = None
            for j in range(NTR):
                w = [ev_rs] if j < 2 else [ev_done[j - 2]]
                ev_l1 = tl.add("sync", dma(rsb[:, j % 2, :], rsv[j]),
                               waits=w, inc=s_crs[j % 2], incn=16)
                ev_l2 = tl.add("sync", dma(spb[:, j % 2, :], spv[j]),
                               waits=[ev_spark_last] + w, inc=s_csp[j % 2], incn=16)
                ev_l3 = tl.add("sync", dma(xpb[:, j % 2, :], xpv[j]),
                               waits=[ev_xmdma[NT - 1], ev_xmdma[NT - 2]] + w,
                               inc=s_cxp[j % 2], incn=16)
                w = [ev_l1, ev_l2]
                if ev_n4_prev is not None:
                    w.append(ev_n4_prev)
                tl.add("vector", lambda e, j=j: e.tensor_add(
                    fo[:], rsb[:, j % 2, :], spb[:, j % 2, :]), waits=w)
                tl.add("vector", lambda e: e.reduce_sum(cst[:, 0:1], fo[:], AX.X),
                       nofence=True)
                tl.add("vector", lambda e: e.scalar_tensor_tensor(
                    cjk[:], fo[:], 1.0, fo[:], ALU.mult, ALU.mult,
                    accum_out=cst[:, 1:2]))
                tl.add("vector", lambda e: e.tensor_scalar(
                    cst[:, 2:3], cst[:, 0:1], -1.0 / D, None, ALU.mult))
                tl.add("vector", lambda e: e.tensor_mul(
                    cjk[:, 0:1], cst[:, 2:3], cst[:, 2:3]))
                tl.add("vector", lambda e: e.tensor_scalar(
                    cjk[:, 1:2], cst[:, 1:2], 1.0 / D, 1e-5, ALU.mult, ALU.add))
                ev_v4 = tl.add("vector", lambda e: e.tensor_sub(
                    cst[:, 3:4], cjk[:, 1:2], cjk[:, 0:1]), inc=sv)
                ev_s4 = tl.add("scalar", lambda e: e.activation(
                    cst[:, 4:5], cst[:, 3:4], AF.Sqrt), waits=[ev_v4], inc=ss)
                ev_r4 = tl.add("vector", lambda e: e.reciprocal(
                    cst[:, 5:6], cst[:, 4:5]), waits=[ev_s4], inc=sv)
                ev_n4 = tl.add("vector", lambda e: e.tensor_scalar(
                    ln4[:], fo[:], cst[:, 2:3], cst[:, 5:6], ALU.add, ALU.mult),
                    waits=[ev_r4], inc=sv)
                ev_n4_prev = ev_n4
                w = [ev_n4, ev_l3]
                if ev_odma is not None:
                    w.append(ev_odma)
                ev_ob = tl.add("vector", lambda e, j=j: e.tensor_add(
                    ob2[:], ln4[:], xpb[:, j % 2, :]), waits=w, inc=sv,
                    nofence=True)
                ev_odma = tl.add("gpsimd", lambda e, j=j: e.dma_start(
                    out=ouv[j], in_=ob2[:]), waits=[ev_ob], inc=s_out, incn=16)
                ev_done[j] = ev_odma

            tl.add("sync", lambda e: None, waits=[ev_odma])


    if not attn_only:
        esC.close()
    with nc.Block() as block:
        @block.sync
        def _(e):
            tl.emit(e, "sync")

        @block.tensor
        def _(e):
            tl.emit(e, "tensor")

        @block.vector
        def _(e):
            tl.emit(e, "vector")

        @block.scalar
        def _(e):
            tl.emit(e, "scalar")

        @block.gpsimd
        def _(e):
            tl.emit(e, "gpsimd")

    es.close()
    nc.finalize()
    return nc


def _chunked(wT):
    Dd, Ff = wT.shape
    return np.ascontiguousarray(wT.reshape(Dd // 128, 128, Ff).transpose(1, 0, 2))


def make_inputs(inputs):
    x = np.asarray(inputs["L0_attn_x"], np.float32)
    wf = np.asarray(inputs["L0_attn_w_forget"], np.float32)
    wi = np.asarray(inputs["L0_attn_w_input"], np.float32)
    wo = np.asarray(inputs["L0_attn_w_output"], np.float32)
    nw = np.asarray(inputs["L0_attn_norm_w"], np.float32)
    gw = np.asarray(inputs["L0_ffn_gate_w"], np.float32)
    Wg = np.asarray(inputs["L0_ffn_Wg"], np.float32)
    Wu = np.asarray(inputs["L0_ffn_Wu"], np.float32)
    Wd = np.asarray(inputs["L0_ffn_Wd"], np.float32)
    Wsg = np.asarray(inputs["L0_ffn_Ws_g"], np.float32)
    Wsu = np.asarray(inputs["L0_ffn_Ws_u"], np.float32)
    Wsd = np.asarray(inputs["L0_ffn_Ws_d"], np.float32)

    wfT = _chunked(np.ascontiguousarray(wf.T))
    wiT = _chunked(np.ascontiguousarray(wi.T))
    woT = _chunked(np.ascontiguousarray(wo.T))
    gateT = _chunked(np.ascontiguousarray(gw.T))
    wsgT = _chunked(np.ascontiguousarray(Wsg.T))
    wsuT = _chunked(np.ascontiguousarray(Wsu.T))
    wsdT = _chunked(np.ascontiguousarray(Wsd.T))

    identm = np.eye(128, dtype=np.float32)
    k = np.arange(128)
    ltrim = (k[:, None] < k[None, :]).astype(np.float32)
    onesc = np.ones((128, 1), np.float32)
    ones1 = np.ones((1, 128), np.float32)
    tokidm = (np.arange(128)[:, None] * 64 + np.arange(64)[None, :]).astype(np.float32)

    in_maps = []
    for c in range(NCORE):
        b, q = c // 4, c % 4
        lo = q * TLOC - WARM
        if lo < 0:
            xw = np.concatenate([np.zeros((WARM, D), np.float32),
                                 x[b, 0:TLOC]], axis=0)
        else:
            xw = x[b, lo:(q + 1) * TLOC]
        ex = [4 * q + t for t in range(4)]
        wgTs = np.stack([_chunked(np.ascontiguousarray(Wg[e].T)) for e in ex])
        wuTs = np.stack([_chunked(np.ascontiguousarray(Wu[e].T)) for e in ex])
        wdTs = np.stack([_chunked(np.ascontiguousarray(Wd[e].T)) for e in ex])
        oh = np.zeros((128, 4, E), np.float32)
        for t in range(4):
            oh[:, t, 4 * q + t] = 1.0
        in_maps.append({
            "xin": np.ascontiguousarray(xw),
            "wfT": wfT, "wiT": wiT, "woT": woT,
            "normw": nw.reshape(1, D),
            "gateT": gateT, "wsgT": wsgT, "wsuT": wsuT, "wsdT": wsdT,
            "wgT": wgTs, "wuT": wuTs, "wdT": wdTs,
            "ident": identm, "ltri": ltrim, "onesc": onesc, "ones1": ones1,
            "tokid": tokidm, "oh4": oh,
        })
    return in_maps


def kernel(**inputs):
    nc = build_program()
    in_maps = make_inputs(inputs)
    res = run_bass_kernel_spmd(nc, in_maps, core_ids=list(range(NCORE)))
    parts = [res.results[c]["out"] for c in range(NCORE)]
    full = np.stack([np.concatenate(parts[0:4], axis=0),
                     np.concatenate(parts[4:8], axis=0)])
    return full.astype(np.float32)


# version 14
# speedup vs baseline: 1.2520x; 1.0467x over previous
import numpy as np
import concourse.bass as bass
import concourse.mybir as mybir
from concourse.bass_utils import run_bass_kernel_spmd
from concourse.bass_core_rust import IndirectOffsetOnAxis

F32 = mybir.dt.float32
BF16 = mybir.dt.bfloat16
I32 = mybir.dt.int32
AF = mybir.ActivationFunctionType
ALU = mybir.AluOpType
AX = mybir.AxisListType

B, T, D = 2, 8192, 1024
H, DH = 8, 128
E, CAP, FE, FS = 16, 1024, 2048, 2048
NCORE = 8
TLOC = 2048
WARM = 512
NTW = WARM // 128
NTR = TLOC // 128
NT = NTW + NTR
KC = D // 128
MF = FE // 128
MS = FS // 128
LN9 = 2.1972245773362196
BIG = 1.0e9
BISECT_ITERS = 42

ENGINES = ("sync", "tensor", "vector", "scalar", "gpsimd")


class TL:
    def __init__(self):
        self.ops = {e: [] for e in ENGINES}
        self.cnt = {}
        self.autofence = {}
        self.last = {}

    def add(self, eng, fn, waits=(), inc=None, incn=1, nofence=False):
        ws = [w for w in waits if w is not None]
        fence_sem = self.autofence.get(eng) if not nofence else None
        if fence_sem is not None:
            last = self.last.get(eng)
            if last is not None:
                ws.append(last)
            if inc is None:
                inc = fence_sem
        m = None
        if inc is not None:
            c = self.cnt.get(id(inc), 0) + incn
            self.cnt[id(inc)] = c
            m = (inc, c)
        if fence_sem is not None:
            self.last[eng] = m
        self.ops[eng].append((fn, ws, inc, incn))
        return m

    def cur(self, sem):
        return (sem, self.cnt.get(id(sem), 0))

    def emit(self, eng_obj, name):
        for fn, waits, inc, incn in self.ops[name]:
            best, order = {}, []
            for (sem, c) in waits:
                if id(sem) not in best:
                    order.append(sem)
                best[id(sem)] = max(best.get(id(sem), -1), c)
            for sem in order:
                eng_obj.wait_ge(sem, best[id(sem)])
            inst = fn(eng_obj)
            if inc is not None:
                if inst is None:
                    inst = eng_obj.nop()
                inst.then_inc(inc, incn)


def build_program(debug_taps=False, attn_only=False, race_detect=True):
    nc = bass.Bass("TRN2", target_bir_lowering=False,
                   detect_race_conditions=race_detect)

    xin = nc.dram_tensor("xin", [NT * 128, D], F32, kind="ExternalInput").ap()
    wfT = nc.dram_tensor("wfT", [128, KC, D], F32, kind="ExternalInput").ap()
    wiT = nc.dram_tensor("wiT", [128, KC, D], F32, kind="ExternalInput").ap()
    woT = nc.dram_tensor("woT", [128, KC, D], F32, kind="ExternalInput").ap()
    normw = nc.dram_tensor("normw", [1, D], F32, kind="ExternalInput").ap()
    gateT = nc.dram_tensor("gateT", [128, KC, E], F32, kind="ExternalInput").ap()
    wsgT = nc.dram_tensor("wsgT", [128, KC, FS], F32, kind="ExternalInput").ap()
    wsuT = nc.dram_tensor("wsuT", [128, KC, FS], F32, kind="ExternalInput").ap()
    wsdT = nc.dram_tensor("wsdT", [128, MS, D], F32, kind="ExternalInput").ap()
    wgT = nc.dram_tensor("wgT", [4, 128, KC, FE], F32, kind="ExternalInput").ap()
    wuT = nc.dram_tensor("wuT", [4, 128, KC, FE], F32, kind="ExternalInput").ap()
    wdT = nc.dram_tensor("wdT", [4, 128, MF, D], F32, kind="ExternalInput").ap()
    ident = nc.dram_tensor("ident", [128, 128], F32, kind="ExternalInput").ap()
    ltri = nc.dram_tensor("ltri", [128, 128], F32, kind="ExternalInput").ap()
    onesc_i = nc.dram_tensor("onesc", [128, 1], F32, kind="ExternalInput").ap()
    ones1_i = nc.dram_tensor("ones1", [1, 128], F32, kind="ExternalInput").ap()
    tokid = nc.dram_tensor("tokid", [128, 64], F32, kind="ExternalInput").ap()
    oh4 = nc.dram_tensor("oh4", [128, 4, E], F32, kind="ExternalInput").ap()
    out = nc.dram_tensor("out", [TLOC, D], F32, kind="ExternalOutput").ap()
    if debug_taps:
        dbg_xmid = nc.dram_tensor("dbg_xmid", [TLOC, D], F32, kind="ExternalOutput").ap()
        dbg_hid = nc.dram_tensor("dbg_hid", [TLOC, D], F32, kind="ExternalOutput").ap()
        dbg_sco = nc.dram_tensor("dbg_sco", [T, E], F32, kind="ExternalOutput").ap()
        dbg_idx = nc.dram_tensor("dbg_idx", [4, CAP], F32, kind="ExternalOutput").ap()
        dbg_dense = nc.dram_tensor("dbg_dense", [T, D], F32, kind="ExternalOutput").ap()
        dbg2 = nc.dram_tensor("dbg2", [12, 128, D], F32, kind="ExternalOutput").ap()
        dbg3 = nc.dram_tensor("dbg3", [6, 128, D], F32, kind="ExternalOutput").ap()

    wsg_b = nc.dram_tensor("wsg_b", [128, KC, FS], BF16).ap()
    wsu_b = nc.dram_tensor("wsu_b", [128, KC, FS], BF16).ap()
    wsd_b = nc.dram_tensor("wsd_b", [128, MS, D], BF16).ap()
    wg_b = nc.dram_tensor("wg_b", [4, 128, KC, FE], BF16).ap()
    wu_b = nc.dram_tensor("wu_b", [4, 128, KC, FE], BF16).ap()
    wd_b = nc.dram_tensor("wd_b", [4, 128, MF, D], BF16).ap()
    hid_loc = nc.dram_tensor("hid_loc", [TLOC, D], BF16).ap()
    hid_full = nc.dram_tensor("hid_full", [T, D], BF16).ap()
    sco_loc = nc.dram_tensor("sco_loc", [TLOC, E], F32).ap()
    sco_full = nc.dram_tensor("sco_full", [T, E], F32).ap()
    dense = nc.dram_tensor("dense", [T, D], BF16).ap()
    rs_out = nc.dram_tensor("rs_out", [TLOC, D], BF16).ap()
    xpark = nc.dram_tensor("xpark", [TLOC, D], F32).ap()
    spark = nc.dram_tensor("spark", [TLOC, D], F32).ap()
    idxl_d = [nc.dram_tensor(f"idxl_d{ee}", [CAP, 1], F32).ap()
              for ee in range(4)]

    xv = xin.rearrange("(t p) d -> t p d", p=128)
    tl = TL()
    groups = [[0, 1, 2, 3], [4, 5, 6, 7]]

    import contextlib
    es = contextlib.ExitStack()

    def mksem(name):
        return es.enter_context(nc.semaphore(name))
    scc = mksem("scc")
    sv = mksem("sv")
    ss = mksem("ss")
    st_ = mksem("st")
    s_const = mksem("s_const")
    s_dense = mksem("s_dense")
    s_x = [mksem(f"s_x{k}") for k in range(3)]
    s_wsh = mksem("s_wsh")
    s_misc = mksem("s_misc")
    s_wg = [mksem(f"s_wg{k}") for k in range(2)]
    s_wu = [mksem(f"s_wu{k}") for k in range(2)]
    s_wd = mksem("s_wd")
    s_crs = [mksem(f"s_crs{k}") for k in range(2)]
    s_csp = [mksem(f"s_csp{k}") for k in range(2)]
    s_cxp = [mksem(f"s_cxp{k}") for k in range(2)]
    s_xp = [mksem(f"s_xp{k}") for k in range(2)]
    s_hd = [mksem(f"s_hd{k}") for k in range(2)]
    s_sc = mksem("s_sc")
    s_sp = mksem("s_sp")
    s_identb = mksem("s_identb")
    s_attnw = mksem("s_attnw")
    s_cast = mksem("s_cast")
    s_isc = mksem("s_isc")
    s_tw = mksem("s_tw")
    s_xg = mksem("s_xg")
    s_eo = [mksem(f"s_eo{k}") for k in range(2)]
    s_out = mksem("s_out")
    s_dbg = mksem("s_dbg")
    tl.autofence = {"vector": sv}

    P1 = es.enter_context(nc.psum_tensor("P1", [128, 1024], F32))
    P2 = es.enter_context(nc.psum_tensor("P2", [128, 1024], F32))
    P3 = es.enter_context(nc.psum_tensor("P3", [128, 1024], F32))
    PB = es.enter_context(nc.psum_tensor("PB", [128, 1024], BF16))
    P4 = es.enter_context(nc.psum_tensor("P4", [128, 512], F32))

    identb32 = es.enter_context(nc.sbuf_tensor("identb32", [128, 128], F32))
    identbb = es.enter_context(nc.sbuf_tensor("identbb", [128, 128], BF16))
    ltrib = es.enter_context(nc.sbuf_tensor("ltrib", [128, 128], F32))
    onescb = es.enter_context(nc.sbuf_tensor("onescb", [128, 1], F32))
    ones1b = es.enter_context(nc.sbuf_tensor("ones1b", [1, 128], F32))
    tokidb = es.enter_context(nc.sbuf_tensor("tokidb", [128, 64], F32))
    oh4b = es.enter_context(nc.sbuf_tensor("oh4b", [128, 4, E], F32))
    zzz = es.enter_context(nc.sbuf_tensor("zzz", [128, 64], F32))
    bigt = es.enter_context(nc.sbuf_tensor("bigt", [128, 64], F32))
    idx8 = es.enter_context(nc.sbuf_tensor("idx8", [8, 128], F32))
    idxi = es.enter_context(nc.sbuf_tensor("idxi", [128, 2, 8], I32))
    tw16 = es.enter_context(nc.sbuf_tensor("tw16", [128, E], F32))
    topw = es.enter_context(nc.sbuf_tensor("topw", [128, 2, 8], F32))
    odsti = es.enter_context(nc.sbuf_tensor("odsti", [128, 4, 64], I32))
    xtok = es.enter_context(nc.sbuf_tensor("xtok", [128, 8, D], BF16))
    eo = es.enter_context(nc.sbuf_tensor("eo", [128, 2, D], BF16))
    zzzb = es.enter_context(nc.sbuf_tensor("zzzb", [128, 1024], BF16))
    gatb = es.enter_context(nc.sbuf_tensor("gatb", [128, KC, E], BF16))
    sco_all = es.enter_context(nc.sbuf_tensor("sco_all", [128, NTR, E], F32))
    gj = es.enter_context(nc.sbuf_tensor("gj", [128, 8], F32))
    junk = es.enter_context(nc.sbuf_tensor("junk", [128, D], F32))
    _hidT_guard = nc.sbuf_tensor("hidT", [128, KC, TLOC], BF16)
    hidT = _hidT_guard.__enter__()

    def dma(dst, src):
        return lambda e: e.dma_start(out=dst, in_=src)

    regs = {}
    def mk_regs(e):
        regs['cap'] = e.to_reg(CAP - 1)
        return None
    tl.add('gpsimd', mk_regs)

    ev_const = None
    for dst, src in ((identb32[:], ident), (ltrib[:], ltri), (onescb[:], onesc_i),
                     (ones1b[:], ones1_i), (tokidb[:], tokid), (oh4b[:], oh4)):
        ev_const = tl.add("sync", dma(dst, src), inc=s_const, incn=16)
    ev_identb = tl.add("gpsimd", dma(identbb[:], ident), inc=s_identb, incn=16)
    ev_zzz = tl.add("vector", lambda e: e.memset(zzz[:], 0.0), inc=sv)
    ev_zzzb = tl.add("vector", lambda e: e.memset(zzzb[:], 0.0), inc=sv)
    ev_big = tl.add("vector", lambda e: e.memset(bigt[:], BIG), inc=sv)



    esA = contextlib.ExitStack()
    wfb = esA.enter_context(nc.sbuf_tensor("wfb", [128, KC, D], BF16))
    wib = esA.enter_context(nc.sbuf_tensor("wib", [128, KC, D], BF16))
    wob = esA.enter_context(nc.sbuf_tensor("wob", [128, KC, D], BF16))
    normwb = esA.enter_context(nc.sbuf_tensor("normwb", [128, D], F32))
    nw1 = esA.enter_context(nc.sbuf_tensor("nw1", [1, D], F32))
    xb = esA.enter_context(nc.sbuf_tensor("xb", [128, 3, D], F32))
    sta = esA.enter_context(nc.sbuf_tensor("sta", [128, 2, 16], F32))
    st2 = esA.enter_context(nc.sbuf_tensor("st2", [128, 2, 16], F32))
    ln1b = esA.enter_context(nc.sbuf_tensor("ln1b", [128, 1, D], F32))
    xnb = esA.enter_context(nc.sbuf_tensor("xnb", [128, D], BF16))
    xnT = esA.enter_context(nc.sbuf_tensor("xnT", [128, KC, 128], BF16))
    fb = esA.enter_context(nc.sbuf_tensor("fb", [128, D], F32))
    ob = esA.enter_context(nc.sbuf_tensor("ob", [128, D], F32))
    vs = esA.enter_context(nc.sbuf_tensor("vs", [128, H], F32))
    ub = esA.enter_context(nc.sbuf_tensor("ub", [128, D], F32))
    fTs = esA.enter_context(nc.sbuf_tensor("fTs", [128, D], F32))
    sTb = esA.enter_context(nc.sbuf_tensor("sTb", [128, 2, D], F32))
    yb = esA.enter_context(nc.sbuf_tensor("yb", [128, D], F32))
    retb = esA.enter_context(nc.sbuf_tensor("retb", [128, 2, D], F32))
    ln2b = esA.enter_context(nc.sbuf_tensor("ln2b", [128, D], F32))
    xmb = esA.enter_context(nc.sbuf_tensor("xmb", [128, 2, D], F32))
    hbb = esA.enter_context(nc.sbuf_tensor("hbb", [128, 2, D], BF16))
    if True:
        tl.add("gpsimd", dma(wfb[:], wfT), inc=s_attnw, incn=16)
        tl.add("gpsimd", dma(wib[:], wiT), inc=s_attnw, incn=16)
        tl.add("gpsimd", dma(wob[:], woT), inc=s_attnw, incn=16)
        ev_attnw = tl.add("gpsimd", dma(gatb[:], gateT), inc=s_attnw, incn=16)
        ev_wf = ev_wi = ev_wo = ev_gat = ev_attnw
        ev_wcast = {}
        for name_, dst_, src_ in (("wsg", wsg_b, wsgT), ("wsu", wsu_b, wsuT),
                                  ("wsd", wsd_b, wsdT)):
            ev_wcast[name_] = tl.add("gpsimd", dma(dst_, src_), inc=s_cast, incn=16)
        for ee_ in range(4):
            ev_wcast[f"wg{ee_}"] = tl.add("gpsimd", dma(wg_b[ee_], wgT[ee_]),
                                          inc=s_cast, incn=16)
            ev_wcast[f"wu{ee_}"] = tl.add("gpsimd", dma(wu_b[ee_], wuT[ee_]),
                                          inc=s_cast, incn=16)
            ev_wcast[f"wd{ee_}"] = tl.add("gpsimd", dma(wd_b[ee_], wdT[ee_]),
                                          inc=s_cast, incn=16)
        ev_cast_all = ev_wcast["wd3"]
        for k_ in ev_wcast:
            ev_wcast[k_] = ev_cast_all
        ev_nw = tl.add("sync", dma(nw1[:], normw), inc=s_const, incn=16)
        ev_const = ev_nw

        def mk_nwb(e):
            e.matmul(P1[:, 0:512], ones1b[:], nw1[:, 0:512], start=True, stop=True)
            return e.matmul(P1[:, 512:1024], ones1b[:], nw1[:, 512:1024],
                            start=True, stop=True)
        ev_nwmm = tl.add("tensor", mk_nwb, waits=[ev_nw, ev_const], inc=st_)
        ev_nwb = tl.add("vector", lambda e: e.tensor_copy(normwb[:], P1[:]),
                        waits=[ev_nwmm], inc=sv)

        evx = [None] * NT
        ev_stats = [None] * NT
        ev_sq = [None] * NT
        ev_recip = [None] * NT
        ev_xn = [None] * NT
        ev_xnTpe = [None] * NT
        ev_exnT = [None] * NT
        ev_zf = [None] * NT
        ev_zv = [None] * NT
        ev_zo = [None] * NT
        ev_zfm = [None] * NT
        ev_f = [None] * NT
        ev_o = [None] * NT
        ev_vs = [None] * NT
        ev_u = [None] * NT
        ev_fTpe = [None] * NT
        ev_uTpe = [None] * NT
        ev_efT = [None] * NT
        ev_scan = [None] * NT
        ev_stbpe = [None] * NT
        ev_y = [None] * NT
        ev_ret = [None] * NT
        ev_st2 = [None] * NT
        ev_sq2 = [None] * NT
        ev_r2 = [None] * NT
        ev_ln2 = [None] * NT
        ev_xm = [None] * NT
        ev_st3 = [None] * NT
        ev_sq3 = [None] * NT
        ev_hid = [None] * NT
        ev_hTpe = [None] * NT
        ev_ehT = [None] * NT
        ev_xmdma = [None] * NT
        ev_hdma = [None] * NT

        def stageA(i):
            real = i >= NTW
            waits = [evx[i - 3], ev_xm[i - 3] if i - 3 >= NTW else ev_xn[i - 3]] if i >= 3 else []
            evx[i] = tl.add("sync", dma(xb[:, i % 3, :], xv[i]), waits=waits,
                            inc=s_x[i % 3], incn=16)
            tl.add("vector", lambda e, i=i: e.reduce_sum(
                sta[:, i % 2, 0:1], xb[:, i % 3, :], AX.X), waits=[evx[i]],
                nofence=True)
            tl.add("vector", lambda e, i=i: e.scalar_tensor_tensor(
                junk[:], xb[:, i % 3, :], 1.0, xb[:, i % 3, :], ALU.mult, ALU.mult,
                accum_out=sta[:, i % 2, 1:2]), waits=[evx[i]])
            tl.add("vector", lambda e, i=i: e.tensor_scalar(
                sta[:, i % 2, 2:3], sta[:, i % 2, 0:1], -1.0 / D, None, ALU.mult))
            tl.add("vector", lambda e, i=i: e.tensor_scalar(
                sta[:, i % 2, 3:4], sta[:, i % 2, 1:2], 1.0 / D, None, ALU.mult))
            tl.add("vector", lambda e, i=i: e.tensor_mul(
                sta[:, i % 2, 4:5], sta[:, i % 2, 2:3], sta[:, i % 2, 2:3]))
            tl.add("vector", lambda e, i=i: e.tensor_sub(
                sta[:, i % 2, 5:6], sta[:, i % 2, 3:4], sta[:, i % 2, 4:5]))
            if real:
                tl.add("vector", lambda e, i=i: e.tensor_scalar(
                    sta[:, i % 2, 6:7], sta[:, i % 2, 5:6], 1.0, 1e-5,
                    ALU.mult, ALU.add))
            ev_stats[i] = tl.add("vector", lambda e, i=i: e.tensor_scalar(
                sta[:, i % 2, 7:8], sta[:, i % 2, 5:6], 1.0 + 1e-6, 1e-11,
                ALU.mult, ALU.add), inc=sv)

            w = [ev_stats[i]]
            if real:
                tl.add("scalar", lambda e, i=i: e.activation(
                    sta[:, i % 2, 8:9], sta[:, i % 2, 6:7], AF.Sqrt), waits=w)
                w = []
            ev_sq[i] = tl.add("scalar", lambda e, i=i: e.activation(
                sta[:, i % 2, 9:10], sta[:, i % 2, 7:8], AF.Sqrt), waits=w, inc=ss)
            if i - 1 >= NTW and ev_st2[i - 1] is not None:
                ev_sq2[i - 1] = tl.add("scalar", lambda e, i=i: e.activation(
                    st2[:, (i - 1) % 2, 4:5], st2[:, (i - 1) % 2, 3:4], AF.Sqrt),
                    waits=[ev_st2[i - 1]], inc=ss)
            if i - 2 >= NTW and ev_st3[i - 2] is not None:
                ev_sq3[i - 2] = tl.add("scalar", lambda e, i=i: e.activation(
                    st2[:, (i - 2) % 2, 12:13], st2[:, (i - 2) % 2, 11:12], AF.Sqrt),
                    waits=[ev_st3[i - 2]], inc=ss)

            w = [ev_sq[i]]
            if real:
                tl.add("vector", lambda e, i=i: e.reciprocal(
                    sta[:, i % 2, 10:11], sta[:, i % 2, 8:9]), waits=w)
                w = []
            ev_recip[i] = tl.add("vector", lambda e, i=i: e.reciprocal(
                sta[:, i % 2, 11:12], sta[:, i % 2, 9:10]), waits=w, inc=sv)
            if real:
                w = [ev_recip[i]]
                if i - 1 >= NTW:
                    w.append(ev_ret[i - 1])
                tl.add("vector", lambda e, i=i: e.tensor_scalar(
                    ln1b[:, 0, :], xb[:, i % 3, :], sta[:, i % 2, 2:3],
                    sta[:, i % 2, 10:11], ALU.add, ALU.mult), waits=w)
            tl.add("vector", lambda e, i=i: e.tensor_scalar(
                ln2b[:], xb[:, i % 3, :], sta[:, i % 2, 2:3], sta[:, i % 2, 11:12],
                ALU.add, ALU.mult), waits=[ev_recip[i]])
            w = [ev_nwb, ev_identb]
            if i >= 1:
                w.append(ev_xnTpe[i - 1])
            ev_xn[i] = tl.add("vector", lambda e: e.tensor_mul(
                xnb[:], ln2b[:], normwb[:]), waits=w, inc=sv, nofence=True)

            w = [ev_xn[i]]
            if i >= 1:
                w.append(ev_exnT[i - 1])
            if i - 3 >= NTW and ev_ehT[i - 3] is not None:
                w.append(ev_ehT[i - 3])

            def mk_xnT(e):
                r = None
                for g in range(KC):
                    r = e.transpose(PB[:, g * 128:(g + 1) * 128],
                                    xnb[:, g * 128:(g + 1) * 128], identbb[:])
                return r
            ev_xnTpe[i] = tl.add("tensor", mk_xnT, waits=w, inc=st_)
            ev_exnT[i] = tl.add("vector", lambda e: e.tensor_copy(
                xnT[:].rearrange("p k n -> p (k n)"), PB[:]),
                waits=[ev_xnTpe[i]], inc=sv, nofence=True)

            def mk_proj(dst, wsb):
                def f(e):
                    r = None
                    for k in range(KC):
                        for n in range(2):
                            r = e.matmul(dst[:, n * 512:(n + 1) * 512],
                                         xnT[:, k, :],
                                         wsb[:, k, n * 512:(n + 1) * 512],
                                         start=(k == 0), stop=(k == KC - 1))
                    return r
                return f
            w = [ev_exnT[i], ev_wf]
            if i >= 1:
                w.append(ev_efT[i - 1])
            ev_zf[i] = tl.add("tensor", mk_proj(P1, wfb), waits=w, inc=st_)
            w = [ev_exnT[i], ev_wi]
            if i >= 1:
                w.append(ev_scan[i - 1])
            ev_zv[i] = tl.add("tensor", mk_proj(P2, wib), waits=w, inc=st_)
            if real:
                w = [ev_exnT[i], ev_wo]
                if i - 1 >= NTW:
                    w.append(ev_y[i - 1])
                ev_zo[i] = tl.add("tensor", mk_proj(P3, wob), waits=w, inc=st_)

            ev_zfm[i] = tl.add("vector", lambda e: e.tensor_scalar(
                junk[:], P1[:], LN9, None, ALU.max), waits=[ev_zf[i]], inc=sv,
                nofence=True)
            ev_f[i] = tl.add("scalar", lambda e: e.activation(
                fb[:], junk[:], AF.Sigmoid), waits=[ev_zfm[i]], inc=ss)
            if real:
                ev_o[i] = tl.add("scalar", lambda e: e.activation(
                    ob[:], P3[:], AF.Sigmoid), waits=[ev_zo[i]], inc=ss)
            ev_vs[i] = tl.add("vector", lambda e: e.tensor_reduce(
                vs[:], P2[:].rearrange("p (h d) -> p h d", h=H), AX.X, ALU.add),
                waits=[ev_zv[i]], inc=sv, nofence=True)
            tl.add("vector", lambda e: e.tensor_scalar(
                junk[:], fb[:], -1.0, 1.0, ALU.mult, ALU.add), waits=[ev_f[i]],
                nofence=True)
            ev_u[i] = tl.add("vector", lambda e: e.tensor_tensor(
                ub[:].rearrange("p (h d) -> p h d", h=H),
                junk[:].rearrange("p (h d) -> p h d", h=H),
                vs[:].to_broadcast([128, H, DH]), ALU.mult), inc=sv)

            def mk_tr(dst_ps, src):
                def f(e):
                    r = None
                    for g in range(KC):
                        r = e.transpose(dst_ps[:, g * 128:(g + 1) * 128],
                                        src[:, g * 128:(g + 1) * 128], identb32[:])
                    return r
                return f
            ev_fTpe[i] = tl.add("tensor", mk_tr(P1, fb[:]),
                                waits=[ev_f[i], ev_zfm[i]], inc=st_)
            ev_uTpe[i] = tl.add("tensor", mk_tr(P2, ub[:]),
                                waits=[ev_u[i], ev_vs[i]], inc=st_)
            ev_efT[i] = tl.add("vector", lambda e: e.tensor_copy(fTs[:], P1[:]),
                               waits=[ev_fTpe[i]], inc=sv, nofence=True)
            ev_euT = tl.add("vector", lambda e: e.tensor_copy(ub[:], P2[:]),
                            waits=[ev_uTpe[i]], inc=sv, nofence=True)

            w = [ev_efT[i], ev_euT]
            if i >= 1:
                w.append(ev_scan[i - 1])
            if i - 2 >= NTW:
                w.append(ev_stbpe[i - 2])

            def mk_scans(e, i=i):
                r = None
                for g in range(KC):
                    init = 0.0 if i == 0 else sTb[:, (i - 1) % 2,
                                                  g * 128 + 127:g * 128 + 128]
                    r = e.tensor_tensor_scan(
                        sTb[:, i % 2, g * 128:(g + 1) * 128],
                        fTs[:, g * 128:(g + 1) * 128],
                        ub[:, g * 128:(g + 1) * 128],
                        init, ALU.mult, ALU.add)
                return r
            ev_scan[i] = tl.add("vector", mk_scans, waits=w, inc=sv)
            if debug_taps and i == 0:
                tl.add("gpsimd", dma(dbg3[0], xb[:, 0, :]), waits=[ev_scan[0]],
                       inc=s_dbg, incn=16)
                tl.add("gpsimd", dma(dbg3[1], ub[:]), waits=[ev_scan[0]],
                       inc=s_dbg, incn=16)
                tl.add("gpsimd", dma(dbg3[2], sTb[:, 0, :]), waits=[ev_scan[0]],
                       inc=s_dbg, incn=16)
                tl.add("gpsimd", dma(dbg3[3], ln2b[:]), waits=[ev_scan[0]],
                       inc=s_dbg, incn=16)
                def _cast_xn(e):
                    return e.dma_start(out=dbg3[4], in_=xnb[:])
                tl.add("gpsimd", _cast_xn, waits=[ev_scan[0]],
                       inc=s_dbg, incn=16)
                tl.add("gpsimd", dma(dbg3[5][:, 0:12], sta[:, 0, 0:12]),
                       waits=[ev_scan[0]], inc=s_dbg, incn=16)

            if not real:
                return
            ev_stbpe[i] = tl.add("tensor", mk_tr(P3, sTb[:, i % 2, :]),
                                 waits=[ev_scan[i], ev_o[i]], inc=st_)
            ev_y[i] = tl.add("vector", lambda e: e.tensor_mul(yb[:], ob[:], P3[:]),
                             waits=[ev_stbpe[i]], inc=sv, nofence=True)
            w = []
            if i - 2 >= NTW:
                w.append(ev_ln2[i - 2])
            ev_ret[i] = tl.add("vector", lambda e, i=i: e.tensor_add(
                retb[:, i % 2, :], ln1b[:, 0, :], yb[:]), waits=w, inc=sv,
                nofence=True)
            w = []
            if i - 2 >= NTW and ev_hid[i - 2] is not None:
                w.append(ev_hid[i - 2])
            tl.add("vector", lambda e, i=i: e.reduce_sum(
                st2[:, i % 2, 0:1], retb[:, i % 2, :], AX.X), waits=w,
                nofence=True)
            tl.add("vector", lambda e, i=i: e.scalar_tensor_tensor(
                junk[:], retb[:, i % 2, :], 1.0, retb[:, i % 2, :], ALU.mult,
                ALU.mult, accum_out=st2[:, i % 2, 1:2]))
            tl.add("vector", lambda e, i=i: e.tensor_scalar(
                st2[:, i % 2, 2:3], st2[:, i % 2, 0:1], -1.0 / D, None, ALU.mult))
            tl.add("vector", lambda e, i=i: e.tensor_mul(
                junk[:, 0:1], st2[:, i % 2, 2:3], st2[:, i % 2, 2:3]))
            tl.add("vector", lambda e, i=i: e.tensor_scalar(
                junk[:, 1:2], st2[:, i % 2, 1:2], 1.0 / D, 1e-5, ALU.mult, ALU.add))
            ev_st2[i] = tl.add("vector", lambda e, i=i: e.tensor_sub(
                st2[:, i % 2, 3:4], junk[:, 1:2], junk[:, 0:1]), inc=sv)

        def stageB(i):
            ev_r2[i] = tl.add("vector", lambda e, i=i: e.reciprocal(
                st2[:, i % 2, 5:6], st2[:, i % 2, 4:5]), waits=[ev_sq2[i]], inc=sv)
            ev_ln2[i] = tl.add("vector", lambda e, i=i: e.tensor_scalar(
                ln2b[:], retb[:, i % 2, :], st2[:, i % 2, 2:3],
                st2[:, i % 2, 5:6], ALU.add, ALU.mult),
                waits=[ev_r2[i]], inc=sv)
            w = [ev_ln2[i]]
            if i - 2 >= NTW:
                w.append(ev_xmdma[i - 2])
            ev_xm[i] = tl.add("vector", lambda e, i=i: e.tensor_add(
                xmb[:, i % 2, :], ln2b[:], xb[:, i % 3, :]), waits=w, inc=sv,
                nofence=True)
            ev_xmdma[i] = tl.add("scalar", lambda e, i=i: e.dma_start(
                out=xpark.rearrange("(t p) d -> t p d", p=128)[i - NTW],
                in_=xmb[:, i % 2, :]), waits=[ev_xm[i]], inc=s_xp[i % 2], incn=16)
            tl.add("vector", lambda e, i=i: e.reduce_sum(
                st2[:, i % 2, 8:9], xmb[:, i % 2, :], AX.X), nofence=True)
            tl.add("vector", lambda e, i=i: e.scalar_tensor_tensor(
                junk[:], xmb[:, i % 2, :], 1.0, xmb[:, i % 2, :], ALU.mult,
                ALU.mult, accum_out=st2[:, i % 2, 9:10]))
            tl.add("vector", lambda e, i=i: e.tensor_scalar(
                st2[:, i % 2, 10:11], st2[:, i % 2, 8:9], -1.0 / D, None, ALU.mult))
            tl.add("vector", lambda e, i=i: e.tensor_mul(
                junk[:, 0:1], st2[:, i % 2, 10:11], st2[:, i % 2, 10:11]))
            tl.add("vector", lambda e, i=i: e.tensor_scalar(
                junk[:, 1:2], st2[:, i % 2, 9:10], 1.0 / D, 1e-5, ALU.mult, ALU.add))
            ev_st3[i] = tl.add("vector", lambda e, i=i: e.tensor_sub(
                st2[:, i % 2, 11:12], junk[:, 1:2], junk[:, 0:1]), inc=sv)

        def stageC(i):
            ev_r3 = tl.add("vector", lambda e, i=i: e.reciprocal(
                st2[:, i % 2, 13:14], st2[:, i % 2, 12:13]), waits=[ev_sq3[i]],
                inc=sv)
            w = [ev_r3]
            if i - 2 >= NTW and ev_hTpe[i - 2] is not None:
                w.append(ev_hTpe[i - 2])
                w.append(ev_hdma[i - 2])
            ev_hid[i] = tl.add("vector", lambda e, i=i: e.tensor_scalar(
                hbb[:, i % 2, :], xmb[:, i % 2, :], st2[:, i % 2, 10:11],
                st2[:, i % 2, 13:14], ALU.add, ALU.mult),
                waits=w, inc=sv)
            ev_hdma[i] = tl.add("scalar", lambda e, i=i: e.dma_start(
                out=hid_loc.rearrange("(t p) d -> t p d", p=128)[i - NTW],
                in_=hbb[:, i % 2, :]), waits=[ev_hid[i]], inc=s_hd[i % 2], incn=16)
            w = [ev_hid[i]]
            tslot = min(i + 2, NT - 1)
            w.append(ev_exnT[tslot])
            if i - 1 >= NTW and ev_ehT[i - 1] is not None:
                w.append(ev_ehT[i - 1])

            def mk_hT(e, i=i):
                r = None
                for g in range(KC):
                    r = e.transpose(PB[:, g * 128:(g + 1) * 128],
                                    hbb[:, i % 2, g * 128:(g + 1) * 128], identbb[:])
                return r
            ev_hTpe[i] = tl.add("tensor", mk_hT, waits=w, inc=st_)
            j = i - NTW
            ev_ehT[i] = tl.add("vector", lambda e, j=j: e.tensor_copy(
                hidT[:, :, j * 128:(j + 1) * 128],
                PB[:].rearrange("p (k n) -> p k n", k=KC)),
                waits=[ev_hTpe[i]], inc=sv, nofence=True)

        for i in range(NT):
            stageA(i)
            if i - 1 >= NTW:
                stageB(i - 1)
            if i - 2 >= NTW:
                stageC(i - 2)
        i = NT - 1
        ev_sq2[i] = tl.add("scalar", lambda e, i=i: e.activation(
            st2[:, i % 2, 4:5], st2[:, i % 2, 3:4], AF.Sqrt),
            waits=[ev_st2[i]], inc=ss)
        stageB(NT - 1)
        for i in (NT - 2, NT - 1):
            ev_sq3[i] = tl.add("scalar", lambda e, i=i: e.activation(
                st2[:, i % 2, 12:13], st2[:, i % 2, 11:12], AF.Sqrt),
                waits=[ev_st3[i]], inc=ss)
            stageC(i)

        dnv = dense.rearrange("(n p) d -> n p d", p=128)
        ev_dense0 = None
        for n_ in range(T // 128):
            ev_dense0 = tl.add("scalar", dma(dnv[n_], zzzb[:]), waits=[ev_zzzb],
                               inc=s_dense, incn=16)

        if True:
            ev_gmm = [None] * NTR
            ev_sco = [None] * NTR
            for j in range(NTR):
                i = j + NTW

                def mk_gate(e, j=j):
                    r = None
                    for k in range(KC):
                        r = e.matmul(P4[:, 0:E], hidT[:, k, j * 128:(j + 1) * 128],
                                     gatb[:, k, :], start=(k == 0),
                                     stop=(k == KC - 1))
                    return r
                w = [ev_ehT[i], ev_gat]
                if j >= 1:
                    w.append(ev_sco[j - 1])
                ev_gmm[j] = tl.add("tensor", mk_gate, waits=w, inc=st_)
                tl.add("vector", lambda e: e.reduce_max(
                    gj[:, 2:3], P4[:, 0:E], AX.X), waits=[ev_gmm[j]])
                ev_mx = tl.add("vector", lambda e: e.tensor_scalar(
                    gj[:, 3:4], gj[:, 2:3], -1.0, None, ALU.mult), inc=sv)
                ev_ex = tl.add("scalar", lambda e, j=j: e.activation(
                    sco_all[:, j, :], P4[:, 0:E], AF.Exp, bias=gj[:, 3:4]),
                    waits=[ev_mx], inc=ss)
                tl.add("vector", lambda e, j=j: e.reduce_sum(
                    gj[:, 4:5], sco_all[:, j, :], AX.X), waits=[ev_ex])
                ev_rc = tl.add("vector", lambda e: e.reciprocal(
                    gj[:, 5:6], gj[:, 4:5]), inc=sv)
                ev_sco[j] = tl.add("vector", lambda e, j=j: e.tensor_scalar(
                    sco_all[:, j, :], sco_all[:, j, :], gj[:, 5:6], None,
                    ALU.mult), waits=[ev_rc], inc=sv)
            ev_scodma = tl.add("scalar", lambda e: e.dma_start(
                out=sco_loc.rearrange("(j p) e -> p j e", p=128), in_=sco_all[:]),
                waits=[ev_sco[NTR - 1]], inc=s_sc, incn=16)

        ev_agsco = tl.add("gpsimd", lambda e: e.collective_compute(
            "AllGather", ALU.bypass, replica_groups=groups,
            ins=[sco_loc.opt()], outs=[sco_full.opt()]),
            waits=[ev_scodma], inc=scc)
        ev_aghid = tl.add("gpsimd", lambda e: e.collective_compute(
            "AllGather", ALU.bypass, replica_groups=groups,
            ins=[hid_loc.opt()], outs=[hid_full.opt()]),
            waits=[ev_hdma[NT - 1], ev_hdma[NT - 2]], inc=scc)

        if debug_taps:
            tl.add("gpsimd", dma(dbg_xmid, xpark),
                   waits=[ev_xmdma[NT - 1], ev_xmdma[NT - 2]],
                   inc=s_dbg, incn=16)
            tl.add("gpsimd", dma(dbg_hid, hid_loc),
                   waits=[ev_hdma[NT - 1], ev_hdma[NT - 2]],
                   inc=s_dbg, incn=16)
            tl.add("gpsimd", dma(dbg_sco, sco_full), waits=[ev_agsco],
                   inc=s_dbg, incn=16)
            _i = NT - 1
            _dumps = [xb[:, _i % 3, :], ln1b[:, 0, :], ln2b[:], fb[:],
                      ob[:], ub[:], fTs[:], sTb[:, _i % 2, :], yb[:],
                      retb[:, _i % 2, :], xmb[:, _i % 2, :], junk[:]]
            _dw = [ev_agsco, ev_hdma[NT - 1], ev_xmdma[NT - 1],
                   ev_scan[NT - 1]]
            for _k, _src in enumerate(_dumps):
                tl.add("gpsimd", dma(dbg2[_k], _src), waits=_dw,
                       inc=s_dbg, incn=16)

        ev_last_efT = ev_efT[NT - 1]
        ev_last_scan = ev_scan[NT - 1]
        ev_last_y = ev_y[NT - 1]
        ev_last_ehT = ev_ehT[NT - 1]

    if attn_only:
        ev_fin = ev_hdma[NT - 1]
        if debug_taps:
            ev_fin = tl.add('gpsimd', lambda e: e.nop(), waits=[tl.cur(s_dbg)], inc=None) or ev_fin
        tl.add('sync', lambda e: None, waits=[ev_aghid, ev_agsco, ev_fin])
        esA.close()
        _hidT_guard.__exit__(None, None, None)
    else:
        esA.close()
        ev_spark_last = None
        esT = contextlib.ExitStack()
        sco4 = esT.enter_context(nc.sbuf_tensor("sco4", [128, 64, E], F32))
        lot = esT.enter_context(nc.sbuf_tensor("lot", [1, E], F32))
        hit = esT.enter_context(nc.sbuf_tensor("hit", [1, E], F32))
        midt = esT.enter_context(nc.sbuf_tensor("midt", [1, E], F32))
        cntt = esT.enter_context(nc.sbuf_tensor("cntt", [1, E], F32))
        needt = esT.enter_context(nc.sbuf_tensor("needt", [1, E], F32))
        geb = esT.enter_context(nc.sbuf_tensor("geb", [1, 2 * E], I32))
        midv = esT.enter_context(nc.sbuf_tensor("midv", [128, E], F32))
        pc = esT.enter_context(nc.sbuf_tensor("pc", [128, E], F32))
        hiv = esT.enter_context(nc.sbuf_tensor("hiv", [128, E], F32))
        lov = esT.enter_context(nc.sbuf_tensor("lov", [128, E], F32))
        needv = esT.enter_context(nc.sbuf_tensor("needv", [128, E], F32))
        m_hi = esT.enter_context(nc.sbuf_tensor("m_hi", [128, 64], F32))
        m_bd = esT.enter_context(nc.sbuf_tensor("m_bd", [128, 64], F32))
        cum = esT.enter_context(nc.sbuf_tensor("cum", [128, 64], F32))
        rnk = esT.enter_context(nc.sbuf_tensor("rnk", [128, 64], F32))
        tk = esT.enter_context(nc.sbuf_tensor("tk", [128, 64], F32))
        selb = esT.enter_context(nc.sbuf_tensor("selb", [128, 64], F32))
        selbi = esT.enter_context(nc.sbuf_tensor("selbi", [128, 64], I32))
        sco4o = esT.enter_context(nc.sbuf_tensor("sco4o", [128, 4, 64], F32))
        hiv4 = esT.enter_context(nc.sbuf_tensor("hiv4", [128, 4], F32))
        lov4 = esT.enter_context(nc.sbuf_tensor("lov4", [128, 4], F32))
        needv4 = esT.enter_context(nc.sbuf_tensor("needv4", [128, 4], F32))
        rowt = esT.enter_context(nc.sbuf_tensor("rowt", [128, 1], F32))
        offs = esT.enter_context(nc.sbuf_tensor("offs", [128, 1], F32))
        odst = esT.enter_context(nc.sbuf_tensor("odst", [128, 64], F32))
        esS = contextlib.ExitStack()
        wsgs = esS.enter_context(nc.sbuf_tensor("wsgs", [128, KC, 2, 1024], BF16))
        wsus = esS.enter_context(nc.sbuf_tensor("wsus", [128, KC, 2, 1024], BF16))
        wsds = esS.enter_context(nc.sbuf_tensor("wsds", [128, 2, MS, 512], BF16))
        hsT = esS.enter_context(nc.sbuf_tensor("hsT", [128, MS, 1024], BF16))
        gtmp = esS.enter_context(nc.sbuf_tensor("gtmp", [128, 1024], BF16))
        shout = esS.enter_context(nc.sbuf_tensor("shout", [128, 512], F32))
        if True:
            attn_dead = [ev_ehT[NT - 1], ev_hdma[NT - 1], ev_hdma[NT - 2],
                         ev_xmdma[NT - 1], ev_xmdma[NT - 2]]
            if debug_taps:
                attn_dead.append(tl.add(
                    "gpsimd", lambda e: e.nop(),
                    waits=[tl.cur(s_dbg)], inc=scc))
            for nh in range(2):
                tl.add("sync", dma(
                    wsds[:, nh, :, :],
                    wsd_b.rearrange("p m (n c) -> p m n c", n=2)[:, :, nh, :]),
                    waits=[ev_wcast["wsd"]] + attn_dead, inc=s_wsh, incn=16)
            for hh in range(2):
                tl.add("sync", dma(
                    wsgs[:, :, hh, :], wsg_b[:, :, hh * 1024:(hh + 1) * 1024]),
                    waits=[ev_wcast["wsg"]] + attn_dead, inc=s_wsh, incn=16)
                ev_wsh_t = tl.add("sync", dma(
                    wsus[:, :, hh, :], wsu_b[:, :, hh * 1024:(hh + 1) * 1024]),
                    waits=[ev_wcast["wsu"]] + attn_dead, inc=s_wsh, incn=16)
            ev_wsd = ev_wsh_t
            ev_wsg_h = [ev_wsh_t, ev_wsh_t]
            ev_wsu_h = [ev_wsh_t, ev_wsh_t]

            ev_idxsc = {}
            ev_oi = {}

            def topk_gen():
                ev_sco4 = tl.add("sync", dma(
                    sco4[:], sco_full.rearrange("(p j) e -> p j e", p=128)),
                    waits=[ev_agsco] + attn_dead, inc=s_misc, incn=16)
                yield
                tl.add("vector", lambda e: e.memset(lot[:], 0.0),
                       waits=attn_dead)
                yield
                tl.add("vector", lambda e: e.memset(hit[:], 1.0))
                yield
                ev_upd = tl.add("vector", lambda e: e.memset(midt[:], 0.5),
                                inc=sv)
                yield
                for it in range(BISECT_ITERS + 1):
                    last = it == BISECT_ITERS
                    src_ = hit if last else midt
                    ev_midB = tl.add("tensor", lambda e, src_=src_: e.matmul(
                        P4[:, 16:16 + E], ones1b[:], src_[:], start=True,
                        stop=True),
                        waits=[ev_upd, ev_const, ev_sco[NTR - 1]], inc=st_)
                    yield
                    ev_mve = tl.add("vector", lambda e: e.tensor_copy(
                        midv[:], P4[:, 16:16 + E]), waits=[ev_midB], inc=sv)
                    yield
                    tl.add("vector", lambda e: e.tensor_tensor(
                        junk[:].rearrange("p (j e) -> p j e", e=E), sco4[:],
                        midv[:].rearrange("p (a e) -> p a e", a=1).to_broadcast(
                            [128, 64, E]), ALU.is_ge),
                        waits=[ev_mve, ev_sco4])
                    yield
                    ev_pc = tl.add("vector", lambda e: e.tensor_reduce(
                        pc[:], junk[:].rearrange("p (j e) -> p e j", e=E),
                        AX.X, ALU.add), inc=sv)
                    yield
                    ev_cmm = tl.add("tensor", lambda e: e.matmul(
                        P4[0:1, 0:E], onescb[:], pc[:], start=True, stop=True),
                        waits=[ev_pc], inc=st_)
                    yield
                    tl.add("vector", lambda e: e.tensor_copy(
                        cntt[:], P4[0:1, 0:E]), waits=[ev_cmm])
                    yield
                    if last:
                        ev_upd = tl.add("vector", lambda e: e.tensor_scalar(
                            needt[:], cntt[:], -1.0, float(CAP), ALU.mult,
                            ALU.add), inc=sv)
                        yield
                    else:
                        tl.add("vector", lambda e: e.tensor_scalar(
                            geb[:, 0:E], cntt[:], float(CAP), None, ALU.is_ge))
                        tl.add("vector", lambda e: e.tensor_scalar(
                            geb[:, E:2 * E], cntt[:], float(CAP), None,
                            ALU.is_lt))
                        yield
                        tl.add("vector", lambda e: e.copy_predicated(
                            lot[:], geb[:, 0:E], midt[:]))
                        tl.add("vector", lambda e: e.copy_predicated(
                            hit[:], geb[:, E:2 * E], midt[:]))
                        yield
                        tl.add("vector", lambda e: e.tensor_add(
                            midt[:], lot[:], hit[:]))
                        ev_upd = tl.add("vector", lambda e: e.tensor_scalar(
                            midt[:], midt[:], 0.5, None, ALU.mult), inc=sv)
                        yield
                ev_bcp = ev_upd
                for srct, dstv in ((lot, lov), (hit, hiv), (needt, needv)):
                    evm = tl.add("tensor", lambda e, srct=srct: e.matmul(
                        P4[:, 16:16 + E], ones1b[:], srct[:], start=True,
                        stop=True), waits=[ev_bcp], inc=st_)
                    yield
                    ev_bcp = tl.add("vector", lambda e, dstv=dstv:
                                    e.tensor_copy(dstv[:], P4[:, 16:16 + E]),
                                    waits=[evm], inc=sv)
                    yield
                for le in range(4):
                    tl.add("vector", lambda e, le=le: e.tensor_tensor(
                        junk[:].rearrange("p (j e) -> p j e", e=E), sco4[:],
                        oh4b[:, le, :].rearrange(
                            "p (a e) -> p a e", a=1).to_broadcast(
                            [128, 64, E]), ALU.mult), waits=[ev_bcp])
                    yield
                    tl.add("vector", lambda e, le=le: e.tensor_reduce(
                        sco4o[:, le, :],
                        junk[:].rearrange("p (j e) -> p j e", e=E),
                        AX.X, ALU.add))
                    yield
                    for srcv, dst4 in ((hiv, hiv4), (lov, lov4),
                                       (needv, needv4)):
                        tl.add("vector", lambda e, srcv=srcv, le=le:
                               e.tensor_mul(pc[:], srcv[:], oh4b[:, le, :]))
                        tl.add("vector", lambda e, dst4=dst4, le=le:
                               e.tensor_reduce(dst4[:, le:le + 1], pc[:],
                                               AX.X, ALU.add))
                        yield
                    tl.add("vector", lambda e, le=le: e.tensor_tensor(
                        m_hi[:], sco4o[:, le, :],
                        hiv4[:, le:le + 1].to_broadcast([128, 64]),
                        ALU.is_ge))
                    yield
                    tl.add("vector", lambda e, le=le: e.tensor_tensor(
                        m_bd[:], sco4o[:, le, :],
                        lov4[:, le:le + 1].to_broadcast([128, 64]),
                        ALU.is_ge))
                    tl.add("vector", lambda e: e.tensor_sub(
                        m_bd[:], m_bd[:], m_hi[:]))
                    yield
                    tl.add("vector", lambda e: e.tensor_tensor_scan(
                        cum[:], m_bd[:], zzz[:, 0:64], 0.0, ALU.add, ALU.add),
                        waits=[ev_zzz])
                    ev_rt = tl.add("vector", lambda e: e.tensor_copy(
                        rowt[:], cum[:, 63:64]), inc=sv)
                    yield
                    ev_om = tl.add("tensor", lambda e: e.matmul(
                        P4[:, 48:49], ltrib[:], rowt[:], start=True,
                        stop=True), waits=[ev_rt], inc=st_)
                    yield
                    ev_of = tl.add("vector", lambda e: e.tensor_copy(
                        offs[:], P4[:, 48:49]), waits=[ev_om], inc=sv)
                    tl.add("vector", lambda e: e.scalar_tensor_tensor(
                        rnk[:], cum[:], offs[:], m_bd[:], ALU.add,
                        ALU.subtract), waits=[ev_of])
                    yield
                    tl.add("vector", lambda e, le=le: e.tensor_tensor(
                        tk[:], rnk[:],
                        needv4[:, le:le + 1].to_broadcast([128, 64]),
                        ALU.is_lt))
                    tl.add("vector", lambda e: e.tensor_mul(
                        tk[:], tk[:], m_bd[:]))
                    tl.add("vector", lambda e: e.tensor_add(
                        selb[:], m_hi[:], tk[:]))
                    yield
                    tl.add("vector", lambda e: e.tensor_tensor_scan(
                        cum[:], selb[:], zzz[:, 0:64], 0.0, ALU.add, ALU.add))
                    ev_rt2 = tl.add("vector", lambda e: e.tensor_copy(
                        rowt[:], cum[:, 63:64]), inc=sv)
                    yield
                    ev_om2 = tl.add("tensor", lambda e: e.matmul(
                        P4[:, 48:49], ltrib[:], rowt[:], start=True,
                        stop=True), waits=[ev_rt2], inc=st_)
                    yield
                    ev_of2 = tl.add("vector", lambda e: e.tensor_copy(
                        offs[:], P4[:, 48:49]), waits=[ev_om2], inc=sv)
                    tl.add("vector", lambda e: e.scalar_tensor_tensor(
                        rnk[:], cum[:], offs[:], selb[:], ALU.add,
                        ALU.subtract), waits=[ev_of2])
                    yield
                    tl.add("vector", lambda e: e.tensor_copy(
                        selbi[:], selb[:]))
                    tl.add("vector", lambda e: e.select(
                        odst[:], selbi[:], rnk[:], bigt[:]), waits=[ev_big])
                    yield
                    ev_oi[le] = tl.add("vector", lambda e, le=le:
                                       e.tensor_copy(odsti[:, le, :],
                                                     odst[:]), inc=sv)
                    yield
                for le in range(4):
                    for j in range(64):
                        evp = tl.add("gpsimd", lambda e, le=le, j=j:
                                     e.indirect_dma_start(
                                         out=idxl_d[le],
                                         out_offset=IndirectOffsetOnAxis(
                                             odsti[:, le, j:j + 1], 0),
                                         in_=tokidb[:, j:j + 1],
                                         in_offset=None,
                                         bounds_check=regs['cap'],
                                         oob_is_err=False),
                                     waits=[ev_oi[3], ev_const]
                                     if j == 0 and le == 0 else [],
                                     inc=s_isc, incn=16)
                        if j % 16 == 15:
                            yield
                    ev_idxsc[le] = evp

            gtk = topk_gen()

            def pump(n):
                for _ in range(n):
                    if next(gtk, "END") == "END":
                        break

            ev_sh_h = {}
            ev_gel_prev = None
            ev_dn_half = [None, None]
            ev_shev = None
            for th in range(2):
                for m in range(MS):
                    hh = m // 8

                    def mk_g(e, th=th, m=m, hh=hh):
                        r = None
                        for tb in range(2):
                            tsl = slice(th * 1024 + tb * 512, th * 1024 + (tb + 1) * 512)
                            for k in range(KC):
                                r = e.matmul(
                                    P1[:, tb * 512:(tb + 1) * 512],
                                    wsgs[:, k, hh, (m % 8) * 128:(m % 8 + 1) * 128],
                                    hidT[:, k, tsl],
                                    start=(k == 0), stop=(k == KC - 1))
                        return r
                    w = [ev_wsg_h[hh], ev_last_ehT]
                    if ev_gel_prev is not None:
                        w.append(ev_gel_prev)
                    if th == 1 and m == 0:
                        w.append(ev_dn_half[0])
                    if th == 0 and m == 0:
                        w.append(ev_last_efT)
                        w.append(ev_last_scan)
                    ev_gmmS = tl.add("tensor", mk_g, waits=w, inc=st_)

                    def mk_u(e, th=th, m=m, hh=hh):
                        r = None
                        for tb in range(2):
                            tsl = slice(th * 1024 + tb * 512, th * 1024 + (tb + 1) * 512)
                            for k in range(KC):
                                r = e.matmul(
                                    P2[:, tb * 512:(tb + 1) * 512],
                                    wsus[:, k, hh, (m % 8) * 128:(m % 8 + 1) * 128],
                                    hidT[:, k, tsl],
                                    start=(k == 0), stop=(k == KC - 1))
                        return r
                    wu_ = [ev_wsu_h[hh], ev_last_ehT]
                    if (th, m - 1) in ev_sh_h:
                        wu_.append(ev_sh_h[(th, m - 1)])
                    elif th == 1 and m == 0:
                        wu_.append(ev_sh_h[(0, MS - 1)])
                    elif th == 0 and m == 0:
                        wu_.append(ev_last_scan)
                    ev_ummS = tl.add("tensor", mk_u, waits=wu_, inc=st_)
                    w2 = [ev_gmmS]
                    if ev_gel_prev is not None:
                        w2.append(ev_sh_h.get((th, m - 1)) or ev_sh_h.get((0, MS - 1)))
                    ev_gel = tl.add("scalar", lambda e: e.activation(
                        gtmp[:], P1[:], AF.Gelu), waits=w2, inc=ss)
                    ev_gel_prev = ev_gel
                    ev_sh_h[(th, m)] = tl.add("vector", lambda e, m=m: e.tensor_mul(
                        hsT[:, m, :], gtmp[:], P2[:]), waits=[ev_gel, ev_ummS], inc=sv,
                        nofence=True)
                    if th * MS + m >= 3:
                        pump(0)
                ev_dn = None
                for nh in range(2):
                    for tt in range(8):
                        def mk_dn(e, th=th, tt=tt, nh=nh):
                            r = None
                            for fc in range(MS):
                                r = e.matmul(P3[:, nh * 512:(nh + 1) * 512],
                                             hsT[:, fc, tt * 128:(tt + 1) * 128],
                                             wsds[:, nh, fc, :],
                                             start=(fc == 0), stop=(fc == MS - 1))
                            return r
                        w = [ev_sh_h[(th, MS - 1)], ev_wsd]
                        if ev_shev is not None:
                            w.append(ev_shev)
                        if th == 0 and tt == 0:
                            w.append(ev_last_y)
                        ev_dn = tl.add("tensor", mk_dn, waits=w, inc=st_)
                        w = [ev_dn]
                        if ev_spark_last is not None:
                            w.append(ev_spark_last)
                        ev_shev = tl.add("vector", lambda e, nh=nh: e.tensor_copy(
                            shout[:], P3[:, nh * 512:(nh + 1) * 512]),
                            waits=w, inc=sv, nofence=True)
                        ev_spark_last = tl.add(
                            "scalar", lambda e, th=th, tt=tt, nh=nh: e.dma_start(
                                out=spark.rearrange("(t p) d -> t p d", p=128)
                                [th * 8 + tt][:, nh * 512:(nh + 1) * 512],
                                in_=shout[:]),
                            waits=[ev_shev], inc=s_sp, incn=16)
                        pump(0)
                ev_dn_half[th] = ev_dn
            ev_shev_last = ev_shev
            ev_sh_h_last = ev_sh_h[(1, MS - 1)]

        esS.close()
        if True:
            shared_dead = [ev_spark_last, ev_shev_last]
            for _ in gtk:
                pass

            if debug_taps:
                for ee in range(4):
                    tl.add("gpsimd", dma(dbg_idx[ee].rearrange("(c a) -> c a", a=1), idxl_d[ee]), waits=[ev_idxsc[3]],
                           inc=s_dbg, incn=16)

            esE = contextlib.ExitStack()
            xtokT = esE.enter_context(nc.sbuf_tensor("xtokT", [128, KC, CAP], BF16))
            hTx = esE.enter_context(nc.sbuf_tensor("hTx", [128, MF, CAP], BF16))
            wgs = esE.enter_context(nc.sbuf_tensor("wgs", [128, KC, 2, 512], BF16))
            wus = esE.enter_context(nc.sbuf_tensor("wus", [128, KC, 2, 512], BF16))
            wds = esE.enter_context(nc.sbuf_tensor("wds", [128, MF, D], BF16))
            gtmp2 = esE.enter_context(nc.sbuf_tensor("gtmp2", [128, 1024], BF16))
            if True:
                ev_wg_h = {}
                ev_wu_h = {}
                ev_mmq = {}
                ev_wd_e = {}
                ev_lastmm_gu = {}
                ev_lastmm_d = {}
                ev_scat = {}
                ev_eo_evac = {}
                prev_scat_end = None
                ev_xg_last = None
                ev_ii = {}

                ev_prep = {}

                def prep(ee):
                    par = ee % 2
                    w = [ev_idxsc[ee]]
                    if ee > 0:
                        w.append(ev_prep[ee - 1]["it"])
                    ev_i8 = tl.add("sync", dma(
                        idx8[:], idxl_d[ee].rearrange("(a b) c -> a (b c)", a=8)),
                        waits=w, inc=s_misc, incn=16)
                    w = [ev_i8]
                    if ee > 0:
                        w.append(ev_prep[ee - 1]["ii"])
                    else:
                        w.append(ev_oi[3])
                    ev_it = tl.add("tensor", lambda e: e.transpose(
                        P4[:, 64:72], idx8[:], identb32[0:8, 0:8]), waits=w, inc=st_)
                    w = [ev_it]
                    if ee > 1:
                        w.append(ev_scat[(ee - 2, 7)])
                        w.append(ev_scat[(ee - 2, 6)])
                    ev_ii = tl.add("vector", lambda e, par=par: e.tensor_copy(
                        idxi[:, par, :], P4[:, 64:72]), waits=w, inc=sv)
                    ev_xg = None
                    ev_tw = None
                    for s in range(8):
                        w = [ev_ii]
                        if ev_tw is not None:
                            w.append(ev_tw)
                        if ee > 1 and s == 0:
                            w.append(ev_eo_evac[(ee - 2, 7)])
                        ev_g1 = tl.add("gpsimd", lambda e, s=s, par=par:
                            e.indirect_dma_start(
                                out=tw16[:], out_offset=None, in_=sco_full,
                                in_offset=IndirectOffsetOnAxis(idxi[:, par, s:s + 1], 0),
                                bounds_check=None, oob_is_err=False),
                            waits=w, inc=s_tw, incn=16)
                        tl.add("vector", lambda e, ee=ee: e.tensor_mul(
                            tw16[:], tw16[:], oh4b[:, ee, :]), waits=[ev_g1],
                            nofence=True)
                        ev_tw = tl.add("vector", lambda e, s=s, par=par:
                            e.tensor_reduce(
                                topw[:, par, s:s + 1], tw16[:], AX.X, ALU.add),
                            inc=sv)
                        w = [ev_ii, ev_aghid]
                        if ee > 0 and s == 0:
                            w.append(ev_prep[ee - 1]["xT"])
                        ev_xg = tl.add("gpsimd", lambda e, s=s, par=par:
                            e.indirect_dma_start(
                                out=xtok[:, s, :], out_offset=None, in_=hid_full,
                                in_offset=IndirectOffsetOnAxis(idxi[:, par, s:s + 1], 0),
                                bounds_check=None, oob_is_err=False),
                            waits=w, inc=s_xg, incn=16)
                    ev_xT = None
                    ev_xTe = None
                    for s in range(8):
                        def mk_xT(e, s=s):
                            r = None
                            for g in range(KC):
                                r = e.transpose(PB[:, g * 128:(g + 1) * 128],
                                                xtok[:, s, g * 128:(g + 1) * 128],
                                                identbb[:])
                            return r
                        if ev_xTe is None:
                            w = [ev_xg, ev_last_ehT]
                            if ee > 0:
                                w.append(ev_mmq[(ee - 1, 3)])
                        else:
                            w = [ev_xTe]
                        ev_xT = tl.add("tensor", mk_xT, waits=w, inc=st_)
                        ev_xTe = tl.add("vector", lambda e, s=s: e.tensor_copy(
                            xtokT[:, :, s * 128:(s + 1) * 128],
                            PB[:].rearrange("p (k n) -> p k n", k=KC)),
                            waits=[ev_xT], inc=sv, nofence=True)
                    ev_prep[ee] = {"it": ev_it, "ii": ev_ii, "tw": ev_tw,
                                   "xT": ev_xT, "xTe": ev_xTe}

                prep(0)
                ev_gel_last = None
                for ee in range(4):
                    par = ee % 2
                    def load_q(ee, qd):
                        w = [ev_wcast[f"wg{ee}"]]
                        if qd >= 2:
                            w.append(ev_mmq[(ee, qd - 2)])
                        elif ee > 0:
                            w.append(ev_mmq[(ee - 1, qd + 2)])
                        else:
                            w.extend(shared_dead)
                        ev_wg_h[(ee, qd)] = tl.add("sync", dma(
                            wgs[:, :, qd % 2, :],
                            wg_b[ee][:, :, qd * 512:(qd + 1) * 512]),
                            waits=w, inc=s_wg[qd % 2], incn=16)
                        ev_wu_h[(ee, qd)] = tl.add("sync", dma(
                            wus[:, :, qd % 2, :],
                            wu_b[ee][:, :, qd * 512:(qd + 1) * 512]),
                            waits=[ev_wcast[f"wu{ee}"]] + w[1:],
                            inc=s_wu[qd % 2], incn=16)
                    for qd in range(2):
                        load_q(ee, qd)
                    w = [ev_wcast[f"wd{ee}"]]
                    if ee > 0:
                        w.append(ev_lastmm_d.get(ee - 1))
                    else:
                        w.extend(shared_dead)
                    ev_wd_e[ee] = tl.add("sync", dma(wds[:], wd_b[ee]),
                                         waits=w, inc=s_wd, incn=16)

                    ev_h = {}
                    ev_gel2 = None
                    for m in range(MF):
                        hh = m // 8

                        def mk_gx(e, m=m):
                            r = None
                            for n in range(2):
                                for k in range(KC):
                                    r = e.matmul(
                                        P1[:, n * 512:(n + 1) * 512],
                                        wgs[:, k, (m // 4) % 2,
                                            (m % 4) * 128:(m % 4 + 1) * 128],
                                        xtokT[:, k, n * 512:(n + 1) * 512],
                                        start=(k == 0), stop=(k == KC - 1))
                            return r
                        w = [ev_wg_h[(ee, m // 4)], ev_prep[ee]["xTe"]]
                        if ev_gel2 is not None:
                            w.append(ev_gel2)
                        elif ee > 0:
                            w.append(ev_gel_last)
                        else:
                            w.append(ev_sh_h_last)
                        ev_gmx = tl.add("tensor", mk_gx, waits=w, inc=st_)

                        def mk_ux(e, m=m):
                            r = None
                            for n in range(2):
                                for k in range(KC):
                                    r = e.matmul(
                                        P2[:, n * 512:(n + 1) * 512],
                                        wus[:, k, (m // 4) % 2,
                                            (m % 4) * 128:(m % 4 + 1) * 128],
                                        xtokT[:, k, n * 512:(n + 1) * 512],
                                        start=(k == 0), stop=(k == KC - 1))
                            return r
                        w = [ev_wu_h[(ee, m // 4)], ev_prep[ee]["xTe"]]
                        if m > 0:
                            w.append(ev_h[m - 1])
                        elif ee > 0:
                            w.append(ev_eo_evac[(ee - 1, 7)])
                        else:
                            w.append(ev_sh_h_last)
                        ev_umx = tl.add("tensor", mk_ux, waits=w, inc=st_)
                        ev_mmq[(ee, m // 4)] = ev_umx
                        if m == 3 or m == 7:
                            load_q(ee, m // 4 + 2)
                        w2 = [ev_gmx]
                        if m > 0:
                            w2.append(ev_h[m - 1])
                        ev_gel2 = tl.add("scalar", lambda e: e.activation(
                            gtmp2[:], P1[:], AF.Gelu), waits=w2, inc=ss)
                        ev_h[m] = tl.add("vector", lambda e, m=m: e.tensor_mul(
                            hTx[:, m, :], gtmp2[:], P2[:]),
                            waits=[ev_gel2, ev_umx], inc=sv, nofence=True)
                    ev_gel_last = ev_gel2

                    if ee < 3:
                        prep(ee + 1)

                    ev_dn = None
                    for s in range(8):
                        def mk_dnx(e, s=s):
                            r = None
                            for n in range(2):
                                for fc in range(MF):
                                    r = e.matmul(P3[:, n * 512:(n + 1) * 512],
                                                 hTx[:, fc, s * 128:(s + 1) * 128],
                                                 wds[:, fc, n * 512:(n + 1) * 512],
                                                 start=(fc == 0), stop=(fc == MF - 1))
                            return r
                        w = [ev_h[MF - 1], ev_wd_e[ee]]
                        if ev_dn is not None:
                            w.append(ev_eo_evac[(ee, s - 1)])
                        elif ee == 0:
                            w.append(ev_shev_last)
                        else:
                            w.append(ev_eo_evac[(ee - 1, 7)])
                        ev_dn = tl.add("tensor", mk_dnx, waits=w, inc=st_)
                        ev_lastmm_d[ee] = ev_dn
                        w = [ev_dn, ev_prep[ee]["tw"]]
                        if s >= 2:
                            w.append(ev_scat[(ee, s - 2)])
                        elif ee > 0:
                            w.append(ev_scat[(ee - 1, s + 6)])
                        ev_eo_evac[(ee, s)] = tl.add(
                            "vector", lambda e, s=s, par=par: e.tensor_scalar(
                                eo[:, s % 2, :], P3[:], topw[:, par, s:s + 1], None,
                                ALU.mult), waits=w, inc=sv, nofence=True)
                        w = [ev_eo_evac[(ee, s)], ev_dense0]
                        if s == 0 and prev_scat_end is not None:
                            w.append(prev_scat_end)
                        ev_scat[(ee, s)] = tl.add(
                            "gpsimd", lambda e, ee=ee, s=s, par=par:
                            e.indirect_dma_start(
                                out=dense,
                                out_offset=IndirectOffsetOnAxis(idxi[:, par, s:s + 1], 0),
                                in_=eo[:, s % 2, :], in_offset=None,
                                bounds_check=None, oob_is_err=False,
                                compute_op=ALU.add),
                            waits=w, inc=s_eo[s % 2], incn=16)
                    prev_scat_end = ev_scat[(ee, 7)]

                ev_rs = tl.add("gpsimd", lambda e: e.collective_compute(
                    "ReduceScatter", ALU.add, replica_groups=groups,
                    ins=[dense.opt()], outs=[rs_out.opt()]),
                    waits=[prev_scat_end, ev_scat[(3, 6)]], inc=scc)

                if debug_taps:
                    tl.add("gpsimd", dma(dbg_dense, dense),
                           waits=[prev_scat_end, ev_scat[(3, 6)]],
                           inc=s_dbg, incn=16)

        esE.close()
        esT.close()
        _hidT_guard.__exit__(None, None, None)
        esC = contextlib.ExitStack()
        rsb = esC.enter_context(nc.sbuf_tensor("rsb", [128, 2, D], BF16))
        spb = esC.enter_context(nc.sbuf_tensor("spb", [128, 2, D], F32))
        xpb = esC.enter_context(nc.sbuf_tensor("xpb", [128, 2, D], F32))
        fo = esC.enter_context(nc.sbuf_tensor("fo", [128, D], F32))
        ln4 = esC.enter_context(nc.sbuf_tensor("ln4", [128, D], F32))
        ob2 = esC.enter_context(nc.sbuf_tensor("ob2", [128, D], F32))
        cst = esC.enter_context(nc.sbuf_tensor("cst", [128, 8], F32))
        cjk = esC.enter_context(nc.sbuf_tensor("cjk", [128, D], F32))
        if True:
            rsv = rs_out.rearrange("(t p) d -> t p d", p=128)
            spv = spark.rearrange("(t p) d -> t p d", p=128)
            xpv = xpark.rearrange("(t p) d -> t p d", p=128)
            ouv = out.rearrange("(t p) d -> t p d", p=128)
            ev_done = {}
            ev_odma = None
            ev_n4_prev = None
            for j in range(NTR):
                w = [ev_rs] if j < 2 else [ev_done[j - 2]]
                ev_l1 = tl.add("sync", dma(rsb[:, j % 2, :], rsv[j]),
                               waits=w, inc=s_crs[j % 2], incn=16)
                ev_l2 = tl.add("sync", dma(spb[:, j % 2, :], spv[j]),
                               waits=[ev_spark_last] + w, inc=s_csp[j % 2], incn=16)
                ev_l3 = tl.add("sync", dma(xpb[:, j % 2, :], xpv[j]),
                               waits=[ev_xmdma[NT - 1], ev_xmdma[NT - 2]] + w,
                               inc=s_cxp[j % 2], incn=16)
                w = [ev_l1, ev_l2]
                if ev_n4_prev is not None:
                    w.append(ev_n4_prev)
                tl.add("vector", lambda e, j=j: e.tensor_add(
                    fo[:], rsb[:, j % 2, :], spb[:, j % 2, :]), waits=w)
                tl.add("vector", lambda e: e.reduce_sum(cst[:, 0:1], fo[:], AX.X),
                       nofence=True)
                tl.add("vector", lambda e: e.scalar_tensor_tensor(
                    cjk[:], fo[:], 1.0, fo[:], ALU.mult, ALU.mult,
                    accum_out=cst[:, 1:2]))
                tl.add("vector", lambda e: e.tensor_scalar(
                    cst[:, 2:3], cst[:, 0:1], -1.0 / D, None, ALU.mult))
                tl.add("vector", lambda e: e.tensor_mul(
                    cjk[:, 0:1], cst[:, 2:3], cst[:, 2:3]))
                tl.add("vector", lambda e: e.tensor_scalar(
                    cjk[:, 1:2], cst[:, 1:2], 1.0 / D, 1e-5, ALU.mult, ALU.add))
                ev_v4 = tl.add("vector", lambda e: e.tensor_sub(
                    cst[:, 3:4], cjk[:, 1:2], cjk[:, 0:1]), inc=sv)
                ev_s4 = tl.add("scalar", lambda e: e.activation(
                    cst[:, 4:5], cst[:, 3:4], AF.Sqrt), waits=[ev_v4], inc=ss)
                ev_r4 = tl.add("vector", lambda e: e.reciprocal(
                    cst[:, 5:6], cst[:, 4:5]), waits=[ev_s4], inc=sv)
                ev_n4 = tl.add("vector", lambda e: e.tensor_scalar(
                    ln4[:], fo[:], cst[:, 2:3], cst[:, 5:6], ALU.add, ALU.mult),
                    waits=[ev_r4], inc=sv)
                ev_n4_prev = ev_n4
                w = [ev_n4, ev_l3]
                if ev_odma is not None:
                    w.append(ev_odma)
                ev_ob = tl.add("vector", lambda e, j=j: e.tensor_add(
                    ob2[:], ln4[:], xpb[:, j % 2, :]), waits=w, inc=sv,
                    nofence=True)
                ev_odma = tl.add("gpsimd", lambda e, j=j: e.dma_start(
                    out=ouv[j], in_=ob2[:]), waits=[ev_ob], inc=s_out, incn=16)
                ev_done[j] = ev_odma

            tl.add("sync", lambda e: None, waits=[ev_odma])


    if not attn_only:
        esC.close()
    with nc.Block() as block:
        @block.sync
        def _(e):
            tl.emit(e, "sync")

        @block.tensor
        def _(e):
            tl.emit(e, "tensor")

        @block.vector
        def _(e):
            tl.emit(e, "vector")

        @block.scalar
        def _(e):
            tl.emit(e, "scalar")

        @block.gpsimd
        def _(e):
            tl.emit(e, "gpsimd")

    es.close()
    nc.finalize()
    return nc


def _chunked(wT):
    Dd, Ff = wT.shape
    return np.ascontiguousarray(wT.reshape(Dd // 128, 128, Ff).transpose(1, 0, 2))


def make_inputs(inputs):
    x = np.asarray(inputs["L0_attn_x"], np.float32)
    wf = np.asarray(inputs["L0_attn_w_forget"], np.float32)
    wi = np.asarray(inputs["L0_attn_w_input"], np.float32)
    wo = np.asarray(inputs["L0_attn_w_output"], np.float32)
    nw = np.asarray(inputs["L0_attn_norm_w"], np.float32)
    gw = np.asarray(inputs["L0_ffn_gate_w"], np.float32)
    Wg = np.asarray(inputs["L0_ffn_Wg"], np.float32)
    Wu = np.asarray(inputs["L0_ffn_Wu"], np.float32)
    Wd = np.asarray(inputs["L0_ffn_Wd"], np.float32)
    Wsg = np.asarray(inputs["L0_ffn_Ws_g"], np.float32)
    Wsu = np.asarray(inputs["L0_ffn_Ws_u"], np.float32)
    Wsd = np.asarray(inputs["L0_ffn_Ws_d"], np.float32)

    wfT = _chunked(np.ascontiguousarray(wf.T))
    wiT = _chunked(np.ascontiguousarray(wi.T))
    woT = _chunked(np.ascontiguousarray(wo.T))
    gateT = _chunked(np.ascontiguousarray(gw.T))
    wsgT = _chunked(np.ascontiguousarray(Wsg.T))
    wsuT = _chunked(np.ascontiguousarray(Wsu.T))
    wsdT = _chunked(np.ascontiguousarray(Wsd.T))

    identm = np.eye(128, dtype=np.float32)
    k = np.arange(128)
    ltrim = (k[:, None] < k[None, :]).astype(np.float32)
    onesc = np.ones((128, 1), np.float32)
    ones1 = np.ones((1, 128), np.float32)
    tokidm = (np.arange(128)[:, None] * 64 + np.arange(64)[None, :]).astype(np.float32)

    in_maps = []
    for c in range(NCORE):
        b, q = c // 4, c % 4
        lo = q * TLOC - WARM
        if lo < 0:
            xw = np.concatenate([np.zeros((WARM, D), np.float32),
                                 x[b, 0:TLOC]], axis=0)
        else:
            xw = x[b, lo:(q + 1) * TLOC]
        ex = [4 * q + t for t in range(4)]
        wgTs = np.stack([_chunked(np.ascontiguousarray(Wg[e].T)) for e in ex])
        wuTs = np.stack([_chunked(np.ascontiguousarray(Wu[e].T)) for e in ex])
        wdTs = np.stack([_chunked(np.ascontiguousarray(Wd[e].T)) for e in ex])
        oh = np.zeros((128, 4, E), np.float32)
        for t in range(4):
            oh[:, t, 4 * q + t] = 1.0
        in_maps.append({
            "xin": np.ascontiguousarray(xw),
            "wfT": wfT, "wiT": wiT, "woT": woT,
            "normw": nw.reshape(1, D),
            "gateT": gateT, "wsgT": wsgT, "wsuT": wsuT, "wsdT": wsdT,
            "wgT": wgTs, "wuT": wuTs, "wdT": wdTs,
            "ident": identm, "ltri": ltrim, "onesc": onesc, "ones1": ones1,
            "tokid": tokidm, "oh4": oh,
        })
    return in_maps


def kernel(**inputs):
    nc = build_program()
    in_maps = make_inputs(inputs)
    res = run_bass_kernel_spmd(nc, in_maps, core_ids=list(range(NCORE)))
    parts = [res.results[c]["out"] for c in range(NCORE)]
    full = np.stack([np.concatenate(parts[0:4], axis=0),
                     np.concatenate(parts[4:8], axis=0)])
    return full.astype(np.float32)
